# version 07
# baseline (speedup 1.0000x reference)
import math
import numpy as np
import ml_dtypes
from contextlib import ExitStack

import concourse.bass as bass
import concourse.mybir as mybir
from concourse.bass_utils import run_bass_kernel_spmd
from concourse.bass_core_rust import IndirectOffsetOnAxis

F32 = mybir.dt.float32
BF16 = mybir.dt.bfloat16
I32 = mybir.dt.int32
U32 = mybir.dt.uint32
AF = mybir.ActivationFunctionType
ALU = mybir.AluOpType
AX = mybir.AxisListType

L = 4096
D = 1024
NCH = D // 128
TT = 512
NT = L // TT
NEG = -3.0e38
LADC = 1.0e30
LADD = 1.0e24
N_PHASES_DEFAULT = 99
DEBUG_COPY = None


class Ctr:
    def __init__(self, nc, name):
        self.sem = nc.alloc_semaphore(name)
        self.n = 0


class P:
    def __init__(self, nc, ctx):
        self.nc = nc
        self.ctx = ctx
        self.sems = []

    def sbuf(self, name, shape, dt):
        return self.ctx.enter_context(self.nc.sbuf_tensor(name, shape, dt))

    def psum(self, name, shape, dt=F32):
        return self.ctx.enter_context(self.nc.psum_tensor(name, shape, dt))

    def ctr(self, name):
        c = Ctr(self.nc, name)
        self.sems.append(c.sem)
        return c


def phase_end(nc, pp):
    nc.all_engine_barrier()
    if pp.sems:
        nc.clear_and_free_semaphores(pp.sems)
        pp.sems = []
    nc.all_engine_barrier()


def fm_view(dram_ap, c0, nch, t0, nt):
    return dram_ap[c0 * 128:(c0 + nch) * 128, t0:t0 + nt].rearrange(
        "(c p) t -> p c t", p=128)



def emit_cast_bf16(nc, jobs, tag):
    steps = []
    for src, dst in jobs:
        R, C = src.tensor.shape[-2], src.tensor.shape[-1]
        for r in range((R + 127) // 128):
            steps.append((src, dst, r, min(128, R - r * 128), C))
    NB = 4
    with ExitStack() as ctx:
        pp = P(nc, ctx)
        sf = [pp.sbuf(f"{tag}_f{b}", [128, 4096], F32) for b in range(NB)]
        sb = [pp.sbuf(f"{tag}_b{b}", [128, 4096], BF16) for b in range(NB)]
        ld = [pp.ctr(f"{tag}_ld{b}") for b in range(NB)]
        st = [pp.ctr(f"{tag}_st{b}") for b in range(NB)]
        ck = pp.ctr(f"{tag}_ck")
        n = len(steps)
        ldc = [0] * NB
        stc = [0] * NB

        def loader(eng, par):
            for i, (src, dst, r, rows, C) in enumerate(steps):
                b = i % NB
                if b % 2 != par:
                    continue
                if i >= NB:
                    eng.wait_ge(ck.sem, i - NB + 1)
                eng.dma_start(out=sf[b][:rows, :C],
                              in_=src[r * 128:r * 128 + rows, :]).then_inc(ld[b].sem, 16)

        with nc.Block() as blk:
            @blk.sync
            def _(sy):
                loader(sy, 0)

            @blk.scalar
            def _(sc):
                loader(sc, 1)

            @blk.vector
            def _(v):
                for i, (src, dst, r, rows, C) in enumerate(steps):
                    b = i % NB
                    ldc[b] += 1
                    v.wait_ge(ld[b].sem, 16 * ldc[b])
                    if i >= NB:
                        v.wait_ge(st[b].sem, 16 * (i // NB))
                    v.tensor_copy(out=sb[b][:rows, :C], in_=sf[b][:rows, :C]
                                  ).then_inc(ck.sem, 1)

            @blk.gpsimd
            def _(g):
                for i, (src, dst, r, rows, C) in enumerate(steps):
                    b = i % NB
                    g.wait_ge(ck.sem, i + 1)
                    g.dma_start(out=dst[r * 128:r * 128 + rows, :],
                                in_=sb[b][:rows, :C]).then_inc(st[b].sem, 16)
                for b in range(NB):
                    g.wait_ge(st[b].sem, 16 * len([i for i in range(n) if i % NB == b]))
        phase_end(nc, pp)


def emit_transpose_bf16(nc, jobs, ident_bf_dram, tag):
    with ExitStack() as ctx:
        pp = P(nc, ctx)
        s_id = pp.sbuf(f"{tag}_id", [128, 128], F32)
        s_in = [pp.sbuf(f"{tag}_in{b}", [128, 128], F32) for b in range(2)]
        s_out = [pp.sbuf(f"{tag}_out{b}", [128, 128], BF16) for b in range(2)]
        ps_t = [pp.psum(f"{tag}_ps{b}", [128, 128], F32) for b in range(2)]
        lid = pp.ctr(f"{tag}_li")
        l2 = [pp.ctr(f"{tag}_l{b}") for b in range(2)]
        s2 = [pp.ctr(f"{tag}_s{b}") for b in range(2)]
        e2 = pp.ctr(f"{tag}_e"); v2 = pp.ctr(f"{tag}_v")
        steps = []
        for src, dst in jobs:
            R, C = src.tensor.shape[-2], src.tensor.shape[-1]
            for r in range((R + 127) // 128):
                for c in range(C // 128):
                    steps.append((src, dst, r, c, min(128, R - r * 128)))
        with nc.Block() as blk:
            @blk.sync
            def _(sy):
                sy.dma_start(out=s_id[:], in_=ident_bf_dram).then_inc(lid.sem, 16)
                for i, (src, dst, r, c, rows) in enumerate(steps):
                    if i >= 2:
                        sy.wait_ge(e2.sem, i - 1)
                    sy.dma_start(out=s_in[i % 2][:rows, :],
                                 in_=src[r * 128:r * 128 + rows, c * 128:(c + 1) * 128]
                                 ).then_inc(l2[i % 2].sem, 16)

            @blk.tensor
            def _(t):
                t.wait_ge(lid.sem, 16)
                for i, (src, dst, r, c, rows) in enumerate(steps):
                    t.wait_ge(l2[i % 2].sem, 16 * (i // 2 + 1))
                    if i >= 2:
                        t.wait_ge(v2.sem, i - 1)
                    t.transpose(ps_t[i % 2][:, :rows], in_=s_in[i % 2][:rows, :],
                                identity=s_id[0:rows, 0:rows]).then_inc(e2.sem, 1)

            @blk.vector
            def _(v):
                for i, (src, dst, r, c, rows) in enumerate(steps):
                    v.wait_ge(e2.sem, i + 1)
                    if i >= 2:
                        v.wait_ge(s2[i % 2].sem, 16 * (i // 2))
                    v.tensor_copy(out=s_out[i % 2][:, :rows], in_=ps_t[i % 2][:, :rows]
                                  ).then_inc(v2.sem, 1)

            @blk.gpsimd
            def _(g):
                for i, (src, dst, r, c, rows) in enumerate(steps):
                    g.wait_ge(v2.sem, i + 1)
                    g.dma_start(out=dst[c * 128:(c + 1) * 128, r * 128:r * 128 + rows],
                                in_=s_out[i % 2][:, :rows]).then_inc(s2[i % 2].sem, 16)
                g.wait_ge(s2[0].sem, 16 * ((len(steps) + 1) // 2))
                g.wait_ge(s2[1].sem, 16 * (len(steps) // 2))
        phase_end(nc, pp)


def emit_mm1024(nc, pairs, dst_dram, tag):
    npair = len(pairs)
    with ExitStack() as ctx:
        pp = P(nc, ctx)
        sL = [pp.sbuf(f"{tag}_L{i}", [128, NCH, D], BF16) for i in range(npair)]
        sR = [[pp.sbuf(f"{tag}_R{b}{i}", [128, NCH, TT], BF16) for i in range(npair)]
              for b in range(2)]
        sOb = pp.sbuf(f"{tag}_O", [128, TT], BF16)
        ps = [pp.psum(f"{tag}_P{b}", [128, TT], F32) for b in range(2)]
        lw = pp.ctr(f"{tag}_lw"); lr = [pp.ctr(f"{tag}_lr{b}") for b in range(2)]
        mm = pp.ctr(f"{tag}_mm"); ev = pp.ctr(f"{tag}_ev"); so = pp.ctr(f"{tag}_so")
        with nc.Block() as blk:
            @blk.sync
            def _(sy):
                for i, (lhsT, rhs) in enumerate(pairs):
                    sy.dma_start(out=sL[i][:],
                                 in_=lhsT.rearrange("(c p) m -> p c m", p=128)
                                 ).then_inc(lw.sem, 16)
                for nt in range(2):
                    for i, (lhsT, rhs) in enumerate(pairs):
                        sy.dma_start(out=sR[nt][i][:],
                                     in_=fm_view(rhs, 0, NCH, nt * TT, TT)
                                     ).then_inc(lr[nt].sem, 16)

            @blk.tensor
            def _(t):
                t.wait_ge(lw.sem, 16 * npair)
                for nt in range(2):
                    t.wait_ge(lr[nt].sem, 16 * npair)
                    for mc in range(8):
                        b = mc % 2
                        tot = nt * 8 + mc
                        if tot >= 2:
                            t.wait_ge(ev.sem, tot - 1)
                        for i in range(npair):
                            for kc in range(NCH):
                                inst = t.matmul(ps[b][:],
                                                lhsT=sL[i][:, kc, mc * 128:(mc + 1) * 128],
                                                rhs=sR[nt][i][:, kc, :],
                                                start=(i == 0 and kc == 0),
                                                stop=(i == npair - 1 and kc == NCH - 1))
                        inst.then_inc(mm.sem, 1)

            @blk.vector
            def _(v):
                for tot in range(16):
                    v.wait_ge(mm.sem, tot + 1)
                    if tot >= 1:
                        v.wait_ge(so.sem, 16 * tot)
                    v.tensor_copy(out=sOb[:], in_=ps[tot % 2][:]).then_inc(ev.sem, 1)

            @blk.scalar
            def _(s):
                for nt in range(2):
                    for mc in range(8):
                        tot = nt * 8 + mc
                        s.wait_ge(ev.sem, tot + 1)
                        s.dma_start(out=dst_dram[mc * 128:(mc + 1) * 128, nt * TT:(nt + 1) * TT],
                                    in_=sOb[:]).then_inc(so.sem, 16)
                s.wait_ge(so.sem, 16 * 16)
        phase_end(nc, pp)



def emit_ln(nc, IN, SC, src_dram, want_f32, tag, gamma=None, beta=None,
            out_bf=None, out_f32=None, res_src=None, res_dst=None, res_scale=0.5):
    if out_bf is None:
        out_bf = SC["xn_bf"].ap()
    if out_f32 is None:
        out_f32 = SC["xn_f32"].ap()
    with ExitStack() as ctx:
        pp = P(nc, ctx)
        sx = [pp.sbuf(f"{tag}_x{b}", [128, NCH, TT], F32) for b in range(2)]
        ssq = pp.sbuf(f"{tag}_sq", [128, NCH, TT], F32)
        sap = pp.sbuf(f"{tag}_ap", [128, NCH, TT], F32)
        sst = pp.sbuf(f"{tag}_st", [1, 4 * TT], F32)
        sab = pp.sbuf(f"{tag}_ab", [1, 2 * TT], F32)
        sxn = [pp.sbuf(f"{tag}_o{b}", [128, NCH, TT], BF16) for b in range(2)]
        sxf = [pp.sbuf(f"{tag}_f{b}", [128, NCH, TT], F32) for b in range(2)] if want_f32 else None
        want_res = res_dst is not None
        sgb = pp.sbuf(f"{tag}_gb", [128, 2 * NCH], F32) if gamma is not None else None
        srs = [pp.sbuf(f"{tag}_rs{b}", [128, NCH, TT], F32) for b in range(2)] if want_res else None
        sro = [pp.sbuf(f"{tag}_ro{b}", [128, NCH, TT], F32) for b in range(2)] if want_res else None
        ones_c = pp.sbuf(f"{tag}_onc", [128, 1], F32)
        ones_r = pp.sbuf(f"{tag}_onr", [1, 128], F32)
        ps_s = pp.psum(f"{tag}_ps", [1, 2 * TT], F32)
        ps_a = pp.psum(f"{tag}_pa", [128, TT], F32)
        ps_b = pp.psum(f"{tag}_pb", [128, TT], F32)
        cst = pp.ctr(f"{tag}_cs")
        ld = [pp.ctr(f"{tag}_ld{b}") for b in range(2)]
        vq = pp.ctr(f"{tag}_vq")
        sq = pp.ctr(f"{tag}_sq"); pe = pp.ctr(f"{tag}_pe")
        dv = pp.ctr(f"{tag}_dv"); bc = pp.ctr(f"{tag}_bc"); ap = pp.ctr(f"{tag}_aq")
        st = [pp.ctr(f"{tag}_s2{b}") for b in range(2)]
        nsp = 1 if res_dst is not None else (2 if want_f32 else 1)
        ncst_tot = 64 if gamma is not None else 32

        with nc.Block() as blk:
            @blk.sync
            def _(sy):
                sy.dma_start(out=ones_c[:], in_=IN["ones_col"]).then_inc(cst.sem, 16)
                sy.dma_start(out=ones_r[:], in_=IN["ones_row"]).then_inc(cst.sem, 16)
                ncst = 2
                if gamma is not None:
                    sy.dma_start(out=sgb[:, 0:NCH], in_=gamma.rearrange("(c p) -> p c", p=128),
                                 allow_slow_non_contiguous=True).then_inc(cst.sem, 16)
                    sy.dma_start(out=sgb[:, NCH:2 * NCH], in_=beta.rearrange("(c p) -> p c", p=128),
                                 allow_slow_non_contiguous=True).then_inc(cst.sem, 16)
                    ncst = 4
                for i in range(NT):
                    if i >= 2:
                        sy.wait_ge(ap.sem, i - 1)
                    sy.dma_start(out=sx[i % 2][:],
                                 in_=fm_view(src_dram, 0, NCH, i * TT, TT)
                                 ).then_inc(ld[i % 2].sem, 16)
                    if want_res:
                        sy.dma_start(out=srs[i % 2][:],
                                     in_=fm_view(res_src, 0, NCH, i * TT, TT)
                                     ).then_inc(ld[i % 2].sem, 16)

            nldp = 32 if want_res else 16

            @blk.scalar
            def _(s):
                for i in range(NT):
                    s.wait_ge(ld[i % 2].sem, nldp * (i // 2 + 1))
                    if i >= 1:
                        s.wait_ge(pe.sem, 2 * i)
                    s.activation(out=ssq[:], in_=sx[i % 2][:], func=AF.Square
                                 ).then_inc(sq.sem, 1)
                    s.wait_ge(dv.sem, 2 * i + 1)
                    s.activation(out=sst[:, 3 * TT:4 * TT], in_=sst[:, 2 * TT:3 * TT],
                                 func=AF.Sqrt).then_inc(sq.sem, 1)

            @blk.tensor
            def _(t):
                t.wait_ge(cst.sem, ncst_tot)
                for i in range(NT):
                    t.wait_ge(ld[i % 2].sem, nldp * (i // 2 + 1))
                    t.wait_ge(sq.sem, 2 * i + 1)
                    if i >= 1:
                        t.wait_ge(dv.sem, 2 * i - 1)
                    for kc in range(NCH):
                        t.matmul(ps_s[:, 0:TT], lhsT=ones_c[:], rhs=sx[i % 2][:, kc, :],
                                 start=(kc == 0), stop=(kc == NCH - 1))
                    for kc in range(NCH):
                        inst = t.matmul(ps_s[:, TT:2 * TT], lhsT=ones_c[:], rhs=ssq[:, kc, :],
                                        start=(kc == 0), stop=(kc == NCH - 1))
                    inst.then_inc(pe.sem, 2)
                    t.wait_ge(dv.sem, 2 * i + 2)
                    t.matmul(ps_a[:], lhsT=ones_r[:], rhs=sab[:, 0:TT], start=True, stop=True)
                    t.matmul(ps_b[:], lhsT=ones_r[:], rhs=sab[:, TT:2 * TT], start=True, stop=True
                             ).then_inc(bc.sem, 1)

            @blk.vector
            def _(v):
                def step(inst):
                    inst.then_inc(vq.sem, 1)
                    vq.n += 1
                    v.wait_ge(vq.sem, vq.n)
                v.wait_ge(cst.sem, ncst_tot)
                for i in range(NT):
                    v.wait_ge(pe.sem, 2 * (i + 1))
                    step(v.tensor_scalar(out=sst[:, 0:TT], in0=ps_s[:, 0:TT], scalar1=1.0 / D,
                                         scalar2=None, op0=ALU.mult))
                    step(v.tensor_scalar(out=sst[:, TT:2 * TT], in0=ps_s[:, TT:2 * TT],
                                         scalar1=1.0 / D, scalar2=None, op0=ALU.mult))
                    step(v.tensor_tensor(out=sst[:, 2 * TT:3 * TT], in0=sst[:, 0:TT],
                                         in1=sst[:, 0:TT], op=ALU.mult))
                    step(v.tensor_tensor(out=sst[:, 2 * TT:3 * TT], in0=sst[:, TT:2 * TT],
                                         in1=sst[:, 2 * TT:3 * TT], op=ALU.subtract))
                    v.tensor_scalar(out=sst[:, 2 * TT:3 * TT], in0=sst[:, 2 * TT:3 * TT],
                                    scalar1=1e-5, scalar2=None, op0=ALU.add).then_inc(dv.sem, 1)
                    v.wait_ge(sq.sem, 2 * i + 2)
                    step(v.reciprocal(out=sab[:, 0:TT], in_=sst[:, 3 * TT:4 * TT]))
                    step(v.tensor_tensor(out=sab[:, TT:2 * TT], in0=sst[:, 0:TT],
                                         in1=sab[:, 0:TT], op=ALU.mult))
                    v.tensor_scalar(out=sab[:, TT:2 * TT], in0=sab[:, TT:2 * TT],
                                    scalar1=-1.0, scalar2=None, op0=ALU.mult).then_inc(dv.sem, 1)
                    v.wait_ge(bc.sem, i + 1)
                    if i >= 2:
                        v.wait_ge(st[i % 2].sem, 16 * nsp * (i // 2))
                    for kc in range(NCH):
                        step(v.tensor_tensor(out=sap[:, kc, :], in0=sx[i % 2][:, kc, :],
                                             in1=ps_a[:], op=ALU.mult))
                        if gamma is None:
                            inst = v.tensor_tensor(out=sxn[i % 2][:, kc, :], in0=sap[:, kc, :],
                                                   in1=ps_b[:], op=ALU.add)
                            if want_f32:
                                inst = v.tensor_tensor(out=sxf[i % 2][:, kc, :], in0=sap[:, kc, :],
                                                       in1=ps_b[:], op=ALU.add)
                            if want_res:
                                step(v.tensor_tensor(out=sap[:, kc, :], in0=sap[:, kc, :],
                                                     in1=ps_b[:], op=ALU.add))
                        else:
                            step(v.tensor_tensor(out=sap[:, kc, :], in0=sap[:, kc, :],
                                                 in1=ps_b[:], op=ALU.add))
                            if want_res:
                                step(v.tensor_scalar(out=sap[:, kc, :], in0=sap[:, kc, :],
                                                     scalar1=sgb[:, kc:kc + 1],
                                                     scalar2=sgb[:, NCH + kc:NCH + kc + 1],
                                                     op0=ALU.mult, op1=ALU.add))
                            else:
                                inst = v.tensor_scalar(out=sxn[i % 2][:, kc, :], in0=sap[:, kc, :],
                                                       scalar1=sgb[:, kc:kc + 1],
                                                       scalar2=sgb[:, NCH + kc:NCH + kc + 1],
                                                       op0=ALU.mult, op1=ALU.add)
                                if want_f32:
                                    inst = v.tensor_scalar(out=sxf[i % 2][:, kc, :], in0=sap[:, kc, :],
                                                           scalar1=sgb[:, kc:kc + 1],
                                                           scalar2=sgb[:, NCH + kc:NCH + kc + 1],
                                                           op0=ALU.mult, op1=ALU.add)
                        if want_res:
                            inst = v.scalar_tensor_tensor(out=sro[i % 2][:, kc, :],
                                                          in0=sap[:, kc, :], scalar=res_scale,
                                                          in1=srs[i % 2][:, kc, :],
                                                          op0=ALU.mult, op1=ALU.add)
                    inst.then_inc(ap.sem, 1)

            @blk.gpsimd
            def _(g):
                for i in range(NT):
                    g.wait_ge(ap.sem, i + 1)
                    if want_res:
                        g.dma_start(out=fm_view(res_dst, 0, NCH, i * TT, TT),
                                    in_=sro[i % 2][:]).then_inc(st[i % 2].sem, 16)
                    else:
                        g.dma_start(out=fm_view(out_bf, 0, NCH, i * TT, TT),
                                    in_=sxn[i % 2][:]).then_inc(st[i % 2].sem, 16)
                        if want_f32:
                            g.dma_start(out=fm_view(out_f32, 0, NCH, i * TT, TT),
                                        in_=sxf[i % 2][:]).then_inc(st[i % 2].sem, 16)
                g.wait_ge(st[0].sem, 16 * nsp * ((NT + 1) // 2))
                g.wait_ge(st[1].sem, 16 * nsp * (NT // 2))
        phase_end(nc, pp)



def emit_ffn0(nc, IN, SC, xres_src, xres_dst, tag):
    with ExitStack() as ctx:
        pp = P(nc, ctx)
        sM = pp.sbuf(f"{tag}_M", [128, NCH, D], BF16)
        sb = pp.sbuf(f"{tag}_b", [128, NCH], F32)
        sxn = [pp.sbuf(f"{tag}_x{b}", [128, NCH, TT], BF16) for b in range(2)]
        sxr = [pp.sbuf(f"{tag}_r{b}", [128, NCH, TT], F32) for b in range(2)]
        syo = [pp.sbuf(f"{tag}_y{b}", [128, NCH, TT], F32) for b in range(2)]
        ps = [pp.psum(f"{tag}_p{b}", [128, TT], F32) for b in range(4)]
        cst = pp.ctr(f"{tag}_cs")
        ld = [pp.ctr(f"{tag}_ld{b}") for b in range(2)]
        mm = pp.ctr(f"{tag}_mm")
        vq = pp.ctr(f"{tag}_vq")
        ev = pp.ctr(f"{tag}_ev")
        st = [pp.ctr(f"{tag}_st{b}") for b in range(2)]

        with nc.Block() as blk:
            @blk.sync
            def _(sy):
                sy.dma_start(out=sM[:], in_=SC["M_bf"].ap().rearrange("(c p) m -> p c m", p=128)
                             ).then_inc(cst.sem, 16)
                sy.dma_start(out=sb[:], in_=IN["ffn_bias"].rearrange("(c p) -> p c", p=128),
                             allow_slow_non_contiguous=True).then_inc(cst.sem, 16)
                for i in range(NT):
                    if i >= 2:
                        sy.wait_ge(st[i % 2].sem, 16 * NCH * (i // 2))
                    sy.dma_start(out=sxn[i % 2][:],
                                 in_=fm_view(SC["xn_bf"].ap(), 0, NCH, i * TT, TT)
                                 ).then_inc(ld[i % 2].sem, 16)
                    sy.dma_start(out=sxr[i % 2][:],
                                 in_=fm_view(xres_src, 0, NCH, i * TT, TT)).then_inc(ld[i % 2].sem, 16)

            @blk.tensor
            def _(t):
                t.wait_ge(cst.sem, 32)
                for i in range(NT):
                    t.wait_ge(ld[i % 2].sem, 32 * (i // 2 + 1))
                    for mc in range(NCH):
                        b = mc % 4
                        tot = i * NCH + mc
                        if tot >= 4:
                            t.wait_ge(ev.sem, tot - 3)
                        for kc in range(NCH):
                            inst = t.matmul(ps[b][:], lhsT=sM[:, kc, mc * 128:(mc + 1) * 128],
                                            rhs=sxn[i % 2][:, kc, :],
                                            start=(kc == 0), stop=(kc == NCH - 1))
                        inst.then_inc(mm.sem, 1)

            @blk.vector
            def _(v):
                for i in range(NT):
                    for mc in range(NCH):
                        b = mc % 4
                        tot = i * NCH + mc
                        v.wait_ge(mm.sem, tot + 1)
                        if i >= 2:
                            v.wait_ge(st[i % 2].sem, 16 * NCH * (i // 2))
                        v.tensor_scalar(out=syo[i % 2][:, mc, :], in0=ps[b][:],
                                        scalar1=sb[:, mc:mc + 1], scalar2=0.5,
                                        op0=ALU.add, op1=ALU.mult).then_inc(vq.sem, 1)
                        vq.n += 1
                        v.wait_ge(vq.sem, vq.n)
                        v.tensor_tensor(out=syo[i % 2][:, mc, :], in0=syo[i % 2][:, mc, :],
                                        in1=sxr[i % 2][:, mc, :], op=ALU.add).then_inc(ev.sem, 1)

            @blk.gpsimd
            def _(g):
                for i in range(NT):
                    for mc in range(NCH):
                        g.wait_ge(ev.sem, i * NCH + mc + 1)
                        g.dma_start(out=xres_dst[mc * 128:(mc + 1) * 128, i * TT:(i + 1) * TT],
                                    in_=syo[i % 2][:, mc, :]).then_inc(st[i % 2].sem, 16)
                g.wait_ge(st[0].sem, 16 * NCH * ((NT + 1) // 2))
                g.wait_ge(st[1].sem, 16 * NCH * (NT // 2))
        phase_end(nc, pp)





def emit_proj(nc, IN, tag, W_bf, K, M, x_bf, out_dram, out_dt, bias=None,
              post_scale=None, res_f32=None, f32_in=False, div_den=None):
    KCH, MCH = K // 128, (M + 127) // 128
    mrows = min(128, M)
    DTI = F32 if f32_in else BF16
    with ExitStack() as ctx:
        pp = P(nc, ctx)
        sW = pp.sbuf(f"{tag}_W", [128, KCH, M], DTI)
        sB = pp.sbuf(f"{tag}_b", [mrows, MCH], F32) if bias is not None else None
        sx = [pp.sbuf(f"{tag}_x{b}", [128, KCH, TT], DTI) for b in range(2)]
        sr = [pp.sbuf(f"{tag}_r{b}", [mrows, MCH, TT], F32) for b in range(2)] if res_f32 is not None else None
        so = [pp.sbuf(f"{tag}_o{b}", [mrows, MCH, TT], out_dt) for b in range(2)]
        ps = [pp.psum(f"{tag}_p{b}", [mrows, TT], F32) for b in range(4)]
        cst = pp.ctr(f"{tag}_cs")
        ld = [pp.ctr(f"{tag}_ld{b}") for b in range(2)]
        mm = pp.ctr(f"{tag}_mm"); vq = pp.ctr(f"{tag}_vq")
        ev = pp.ctr(f"{tag}_ev")
        st = [pp.ctr(f"{tag}_st{b}") for b in range(2)]
        nldp = 32 if res_f32 is not None else 16
        ncst = 32 if bias is not None else 16
        if div_den is not None:
            srdn = pp.sbuf(f"{tag}_rdn", [16, L], F32)
            ssel2 = pp.sbuf(f"{tag}_sl2", [16, 8, 128], F32)
            ps_db = [pp.psum(f"{tag}_db{b}", [128, TT], F32) for b in range(2)]
            dnl = pp.ctr(f"{tag}_dnl"); dnr = pp.ctr(f"{tag}_dnr")
            dbb = pp.ctr(f"{tag}_dbb"); dvx = pp.ctr(f"{tag}_dvx")

        with nc.Block() as blk:
            @blk.sync
            def _(sy):
                if div_den is not None:
                    sy.dma_start(out=srdn[:], in_=div_den).then_inc(dnl.sem, 16)
                    sy.dma_start(out=ssel2[:], in_=IN["sel2"].rearrange("j (c p) -> j c p", p=128)
                                 ).then_inc(dnl.sem, 16)
                sy.dma_start(out=sW[:], in_=W_bf.rearrange("(c p) m -> p c m", p=128),
                             allow_slow_non_contiguous=(M < 128)).then_inc(cst.sem, 16)
                if bias is not None:
                    sy.dma_start(out=sB[:], in_=bias.rearrange("(c p) -> p c", p=mrows),
                                 allow_slow_non_contiguous=True).then_inc(cst.sem, 16)
                for i in range(NT):
                    if i >= 2:
                        sy.wait_ge(st[i % 2].sem, 16 * MCH * (i // 2))
                    sy.dma_start(out=sx[i % 2][:], in_=fm_view(x_bf, 0, KCH, i * TT, TT)
                                 ).then_inc(ld[i % 2].sem, 16)
                    if res_f32 is not None:
                        sy.dma_start(out=sr[i % 2][:],
                                     in_=fm_view(res_f32, 0, MCH, i * TT, TT)[:mrows]
                                     ).then_inc(ld[i % 2].sem, 16)

            @blk.tensor
            def _(t):
                t.wait_ge(cst.sem, ncst)
                if div_den is not None:
                    t.wait_ge(dnl.sem, 32)
                    t.wait_ge(dnr.sem, 1)
                for i in range(NT):
                    t.wait_ge(ld[i % 2].sem, nldp * (i // 2 + 1))
                    if div_den is not None:
                        for kc in range(KCH):
                            gdb = i * KCH + kc
                            if gdb >= 2:
                                t.wait_ge(dvx.sem, gdb - 1)
                            t.matmul(ps_db[gdb % 2][:], lhsT=ssel2[:, kc, :],
                                     rhs=srdn[:, i * TT:(i + 1) * TT],
                                     start=True, stop=True).then_inc(dbb.sem, 1)
                        t.wait_ge(dvx.sem, i * KCH + KCH)
                    for mc in range(MCH):
                        tot = i * MCH + mc
                        b = tot % 4
                        if tot >= 4:
                            t.wait_ge(ev.sem, tot - 3)
                        for kc in range(KCH):
                            inst = t.matmul(ps[b][:],
                                            lhsT=sW[:, kc, mc * 128:mc * 128 + mrows],
                                            rhs=sx[i % 2][:, kc, :],
                                            start=(kc == 0), stop=(kc == KCH - 1))
                        inst.then_inc(mm.sem, 1)

            @blk.vector
            def _(v):
                if div_den is not None:
                    v.wait_ge(dnl.sem, 32)
                    v.reciprocal(out=srdn[:], in_=srdn[:]).then_inc(dnr.sem, 1)
                for i in range(NT):
                    if div_den is not None:
                        for kc in range(KCH):
                            gdb = i * KCH + kc
                            v.wait_ge(dbb.sem, gdb + 1)
                            v.tensor_tensor(out=sx[i % 2][:, kc, :], in0=sx[i % 2][:, kc, :],
                                            in1=ps_db[gdb % 2][:], op=ALU.mult
                                            ).then_inc(dvx.sem, 1)
                    for mc in range(MCH):
                        tot = i * MCH + mc
                        b = tot % 4
                        v.wait_ge(mm.sem, tot + 1)
                        if i >= 2:
                            v.wait_ge(st[i % 2].sem, 16 * MCH * (i // 2))
                        dst = so[i % 2][:, mc, :]
                        if res_f32 is not None:
                            if bias is not None:
                                v.tensor_scalar(out=dst, in0=ps[b][:],
                                                scalar1=sB[:, mc:mc + 1], scalar2=None,
                                                op0=ALU.add).then_inc(vq.sem, 1)
                                vq.n += 1
                                v.wait_ge(vq.sem, vq.n)
                                v.tensor_tensor(out=dst, in0=dst, in1=sr[i % 2][:, mc, :],
                                                op=ALU.add).then_inc(ev.sem, 1)
                            else:
                                v.tensor_tensor(out=dst, in0=ps[b][:], in1=sr[i % 2][:, mc, :],
                                                op=ALU.add).then_inc(ev.sem, 1)
                        elif bias is not None and post_scale is not None:
                            v.tensor_scalar(out=dst, in0=ps[b][:], scalar1=sB[:, mc:mc + 1],
                                            scalar2=post_scale, op0=ALU.add, op1=ALU.mult
                                            ).then_inc(ev.sem, 1)
                        elif post_scale is not None:
                            v.tensor_scalar(out=dst, in0=ps[b][:], scalar1=post_scale,
                                            scalar2=None, op0=ALU.mult).then_inc(ev.sem, 1)
                        elif bias is not None:
                            v.tensor_scalar(out=dst, in0=ps[b][:], scalar1=sB[:, mc:mc + 1],
                                            scalar2=None, op0=ALU.add).then_inc(ev.sem, 1)
                        else:
                            v.tensor_copy(out=dst, in_=ps[b][:]).then_inc(ev.sem, 1)

            @blk.gpsimd
            def _(g):
                for i in range(NT):
                    for mc in range(MCH):
                        g.wait_ge(ev.sem, i * MCH + mc + 1)
                        g.dma_start(
                            out=out_dram[mc * 128:mc * 128 + mrows, i * TT:(i + 1) * TT],
                            in_=so[i % 2][:, mc, :]).then_inc(st[i % 2].sem, 16)
                g.wait_ge(st[0].sem, 16 * MCH * ((NT + 1) // 2))
                g.wait_ge(st[1].sem, 16 * MCH * (NT // 2))
        phase_end(nc, pp)


def emit_vaug(nc, IN, tag, Wv_bf, x_bf, bias, out_dram, nh=16, hd=64):
    M = nh * hd
    CW = hd + 1
    NTOK = L // 128
    wide = hd >= 512
    HPH = 1 if wide else 512 // hd
    nhalf = M // 512
    with ExitStack() as ctx:
        pp = P(nc, ctx)
        sW = pp.sbuf(f"{tag}_W", [128, NCH, M], BF16)
        sbv = pp.sbuf(f"{tag}_bv", [1, M], F32)
        sbb = pp.sbuf(f"{tag}_bb", [128, M], F32)
        ones_r = pp.sbuf(f"{tag}_onr", [1, 128], F32)
        sx = [pp.sbuf(f"{tag}_x{b}", [128, NCH, 128], BF16) for b in range(2)]
        so = [pp.sbuf(f"{tag}_o{b}", [128, nh * CW], BF16) for b in range(2)]
        ps = [pp.psum(f"{tag}_p{b}", [128, 512], F32) for b in range(2)]
        psb = pp.psum(f"{tag}_pb", [128, 512], F32)
        cst = pp.ctr(f"{tag}_cs"); bb = pp.ctr(f"{tag}_bb")
        ld = [pp.ctr(f"{tag}_ld{b}") for b in range(2)]
        mm = pp.ctr(f"{tag}_mm"); ev = pp.ctr(f"{tag}_ev"); msk = pp.ctr(f"{tag}_mk")
        st = [pp.ctr(f"{tag}_st{b}") for b in range(2)]

        with nc.Block() as blk:
            @blk.sync
            def _(sy):
                sy.dma_start(out=sW[:], in_=Wv_bf.rearrange("(c p) m -> p c m", p=128)
                             ).then_inc(cst.sem, 16)
                sy.dma_start(out=sbv[:], in_=bias.rearrange("(a m) -> a m", a=1)
                             ).then_inc(cst.sem, 16)
                sy.dma_start(out=ones_r[:], in_=IN["ones_row"]).then_inc(cst.sem, 16)
                for i in range(NTOK):
                    if i >= 2:
                        sy.wait_ge(st[i % 2].sem, 16 * (i // 2))
                    sy.dma_start(out=sx[i % 2][:],
                                 in_=fm_view(x_bf, 0, NCH, i * 128, 128)
                                 ).then_inc(ld[i % 2].sem, 16)

            @blk.tensor
            def _(t):
                t.wait_ge(cst.sem, 48)
                for half in range(nhalf):
                    t.matmul(psb[:], lhsT=ones_r[:], rhs=sbv[:, half * 512:(half + 1) * 512],
                             start=True, stop=True).then_inc(bb.sem, 1)
                    t.wait_ge(bb.sem, 2 * half + 2)
                for i in range(NTOK):
                    t.wait_ge(ld[i % 2].sem, 16 * (i // 2 + 1))
                    for half in range(nhalf):
                        tot = nhalf * i + half
                        if tot >= 2:
                            t.wait_ge(ev.sem, tot - 1)
                        for kc in range(NCH):
                            inst = t.matmul(ps[tot % 2][:],
                                            lhsT=sx[i % 2][:, kc, :],
                                            rhs=sW[:, kc, half * 512:(half + 1) * 512],
                                            start=(kc == 0), stop=(kc == NCH - 1))
                        inst.then_inc(mm.sem, 1)

            @blk.vector
            def _(v):
                for half in range(nhalf):
                    v.wait_ge(bb.sem, 2 * half + 1)
                    v.tensor_copy(out=sbb[:, half * 512:(half + 1) * 512], in_=psb[:]
                                  ).then_inc(bb.sem, 1)
                for i in range(NTOK):
                    if i >= 2:
                        v.wait_ge(st[i % 2].sem, 16 * (i // 2))
                    v.memset(so[i % 2][:].rearrange("p (h c) -> p h c", c=CW)[:, :, hd:hd + 1],
                             1.0).then_inc(msk.sem, 1)
                    for half in range(nhalf):
                        tot = nhalf * i + half
                        v.wait_ge(mm.sem, tot + 1)
                        if wide:
                            out_sl = so[i % 2][:, half * 512:(half + 1) * 512]
                            in1_sl = sbb[:, half * 512:(half + 1) * 512]
                            v.tensor_tensor(out=out_sl, in0=ps[tot % 2][:], in1=in1_sl,
                                            op=ALU.add).then_inc(ev.sem, 1)
                        else:
                            v.tensor_tensor(
                                out=so[i % 2][:].rearrange("p (h c) -> p h c", c=CW)[:, half * HPH:(half + 1) * HPH, 0:hd],
                                in0=ps[tot % 2][:].rearrange("p (h c) -> p h c", c=hd),
                                in1=sbb[:, half * 512:(half + 1) * 512].rearrange("p (h c) -> p h c", c=hd),
                                op=ALU.add).then_inc(ev.sem, 1)

            @blk.gpsimd
            def _(g):
                for i in range(NTOK):
                    g.wait_ge(ev.sem, nhalf * (i + 1))
                    g.wait_ge(msk.sem, i + 1)
                    g.dma_start(out=out_dram[i * 128:(i + 1) * 128, :],
                                in_=so[i % 2][:]).then_inc(st[i % 2].sem, 16)
                g.wait_ge(st[0].sem, 16 * ((NTOK + 1) // 2))
                g.wait_ge(st[1].sem, 16 * (NTOK // 2))
        phase_end(nc, pp)


def emit_l1attn(nc, IN, tag, q_bf, k_bf, vaug_bf, out_dram, den_dram):
    NH, HD = 16, 64
    NG = 16
    with ExitStack() as ctx:
        pp = P(nc, ctx)
        ones_c = pp.sbuf(f"{tag}_onc", [128, 1], F32)
        ones_r = pp.sbuf(f"{tag}_onr", [1, 128], F32)
        sBn = pp.sbuf(f"{tag}_Bn", [128, NH], F32)
        cc = pp.ctr(f"{tag}_cc")

        c1s = ExitStack()
        p1s = P(nc, c1s)
        sqk = p1s.sbuf(f"{tag}_nx", [128, NCH, TT], BF16)
        ssq = p1s.sbuf(f"{tag}_nsq", [128, NCH, TT], F32)
        snr = p1s.sbuf(f"{tag}_nr", [1, 96], F32)
        ps_n = [p1s.psum(f"{tag}_pn{b}", [1, TT], F32) for b in range(2)]
        ps_B = p1s.psum(f"{tag}_pB", [128, NH], F32)
        l1 = pp.ctr(f"{tag}_l1"); a1 = pp.ctr(f"{tag}_a1")
        p1 = pp.ctr(f"{tag}_p1"); vq = pp.ctr(f"{tag}_vq"); b1 = pp.ctr(f"{tag}_b1")

        with nc.Block() as blk:
            @blk.sync
            def _(sy):
                sy.dma_start(out=ones_c[:], in_=IN["ones_row"]).then_inc(cc.sem, 16)
                sy.dma_start(out=ones_r[:], in_=IN["ones_row"]).then_inc(cc.sem, 16)
                for j in range(2 * NT):
                    src = [q_bf, k_bf][j // NT]
                    if j >= 1:
                        sy.wait_ge(a1.sem, j)
                    sy.dma_start(out=sqk[:], in_=fm_view(src, 0, NCH, (j % NT) * TT, TT)
                                 ).then_inc(l1.sem, 16)

            @blk.scalar
            def _(s):
                for j in range(2 * NT):
                    s.wait_ge(l1.sem, 16 * (j + 1))
                    if j >= 1:
                        s.wait_ge(p1.sem, j * NH)
                    s.activation(out=ssq[:], in_=sqk[:], func=AF.Square).then_inc(a1.sem, 1)
                s.wait_ge(b1.sem, 1)
                s.activation(out=snr[:, 48:64], in_=snr[:, 32:48], func=AF.Sqrt
                             ).then_inc(b1.sem, 1)

            @blk.tensor
            def _(t):
                t.wait_ge(cc.sem, 32)
                for j in range(2 * NT):
                    t.wait_ge(a1.sem, j + 1)
                    for h in range(NH):
                        g = j * NH + h
                        hc, ho = (h * HD) // 128, (h * HD) % 128
                        if g >= 2:
                            t.wait_ge(vq.sem, 2 * g - 1)
                        t.matmul(ps_n[g % 2][:], lhsT=ones_c[ho:ho + HD, :],
                                 rhs=ssq[ho:ho + HD, hc, :],
                                 start=True, stop=True).then_inc(p1.sem, 1)
                t.wait_ge(b1.sem, 3)
                t.matmul(ps_B[:], lhsT=ones_r[:], rhs=snr[:, 64:80],
                         start=True, stop=True).then_inc(b1.sem, 1)

            @blk.vector
            def _(v):
                def stepv(inst):
                    inst.then_inc(vq.sem, 1)
                    vq.n += 1
                    v.wait_ge(vq.sem, vq.n)
                stepv(v.memset(snr[:], 0.0))
                for j in range(2 * NT):
                    for h in range(NH):
                        g = j * NH + h
                        col = (j // NT) * NH + h
                        v.wait_ge(p1.sem, g + 1)
                        stepv(v.tensor_reduce(out=snr[:, 80:81], in_=ps_n[g % 2][:],
                                              axis=AX.X, op=ALU.max))
                        stepv(v.tensor_tensor(out=snr[:, col:col + 1], in0=snr[:, col:col + 1],
                                              in1=snr[:, 80:81], op=ALU.max))
                v.tensor_tensor(out=snr[:, 32:48], in0=snr[:, 0:16],
                                in1=snr[:, 16:32], op=ALU.mult).then_inc(b1.sem, 1)
                v.wait_ge(b1.sem, 2)
                v.tensor_scalar(out=snr[:, 64:80], in0=snr[:, 48:64],
                                scalar1=-1.0, scalar2=None, op0=ALU.mult).then_inc(b1.sem, 1)
                v.wait_ge(b1.sem, 4)
                v.tensor_copy(out=sBn[:], in_=ps_B[:]).then_inc(b1.sem, 1)

        nc.all_engine_barrier()
        c1s.close()

        sK = [pp.sbuf(f"{tag}_K{b}", [128, L], BF16) for b in range(3)]
        sV = [pp.sbuf(f"{tag}_V{b}", [128, 32, HD + 1], BF16) for b in range(3)]
        sq = [pp.sbuf(f"{tag}_q{b}", [128, TT], BF16) for b in range(2)]
        sP = [pp.sbuf(f"{tag}_P{b}", [128, 2, TT], BF16) for b in range(3)]
        sAo = [pp.sbuf(f"{tag}_A{b}", [64, TT], BF16) for b in range(2)]
        sDe = [pp.sbuf(f"{tag}_D{b}", [1, TT], F32) for b in range(2)]
        ps_s = [pp.psum(f"{tag}_ss{b}", [128, 2, TT], F32) for b in range(3)]
        ps_o = [pp.psum(f"{tag}_po{b}", [65, TT], F32) for b in range(2)]
        z1 = pp.ctr(f"{tag}_z1")
        kvld = [pp.ctr(f"{tag}_kv{b}") for b in range(3)]
        qld = [pp.ctr(f"{tag}_ql{b}") for b in range(2)]
        ms = pp.ctr(f"{tag}_ms"); ex = pp.ctr(f"{tag}_ex"); pv = pp.ctr(f"{tag}_pv")
        oc = pp.ctr(f"{tag}_oc")
        stq = [pp.ctr(f"{tag}_sq{b}") for b in range(2)]
        NU = NH * 8

        with nc.Block() as blk:
            @blk.vector
            def _(v):
                for b in range(3):
                    v.memset(sK[b][64:128, :], 0.0).then_inc(z1.sem, 1)
                v.memset(sq[0][64:128, :], 0.0).then_inc(z1.sem, 1)
                v.memset(sq[1][64:128, :], 0.0).then_inc(z1.sem, 1)
                for u in range(NU):
                    v.wait_ge(pv.sem, NG * (u + 1))
                    if u >= 2:
                        v.wait_ge(stq[u % 2].sem, 32 * (u // 2))
                    v.tensor_copy(out=sAo[u % 2][:], in_=ps_o[u % 2][0:64, :])
                    v.tensor_copy(out=sDe[u % 2][:], in_=ps_o[u % 2][64:65, :]
                                  ).then_inc(oc.sem, 1)

            @blk.sync
            def _(sy):
                sy.wait_ge(z1.sem, 5)
                for h in range(NH):
                    for qt in range(8):
                        u = h * 8 + qt
                        if u >= 2:
                            sy.wait_ge(stq[u % 2].sem, 32 * (u // 2))
                        sy.dma_start(out=sq[u % 2][0:64, :],
                                     in_=q_bf[h * 64:(h + 1) * 64, qt * TT:(qt + 1) * TT]
                                     ).then_inc(qld[u % 2].sem, 16)

            @blk.scalar
            def _(s):
                s.wait_ge(b1.sem, 5)
                for u in range(NU):
                    h = u // 8
                    for g in range(NG):
                        G = u * NG + g
                        s.wait_ge(ms.sem, G + 1)
                        if G >= 3:
                            s.wait_ge(pv.sem, G - 2)
                        s.activation(out=sP[G % 3][:], in_=ps_s[G % 3][:],
                                     func=AF.Exp, bias=sBn[:, h:h + 1], scale=1.0
                                     ).then_inc(ex.sem, 1)

            @blk.tensor
            def _(t):
                t.wait_ge(b1.sem, 5)
                for u in range(NU):
                    h, qt = u // 8, u % 8
                    t.wait_ge(kvld[h % 3].sem, 32 * (h // 3 + 1))
                    t.wait_ge(qld[u % 2].sem, 16 * (u // 2 + 1))
                    if u >= 2:
                        t.wait_ge(oc.sem, u - 1)
                    for g in range(NG):
                        G = u * NG + g
                        if G >= 3:
                            t.wait_ge(ex.sem, G - 2)
                        for j in range(2):
                            kt = 2 * g + j
                            inst = t.matmul(ps_s[G % 3][:, j, :],
                                            lhsT=sK[h % 3][:, kt * 128:(kt + 1) * 128],
                                            rhs=sq[u % 2][:],
                                            start=True, stop=True)
                        inst.then_inc(ms.sem, 1)
                        if g >= 1:
                            t.wait_ge(ex.sem, G)
                            for j in range(2):
                                kt = 2 * (g - 1) + j
                                inst = t.matmul(ps_o[u % 2][:], lhsT=sV[h % 3][:, kt, :],
                                                rhs=sP[(G - 1) % 3][:, j, :],
                                                start=(g - 1 == 0 and j == 0), stop=False)
                            inst.then_inc(pv.sem, 1)
                    Gl = u * NG + NG - 1
                    t.wait_ge(ex.sem, Gl + 1)
                    for j in range(2):
                        kt = 2 * (NG - 1) + j
                        inst = t.matmul(ps_o[u % 2][:], lhsT=sV[h % 3][:, NG * 2 - 2 + j, :],
                                        rhs=sP[Gl % 3][:, j, :],
                                        start=False, stop=(j == 1))
                    inst.then_inc(pv.sem, 1)

            @blk.gpsimd
            def _(g_e):
                def ldkv(h):
                    g_e.dma_start(out=sK[h % 3][0:64, :], in_=k_bf[h * 64:(h + 1) * 64, :]
                                  ).then_inc(kvld[h % 3].sem, 16)
                    g_e.dma_start(out=sV[h % 3][:],
                                  in_=vaug_bf[:, h * 65:(h + 1) * 65].rearrange(
                                      "(kt p) c -> p kt c", p=128)
                                  ).then_inc(kvld[h % 3].sem, 16)
                g_e.wait_ge(z1.sem, 5)
                ldkv(0); ldkv(1)
                for u in range(NU):
                    h, qt = u // 8, u % 8
                    if qt == 0 and h + 2 < NH:
                        if h >= 1:
                            g_e.wait_ge(oc.sem, 8 * h)
                        ldkv(h + 2)
                    g_e.wait_ge(oc.sem, u + 1)
                    g_e.dma_start(out=out_dram[h * 64:(h + 1) * 64, qt * TT:(qt + 1) * TT],
                                  in_=sAo[u % 2][:]).then_inc(stq[u % 2].sem, 16)
                    g_e.dma_start(out=den_dram[h:h + 1, qt * TT:(qt + 1) * TT],
                                  in_=sDe[u % 2][:]).then_inc(stq[u % 2].sem, 16)
                g_e.wait_ge(stq[0].sem, 32 * (NU // 2))
                g_e.wait_ge(stq[1].sem, 32 * (NU // 2))
        phase_end(nc, pp)


def emit_router_masks(nc, IN, tag, keys_dram, emask_dram, idx_dram):
    CAP = MOE_CAP
    NTI = L // 128
    with ExitStack() as ctx:
        pp = P(nc, ctx)
        sk = pp.sbuf(f"{tag}_k", [1, L], F32)
        si = pp.sbuf(f"{tag}_i", [1, L], I32)
        st_ = pp.sbuf(f"{tag}_t", [1, L], F32)
        sl = pp.sbuf(f"{tag}_l", [1, L], F32)
        sm = pp.sbuf(f"{tag}_m", [1, 4 * L], F32)
        semk = pp.sbuf(f"{tag}_em", [4, L], F32)
        s_idf = pp.sbuf(f"{tag}_idf", [4, 4], F32)
        sCsel = pp.sbuf(f"{tag}_cs", [128, 32, 32], F32)
        sRsel = pp.sbuf(f"{tag}_rs", [32, 32, 128], F32)
        sStriu = pp.sbuf(f"{tag}_su", [34, 32], F32)
        sTriu = pp.sbuf(f"{tag}_tu", [128, 128], F32)
        sIotaT = pp.sbuf(f"{tag}_it", [128, 32], I32)
        sOh = pp.sbuf(f"{tag}_oh", [128, 32, 4], F32)
        sCum = pp.sbuf(f"{tag}_cu", [128, 32, 4], F32)
        sS34 = pp.sbuf(f"{tag}_s34", [34, 4], F32)
        sOff = pp.sbuf(f"{tag}_of", [32, 4], F32)
        sT4 = pp.sbuf(f"{tag}_t4", [128, 4], F32)
        sSlotF = pp.sbuf(f"{tag}_sf", [128, 1], F32)
        sSlotI = pp.sbuf(f"{tag}_si", [128, 32], I32)
        ps_a = pp.psum(f"{tag}_pa", [128, 4], F32)
        ps_c = pp.psum(f"{tag}_pc", [128, 4], F32)
        ps_S = pp.psum(f"{tag}_pS", [32, 4], F32)
        ps_of = pp.psum(f"{tag}_pf", [32, 4], F32)
        ps_b2 = pp.psum(f"{tag}_pb", [128, 4], F32)
        l2 = pp.ctr(f"{tag}_l2"); vq = pp.ctr(f"{tag}_vq"); dn = pp.ctr(f"{tag}_dn")
        dm = pp.ctr(f"{tag}_dm"); cl = pp.ctr(f"{tag}_cl"); eml = pp.ctr(f"{tag}_el")
        pa = pp.ctr(f"{tag}_pac"); ohc = pp.ctr(f"{tag}_ohc")
        pc = pp.ctr(f"{tag}_pcc"); cmc = pp.ctr(f"{tag}_cmc")
        ac = pp.ctr(f"{tag}_ac"); s34 = pp.ctr(f"{tag}_s34c"); of2 = pp.ctr(f"{tag}_of2")
        pb2 = pp.ctr(f"{tag}_pb2"); s2a = pp.ctr(f"{tag}_s2a"); slc = pp.ctr(f"{tag}_slc")
        pf = pp.ctr(f"{tag}_pf2"); sc2 = pp.ctr(f"{tag}_sc2")

        with nc.Block() as blk:
            @blk.sync
            def _(sy):
                sy.dma_start(out=sk[:], in_=keys_dram).then_inc(l2.sem, 16)
                sy.dma_start(out=s_idf[:], in_=IN["ident_f32"][0:4, 0:4]).then_inc(cl.sem, 16)
                sy.dma_start(out=sCsel[:],
                             in_=IN["colsel"].rearrange("p (a q) -> p a q", q=32)
                             ).then_inc(cl.sem, 16)
                sy.dma_start(out=sRsel[:],
                             in_=IN["rowsel"].rearrange("p (a q) -> p a q", q=128)
                             ).then_inc(cl.sem, 16)
                sy.dma_start(out=sStriu[:], in_=IN["striu34"]).then_inc(cl.sem, 16)
                sy.dma_start(out=sTriu[:], in_=IN["triuI"]).then_inc(cl.sem, 16)
                sy.dma_start(out=sIotaT[:], in_=IN["iota_tok"]).then_inc(cl.sem, 16)
                sy.dma_start(out=sS34[32:34, :], in_=IN["ce4m1"]).then_inc(cl.sem, 16)
                sy.wait_ge(dm.sem, 16)
                sy.dma_start(out=semk[:], in_=emask_dram).then_inc(eml.sem, 16)

            @blk.vector
            def _(v):
                def stepv(inst):
                    inst.then_inc(vq.sem, 1)
                    vq.n += 1
                    v.wait_ge(vq.sem, vq.n)
                v.wait_ge(l2.sem, 16)
                stepv(v.tensor_copy(out=si[:], in_=sk[:]))
                stepv(v.tensor_copy(out=st_[:], in_=si[:]))
                stepv(v.tensor_tensor(out=sl[:], in0=sk[:], in1=st_[:], op=ALU.is_lt))
                stepv(v.tensor_tensor(out=st_[:], in0=st_[:], in1=sl[:], op=ALU.subtract))
                stepv(v.tensor_copy(out=si[:], in_=st_[:]))
                stepv(v.tensor_scalar(out=si[:], in0=si[:], scalar1=3, scalar2=None,
                                      op0=ALU.bitwise_and))
                stepv(v.tensor_copy(out=st_[:], in_=si[:]))
                for e in range(4):
                    inst = v.tensor_scalar(out=sm[:, e * L:(e + 1) * L], in0=st_[:],
                                           scalar1=float(e), scalar2=None, op0=ALU.is_equal)
                inst.then_inc(dn.sem, 1)
                for ti in range(NTI):
                    v.wait_ge(pa.sem, ti + 1)
                    v.tensor_copy(out=sOh[:, ti, :], in_=ps_a[:]).then_inc(ohc.sem, 1)
                    v.wait_ge(pc.sem, ti + 1)
                    v.tensor_copy(out=sCum[:, ti, :], in_=ps_c[:]).then_inc(cmc.sem, 1)
                v.wait_ge(ac.sem, 1)
                v.tensor_copy(out=sS34[0:32, :], in_=ps_S[:]).then_inc(s34.sem, 1)
                v.wait_ge(of2.sem, 1)
                v.tensor_copy(out=sOff[:], in_=ps_of[:]).then_inc(of2.sem, 1)
                for ti in range(NTI):
                    v.wait_ge(pb2.sem, ti + 1)
                    v.tensor_tensor(out=sT4[:], in0=ps_b2[:], in1=sCum[:, ti, :],
                                    op=ALU.add).then_inc(s2a.sem, 1)
                    v.wait_ge(s2a.sem, ti + 1)
                    stepv(v.tensor_tensor(out=sT4[:], in0=sT4[:], in1=sOh[:, ti, :],
                                          op=ALU.mult))
                    stepv(v.tensor_reduce(out=sSlotF[:], in_=sT4[:], axis=AX.X, op=ALU.add))
                    v.tensor_copy(out=sSlotI[:, ti:ti + 1], in_=sSlotF[:]
                                  ).then_inc(slc.sem, 1)

            @blk.tensor
            def _(t):
                t.wait_ge(cl.sem, 112)
                t.wait_ge(eml.sem, 16)
                for ti in range(NTI):
                    if ti >= 1:
                        t.wait_ge(ohc.sem, ti)
                    t.transpose(ps_a[:], in_=semk[:, ti * 128:(ti + 1) * 128],
                                identity=s_idf[:]).then_inc(pa.sem, 1)
                    t.wait_ge(ohc.sem, ti + 1)
                    if ti >= 1:
                        t.wait_ge(cmc.sem, ti)
                    t.matmul(ps_c[:], lhsT=sTriu[:], rhs=sOh[:, ti, :],
                             start=True, stop=True).then_inc(pc.sem, 1)
                    inst = t.matmul(ps_S[:], lhsT=sCsel[:, ti, :], rhs=sOh[:, ti, :],
                                    start=(ti == 0), stop=(ti == NTI - 1))
                inst.then_inc(ac.sem, 1)
                t.wait_ge(s34.sem, 1)
                t.matmul(ps_of[:], lhsT=sStriu[:], rhs=sS34[:],
                         start=True, stop=True).then_inc(of2.sem, 1)
                t.wait_ge(of2.sem, 2)
                for ti in range(NTI):
                    if ti >= 1:
                        t.wait_ge(s2a.sem, ti)
                    t.matmul(ps_b2[:], lhsT=sRsel[:, ti, :], rhs=sOff[:],
                             start=True, stop=True).then_inc(pb2.sem, 1)

            @blk.gpsimd
            def _(g):
                g.wait_ge(dn.sem, 1)
                g.dma_start(out=emask_dram.rearrange("e t -> (e t)").rearrange(
                    "(a n) -> a n", a=1), in_=sm[:]).then_inc(dm.sem, 16)
                g.dma_start(out=idx_dram.rearrange("n b -> b (n)"),
                            in_=IN["oobfill"]).then_inc(pf.sem, 16)
                g.wait_ge(pf.sem, 16)
                for ti in range(NTI):
                    g.wait_ge(slc.sem, ti + 1)
                    g.wait_ge(sc2.sem, 16 * ti)
                    g.indirect_dma_start(
                        out=idx_dram,
                        out_offset=IndirectOffsetOnAxis(sSlotI[:, ti:ti + 1], 0),
                        in_=sIotaT[:, ti:ti + 1], in_offset=None,
                    ).then_inc(sc2.sem, 16)
                g.wait_ge(sc2.sem, 16 * NTI)
        phase_end(nc, pp)


def emit_moe(nc, IN, tag, x_bf, emask_dram, W1_bf, b1_f, W2_bf, b2_f, ffacc_dram):
    FF = 4096
    FCH = FF // 128
    with ExitStack() as ctx:
        pp = P(nc, ctx)
        ones_r = pp.sbuf(f"{tag}_onr", [1, 128], F32)
        sW1 = pp.sbuf(f"{tag}_W1", [128, NCH, FF], BF16)
        sW2 = pp.sbuf(f"{tag}_W2", [128, FCH, D], BF16)
        sb1 = pp.sbuf(f"{tag}_b1", [128, FCH], F32)
        sb2 = pp.sbuf(f"{tag}_b2", [128, NCH], F32)
        smr = [pp.sbuf(f"{tag}_mr{b}", [1, TT], F32) for b in range(2)]
        smb = pp.sbuf(f"{tag}_mb", [128, TT], F32)
        sx = [pp.sbuf(f"{tag}_x{b}", [128, NCH, TT], BF16) for b in range(2)]
        sh = pp.sbuf(f"{tag}_h", [128, FCH, TT], BF16)
        syo = [pp.sbuf(f"{tag}_y{b}", [128, TT], F32) for b in range(4)]
        ps1 = [pp.psum(f"{tag}_pa{b}", [128, TT], F32) for b in range(4)]
        ps2 = [pp.psum(f"{tag}_pc{b}", [128, TT], F32) for b in range(2)]
        ps_m = pp.psum(f"{tag}_pm", [128, TT], F32)
        cc = pp.ctr(f"{tag}_cc")
        wl = pp.ctr(f"{tag}_wl")
        xl = [pp.ctr(f"{tag}_xl{b}") for b in range(2)]
        m1 = pp.ctr(f"{tag}_m1"); hq = pp.ctr(f"{tag}_hq")
        m2 = pp.ctr(f"{tag}_m2"); ev = pp.ctr(f"{tag}_ev")
        mb = pp.ctr(f"{tag}_mb"); mv = pp.ctr(f"{tag}_mv")
        st = [pp.ctr(f"{tag}_st{b}") for b in range(4)]
        ml = [pp.ctr(f"{tag}_ml{b}") for b in range(2)]

        with nc.Block() as blk:
            @blk.sync
            def _(sy):
                sy.dma_start(out=ones_r[:], in_=IN["ones_row"]).then_inc(cc.sem, 16)
                for e in range(4):
                    if e >= 1:
                        sy.wait_ge(m1.sem, FCH * NT * e)
                        sy.wait_ge(m2.sem, NCH * NT * e)
                        sy.wait_ge(ev.sem, NCH * NT * e)
                        sy.wait_ge(hq.sem, FCH * NT * e)
                    sy.dma_start(out=sW1[:],
                                 in_=W1_bf[e * D:(e + 1) * D, :].rearrange("(c p) m -> p c m", p=128)
                                 ).then_inc(wl.sem, 16)
                    sy.dma_start(out=sW2[:],
                                 in_=W2_bf[e * FF:(e + 1) * FF, :].rearrange("(c p) m -> p c m", p=128)
                                 ).then_inc(wl.sem, 16)
                    sy.dma_start(out=sb1[:],
                                 in_=b1_f[e * FF:(e + 1) * FF].rearrange("(c p) -> p c", p=128),
                                 allow_slow_non_contiguous=True).then_inc(wl.sem, 16)
                    sy.dma_start(out=sb2[:],
                                 in_=b2_f[e * D:(e + 1) * D].rearrange("(c p) -> p c", p=128),
                                 allow_slow_non_contiguous=True).then_inc(wl.sem, 16)
                    for i in range(NT):
                        u = e * NT + i
                        if u >= 2:
                            sy.wait_ge(m1.sem, FCH * (u - 1))
                            sy.wait_ge(mb.sem, u - 1)

                        sy.dma_start(out=smr[u % 2][:],
                                     in_=emask_dram[e:e + 1, i * TT:(i + 1) * TT]
                                     ).then_inc(ml[u % 2].sem, 16)
                        sy.dma_start(out=sx[u % 2][:], in_=fm_view(x_bf, 0, NCH, i * TT, TT)
                                     ).then_inc(xl[u % 2].sem, 16)

            @blk.tensor
            def _(t):
                t.wait_ge(cc.sem, 16)
                for e in range(4):
                    t.wait_ge(wl.sem, 64 * (e + 1))
                    for i in range(NT):
                        u = e * NT + i
                        t.wait_ge(xl[u % 2].sem, 16 * (u // 2 + 1))
                        t.wait_ge(ml[u % 2].sem, 16 * (u // 2 + 1))
                        if u >= 1:
                            t.wait_ge(mv.sem, u)
                        t.matmul(ps_m[:], lhsT=ones_r[:], rhs=smr[u % 2][:],
                                 start=True, stop=True).then_inc(mb.sem, 1)
                        for fc in range(FCH):
                            b = fc % 4
                            tot1 = u * FCH + fc
                            if tot1 >= 4:
                                t.wait_ge(hq.sem, tot1 - 3)
                            for kc in range(NCH):
                                inst = t.matmul(ps1[b][:],
                                                lhsT=sW1[:, kc, fc * 128:(fc + 1) * 128],
                                                rhs=sx[u % 2][:, kc, :],
                                                start=(kc == 0), stop=(kc == NCH - 1))
                            inst.then_inc(m1.sem, 1)
                        t.wait_ge(hq.sem, (u + 1) * FCH)
                        for mc in range(NCH):
                            b2b = mc % 2
                            tot2 = u * NCH + mc
                            if tot2 >= 2:
                                t.wait_ge(ev.sem, tot2 - 1)
                            for kc in range(FCH):
                                inst = t.matmul(ps2[b2b][:],
                                                lhsT=sW2[:, kc, mc * 128:(mc + 1) * 128],
                                                rhs=sh[:, kc, :],
                                                start=(kc == 0), stop=(kc == FCH - 1))
                            inst.then_inc(m2.sem, 1)

            @blk.scalar
            def _(s):
                for e in range(4):
                    for i in range(NT):
                        u = e * NT + i
                        if u >= 1:
                            s.wait_ge(m2.sem, NCH * u)
                        for fc in range(FCH):
                            b = fc % 4
                            s.wait_ge(m1.sem, u * FCH + fc + 1)
                            s.activation(out=sh[:, fc, :], in_=ps1[b][:], func=AF.Relu,
                                         bias=sb1[:, fc:fc + 1], scale=1.0).then_inc(hq.sem, 1)

            @blk.vector
            def _(v):
                for e in range(4):
                    for i in range(NT):
                        u = e * NT + i
                        v.wait_ge(mb.sem, u + 1)
                        if u >= 1:
                            v.wait_ge(ev.sem, NCH * u)
                        v.tensor_copy(out=smb[:], in_=ps_m[:]).then_inc(mv.sem, 1)
                        for mc in range(NCH):
                            b2b = mc % 2
                            tot2 = u * NCH + mc
                            v.wait_ge(m2.sem, tot2 + 1)
                            if tot2 >= 4:
                                v.wait_ge(st[tot2 % 4].sem, 16 * (tot2 // 4))
                            if mc == 0:
                                v.wait_ge(mv.sem, u + 1)
                            v.scalar_tensor_tensor(out=syo[tot2 % 4][:], in0=ps2[b2b][:],
                                                   scalar=sb2[:, mc:mc + 1], in1=smb[:],
                                                   op0=ALU.add, op1=ALU.mult).then_inc(ev.sem, 1)

            @blk.gpsimd
            def _(g):
                for e in range(4):
                    for i in range(NT):
                        u = e * NT + i
                        for mc in range(NCH):
                            tot2 = u * NCH + mc
                            g.wait_ge(ev.sem, tot2 + 1)
                            g.dma_start(
                                out=ffacc_dram[mc * 128:(mc + 1) * 128, i * TT:(i + 1) * TT],
                                in_=syo[tot2 % 4][:], accum_op=ALU.add
                                ).then_inc(st[tot2 % 4].sem, 16)
                for b in range(4):
                    g.wait_ge(st[b].sem, 16 * (4 * NT * NCH // 4))
        phase_end(nc, pp)



def emit_x2tm(nc, IN, tag, src_fm_f32, dst_tm_bf):
    with ExitStack() as ctx:
        pp = P(nc, ctx)
        szr = pp.sbuf(f"{tag}_zr", [128, D], BF16)
        zq = pp.ctr(f"{tag}_zq"); zs = pp.ctr(f"{tag}_zs")
        s_id = pp.sbuf(f"{tag}_id", [128, 128], F32)
        sxi = [pp.sbuf(f"{tag}_x{b}", [128, NCH, 128], F32) for b in range(2)]
        sob = [pp.sbuf(f"{tag}_o{b}", [128, NCH, 128], BF16) for b in range(2)]
        ps_t = [pp.psum(f"{tag}_p{b}", [128, 128], F32) for b in range(2)]
        cc = pp.ctr(f"{tag}_cc")
        ld = [pp.ctr(f"{tag}_ld{b}") for b in range(2)]
        tp = pp.ctr(f"{tag}_tp"); cp = pp.ctr(f"{tag}_cp")
        st = [pp.ctr(f"{tag}_st{b}") for b in range(2)]
        NTI = L // 128

        with nc.Block() as blk:
            @blk.sync
            def _(sy):
                sy.dma_start(out=s_id[:], in_=IN["ident_f32"]).then_inc(cc.sem, 16)
                for ti in range(NTI):
                    if ti >= 2:
                        sy.wait_ge(tp.sem, NCH * (ti - 1))
                    sy.dma_start(out=sxi[ti % 2][:],
                                 in_=fm_view(src_fm_f32, 0, NCH, ti * 128, 128)
                                 ).then_inc(ld[ti % 2].sem, 16)

            @blk.tensor
            def _(t):
                t.wait_ge(cc.sem, 16)
                for ti in range(NTI):
                    t.wait_ge(ld[ti % 2].sem, 16 * (ti // 2 + 1))
                    for c in range(NCH):
                        gc = ti * NCH + c
                        if gc >= 2:
                            t.wait_ge(cp.sem, gc - 1)
                        t.transpose(ps_t[gc % 2][:], in_=sxi[ti % 2][:, c, :],
                                    identity=s_id[:]).then_inc(tp.sem, 1)

            @blk.vector
            def _(v):
                v.memset(szr[:], 0.0).then_inc(zq.sem, 1)
                for ti in range(NTI):
                    if ti >= 2:
                        v.wait_ge(st[ti % 2].sem, 16 * (ti // 2))
                    for c in range(NCH):
                        gc = ti * NCH + c
                        v.wait_ge(tp.sem, gc + 1)
                        v.tensor_copy(out=sob[ti % 2][:, c, :], in_=ps_t[gc % 2][:]
                                      ).then_inc(cp.sem, 1)

            @blk.gpsimd
            def _(g):
                g.wait_ge(zq.sem, 1)
                g.dma_start(out=dst_tm_bf[L:L + 128, :], in_=szr[:]).then_inc(zs.sem, 16)
                for ti in range(NTI):
                    g.wait_ge(cp.sem, NCH * (ti + 1))
                    g.dma_start(out=dst_tm_bf[ti * 128:(ti + 1) * 128, :],
                                in_=sob[ti % 2][:].rearrange("p c q -> p (c q)")
                                ).then_inc(st[ti % 2].sem, 16)
                g.wait_ge(st[0].sem, 16 * (NTI // 2))
                g.wait_ge(st[1].sem, 16 * (NTI // 2))
                g.wait_ge(zs.sem, 16)
        phase_end(nc, pp)


MOE_CAP = 1536


def emit_moe_routed(nc, IN, tag, xtm_bf, idx_dram, W1_bf, b1_f, W2_bf, b2_f, ff_tm, ident_bf):
    FF = 4096
    FCH = FF // 128
    TPE = MOE_CAP // TT
    NGT = 4 * TPE
    with ExitStack() as ctx:
        pp = P(nc, ctx)
        s_id = pp.sbuf(f"{tag}_id", [128, 128], BF16)
        sW1 = pp.sbuf(f"{tag}_W1", [128, NCH, FF], BF16)
        sW2 = pp.sbuf(f"{tag}_W2", [128, FCH, D], BF16)
        sb1 = pp.sbuf(f"{tag}_b1", [128, FCH], F32)
        sb2 = pp.sbuf(f"{tag}_b2", [128, NCH], F32)
        sIg = pp.sbuf(f"{tag}_ig", [128, NGT * 4], I32)
        sGx = [pp.sbuf(f"{tag}_g{b}", [128, 4, D], BF16) for b in range(2)]
        sXc = pp.sbuf(f"{tag}_xc", [128, NCH, TT], BF16)
        sh = pp.sbuf(f"{tag}_h", [128, FCH, TT], BF16)
        syo = [pp.sbuf(f"{tag}_y{b}", [128, TT], BF16) for b in range(2)]
        sYt = [pp.sbuf(f"{tag}_t{b}", [128, 4, D], BF16) for b in range(2)]
        ps1 = [pp.psum(f"{tag}_pa{b}", [128, TT], F32) for b in range(2)]
        ps2 = [pp.psum(f"{tag}_pc{b}", [128, TT], F32) for b in range(2)]
        ps_ti = [pp.psum(f"{tag}_pi{b}", [128, 128], BF16) for b in range(2)]
        ps_to = [pp.psum(f"{tag}_po{b}", [128, 128], BF16) for b in range(2)]
        cc = pp.ctr(f"{tag}_cc"); wl = pp.ctr(f"{tag}_wl"); zz = pp.ctr(f"{tag}_zz")
        ga = [pp.ctr(f"{tag}_ga{b}") for b in range(2)]
        tit = pp.ctr(f"{tag}_tt"); tic = pp.ctr(f"{tag}_tc")
        m1 = pp.ctr(f"{tag}_m1"); hq = pp.ctr(f"{tag}_hq"); m2 = pp.ctr(f"{tag}_m2")
        ev2 = pp.ctr(f"{tag}_ev"); tot = pp.ctr(f"{tag}_to"); toc = pp.ctr(f"{tag}_oc")
        sc = [pp.ctr(f"{tag}_sc{b}") for b in range(2)]

        with nc.Block() as blk:
            @blk.sync
            def _(sy):
                sy.dma_start(out=s_id[:], in_=ident_bf).then_inc(cc.sem, 16)
                sy.dma_start(out=sIg[:],
                             in_=idx_dram.rearrange("(g p) b -> p (g b)", p=128),
                             allow_slow_non_contiguous=True).then_inc(cc.sem, 16)
                for e in range(4):
                    if e >= 1:
                        sy.wait_ge(m1.sem, FCH * TPE * e)
                        sy.wait_ge(hq.sem, FCH * TPE * e)
                        sy.wait_ge(m2.sem, NCH * TPE * e)
                        sy.wait_ge(ev2.sem, NCH * TPE * e)
                    sy.dma_start(out=sW1[:],
                                 in_=W1_bf[e * D:(e + 1) * D, :].rearrange(
                                     "(c p) m -> p c m", p=128)).then_inc(wl.sem, 16)
                    sy.dma_start(out=sW2[:],
                                 in_=W2_bf[e * FF:(e + 1) * FF, :].rearrange(
                                     "(c p) m -> p c m", p=128)).then_inc(wl.sem, 16)
                    sy.dma_start(out=sb1[:],
                                 in_=b1_f[e * FF:(e + 1) * FF].rearrange("(c p) -> p c", p=128),
                                 allow_slow_non_contiguous=True).then_inc(wl.sem, 16)
                    sy.dma_start(out=sb2[:],
                                 in_=b2_f[e * D:(e + 1) * D].rearrange("(c p) -> p c", p=128),
                                 allow_slow_non_contiguous=True).then_inc(wl.sem, 16)

            @blk.gpsimd
            def _(g):
                def scat(k):
                    g.wait_ge(toc.sem, 32 * (k + 1))
                    for q in range(4):
                        g.wait_ge(sc[0].sem, scn[0] * 16)
                        g.wait_ge(sc[1].sem, scn[1] * 16)
                        g.indirect_dma_start(
                            out=ff_tm,
                            out_offset=IndirectOffsetOnAxis(sIg[:, k * 4 + q:k * 4 + q + 1], 0),
                            in_=sYt[k % 2][:, q, :], in_offset=None,
                        ).then_inc(sc[k % 2].sem, 16)
                        scn[k % 2] += 1
                g.wait_ge(cc.sem, 32)
                g.wait_ge(zz.sem, 2)
                scn = [0, 0]
                for gt in range(NGT):
                    if gt >= 2:
                        g.wait_ge(tit.sem, 32 * (gt - 1))
                    for q in range(4):
                        g.indirect_dma_start(
                            out=sGx[gt % 2][:, q, :], out_offset=None,
                            in_=xtm_bf,
                            in_offset=IndirectOffsetOnAxis(sIg[:, gt * 4 + q:gt * 4 + q + 1], 0),
                        ).then_inc(ga[gt % 2].sem, 16)
                    if gt >= 1:
                        scat(gt - 1)
                scat(NGT - 1)
                g.wait_ge(sc[0].sem, 16 * 4 * (NGT // 2))
                g.wait_ge(sc[1].sem, 16 * 4 * (NGT // 2))

            @blk.tensor
            def _(t):
                t.wait_ge(cc.sem, 32)

                def tin(gt):
                    t.wait_ge(ga[gt % 2].sem, 64 * (gt // 2 + 1))
                    for b32 in range(32):
                        q, kc = b32 // 8, b32 % 8
                        tb = gt * 32 + b32
                        if tb >= 2:
                            t.wait_ge(tic.sem, tb - 1)
                        t.transpose(ps_ti[tb % 2][:],
                                    in_=sGx[gt % 2][:, q, kc * 128:(kc + 1) * 128],
                                    identity=s_id[:]).then_inc(tit.sem, 1)
                tin(0)
                for gt in range(NGT):
                    e = gt // TPE
                    t.wait_ge(wl.sem, 64 * (e + 1))
                    t.wait_ge(tic.sem, 32 * (gt + 1))
                    for fc in range(FCH):
                        tot1 = gt * FCH + fc
                        if tot1 >= 2:
                            t.wait_ge(hq.sem, tot1 - 1)
                        for kc in range(NCH):
                            inst = t.matmul(ps1[fc % 2][:],
                                            lhsT=sW1[:, kc, fc * 128:(fc + 1) * 128],
                                            rhs=sXc[:, kc, :],
                                            start=(kc == 0), stop=(kc == NCH - 1))
                        inst.then_inc(m1.sem, 1)
                    if gt + 1 < NGT:
                        tin(gt + 1)
                    t.wait_ge(hq.sem, FCH * (gt + 1))

                    def tout(mc):
                        t.wait_ge(ev2.sem, gt * NCH + mc + 1)
                        for q in range(4):
                            tb2 = gt * 32 + mc * 4 + q
                            if tb2 >= 2:
                                t.wait_ge(toc.sem, tb2 - 1)
                            t.transpose(ps_to[tb2 % 2][:],
                                        in_=syo[mc % 2][:, q * 128:(q + 1) * 128],
                                        identity=s_id[:]).then_inc(tot.sem, 1)
                    for mc in range(NCH):
                        tot2 = gt * NCH + mc
                        if tot2 >= 2:
                            t.wait_ge(ev2.sem, tot2 - 1)
                        for kc in range(FCH):
                            inst = t.matmul(ps2[mc % 2][:],
                                            lhsT=sW2[:, kc, mc * 128:(mc + 1) * 128],
                                            rhs=sh[:, kc, :],
                                            start=(kc == 0), stop=(kc == FCH - 1))
                        inst.then_inc(m2.sem, 1)
                        if mc >= 1:
                            tout(mc - 1)
                    tout(NCH - 1)

            @blk.scalar
            def _(s):
                for gt in range(NGT):
                    e = gt // TPE
                    for fc in range(FCH):
                        tot1 = gt * FCH + fc
                        if fc == 0:
                            s.wait_ge(wl.sem, 64 * (e + 1))
                            if gt >= 1:
                                s.wait_ge(m2.sem, NCH * gt)
                        s.wait_ge(m1.sem, tot1 + 1)
                        s.activation(out=sh[:, fc, :], in_=ps1[fc % 2][:], func=AF.Relu,
                                     bias=sb1[:, fc:fc + 1], scale=1.0).then_inc(hq.sem, 1)

            @blk.vector
            def _(v):
                v.memset(sGx[0][:], 0.0).then_inc(zz.sem, 1)
                v.memset(sGx[1][:], 0.0).then_inc(zz.sem, 1)

                def tinc(gt):
                    for b32 in range(32):
                        q, kc = b32 // 8, b32 % 8
                        tb = gt * 32 + b32
                        v.wait_ge(tit.sem, tb + 1)
                        v.tensor_copy(out=sXc[:, kc, q * 128:(q + 1) * 128],
                                      in_=ps_ti[tb % 2][:]).then_inc(tic.sem, 1)
                tinc(0)
                for gt in range(NGT):
                    if gt + 1 < NGT:
                        tinc(gt + 1)

                    def toutc(mc):
                        for q in range(4):
                            tb2 = gt * 32 + mc * 4 + q
                            v.wait_ge(tot.sem, tb2 + 1)
                            if mc == 0 and q == 0 and gt >= 2:
                                v.wait_ge(sc[gt % 2].sem, 64 * (gt // 2))
                            v.tensor_copy(out=sYt[gt % 2][:, q, mc * 128:(mc + 1) * 128],
                                          in_=ps_to[tb2 % 2][:]).then_inc(toc.sem, 1)
                    for mc in range(NCH):
                        tot2 = gt * NCH + mc
                        v.wait_ge(m2.sem, tot2 + 1)
                        if tot2 >= 2:
                            v.wait_ge(tot.sem, 4 * (tot2 - 1))
                        v.tensor_scalar(out=syo[mc % 2][:], in0=ps2[mc % 2][:],
                                        scalar1=sb2[:, mc:mc + 1], scalar2=None,
                                        op0=ALU.add).then_inc(ev2.sem, 1)
                        if mc >= 1:
                            toutc(mc - 1)
                    toutc(NCH - 1)
        phase_end(nc, pp)


def emit_l1ffn_epi(nc, IN, tag, xtm_bf, ff_tm, gamma, beta, xres_src, xres_dst, ident_bf):
    NTI = L // 128
    with ExitStack() as ctx:
        pp = P(nc, ctx)
        s_id = pp.sbuf(f"{tag}_id", [128, 128], F32)
        sgr = pp.sbuf(f"{tag}_gr", [1, D], F32)
        sbr = pp.sbuf(f"{tag}_br", [1, D], F32)
        ones_c = pp.sbuf(f"{tag}_on", [1, 128], F32)
        sgb = pp.sbuf(f"{tag}_gb", [128, D], F32)
        sbb = pp.sbuf(f"{tag}_bb", [128, D], F32)
        sff = [pp.sbuf(f"{tag}_f{b}", [128, D], BF16) for b in range(2)]
        sxm = [pp.sbuf(f"{tag}_m{b}", [128, D], BF16) for b in range(2)]
        sxr = [pp.sbuf(f"{tag}_r{b}", [128, NCH, 128], F32) for b in range(2)]
        sz = pp.sbuf(f"{tag}_z", [128, D], F32)
        ssq = pp.sbuf(f"{tag}_sq", [128, D], F32)
        sst = pp.sbuf(f"{tag}_st", [128, 8], F32)
        szb = pp.sbuf(f"{tag}_zb", [128, D], F32)
        sy_ = [pp.sbuf(f"{tag}_s{b}", [128, NCH, 128], F32) for b in range(2)]
        ps_g = pp.psum(f"{tag}_pg", [128, TT], F32)
        ps_t = [pp.psum(f"{tag}_pt{b}", [128, 128], F32) for b in range(2)]
        cc = pp.ctr(f"{tag}_cc"); gb = pp.ctr(f"{tag}_gb")
        ld = [pp.ctr(f"{tag}_ld{b}") for b in range(2)]
        vq = pp.ctr(f"{tag}_vq"); aq = pp.ctr(f"{tag}_aq"); zq = pp.ctr(f"{tag}_zq")
        tp = pp.ctr(f"{tag}_tp"); ev = pp.ctr(f"{tag}_ev")
        st = [pp.ctr(f"{tag}_st{b}") for b in range(2)]

        with nc.Block() as blk:
            @blk.sync
            def _(sy):
                sy.dma_start(out=s_id[:], in_=IN["ident_f32"]).then_inc(cc.sem, 16)
                sy.dma_start(out=ones_c[:], in_=IN["ones_row"]).then_inc(cc.sem, 16)
                sy.dma_start(out=sgr[:], in_=gamma.rearrange("(a d) -> a d", a=1)
                             ).then_inc(cc.sem, 16)
                sy.dma_start(out=sbr[:], in_=beta.rearrange("(a d) -> a d", a=1)
                             ).then_inc(cc.sem, 16)
                for ti in range(NTI):
                    b = ti % 2
                    if ti >= 2:
                        sy.wait_ge(zq.sem, 2 * ti - 3)
                        sy.wait_ge(ev.sem, NCH * (ti - 1))
                    sy.dma_start(out=sff[b][:], in_=ff_tm[ti * 128:(ti + 1) * 128, :]
                                 ).then_inc(ld[b].sem, 16)
                    sy.dma_start(out=sxm[b][:], in_=xtm_bf[ti * 128:(ti + 1) * 128, :]
                                 ).then_inc(ld[b].sem, 16)
                    sy.dma_start(out=sxr[b][:], in_=fm_view(xres_src, 0, NCH, ti * 128, 128)
                                 ).then_inc(ld[b].sem, 16)

            @blk.tensor
            def _(t):
                t.wait_ge(cc.sem, 64)
                for j in range(2):
                    t.matmul(ps_g[:], lhsT=ones_c[:], rhs=sgr[:, j * TT:(j + 1) * TT],
                             start=True, stop=True).then_inc(gb.sem, 1)
                    t.wait_ge(gb.sem, 4 * j + 2)
                    t.matmul(ps_g[:], lhsT=ones_c[:], rhs=sbr[:, j * TT:(j + 1) * TT],
                             start=True, stop=True).then_inc(gb.sem, 1)
                    t.wait_ge(gb.sem, 4 * j + 4)
                for ti in range(NTI):
                    for c in range(NCH):
                        gc = ti * NCH + c
                        t.wait_ge(zq.sem, 2 * ti + 2)
                        if gc >= 2:
                            t.wait_ge(ev.sem, gc - 1)
                        t.transpose(ps_t[gc % 2][:], in_=szb[:, c * 128:(c + 1) * 128],
                                    identity=s_id[:]).then_inc(tp.sem, 1)

            @blk.scalar
            def _(s):
                for ti in range(NTI):
                    s.wait_ge(zq.sem, 2 * ti + 1)
                    s.activation(out=ssq[:], in_=sz[:], func=AF.Square).then_inc(aq.sem, 1)
                    s.wait_ge(vq.sem, 10 * ti + 6)
                    s.activation(out=sst[:, 5:6], in_=sst[:, 4:5], func=AF.Sqrt
                                 ).then_inc(aq.sem, 1)

            @blk.vector
            def _(v):
                def stepv(inst):
                    inst.then_inc(vq.sem, 1)
                    vq.n += 1
                    v.wait_ge(vq.sem, vq.n)
                v.wait_ge(gb.sem, 1)
                v.tensor_scalar(out=sgb[:, 0:TT], in0=ps_g[:], scalar1=0.5, scalar2=None,
                                op0=ALU.mult).then_inc(gb.sem, 1)
                v.wait_ge(gb.sem, 3)
                v.tensor_scalar(out=sbb[:, 0:TT], in0=ps_g[:], scalar1=0.5, scalar2=None,
                                op0=ALU.mult).then_inc(gb.sem, 1)
                v.wait_ge(gb.sem, 5)
                v.tensor_scalar(out=sgb[:, TT:D], in0=ps_g[:], scalar1=0.5, scalar2=None,
                                op0=ALU.mult).then_inc(gb.sem, 1)
                v.wait_ge(gb.sem, 7)
                v.tensor_scalar(out=sbb[:, TT:D], in0=ps_g[:], scalar1=0.5, scalar2=None,
                                op0=ALU.mult).then_inc(gb.sem, 1)
                for ti in range(NTI):
                    b = ti % 2
                    v.wait_ge(ld[b].sem, 48 * (ti // 2 + 1))
                    if ti >= 1:
                        v.wait_ge(aq.sem, 2 * ti)
                    if ti >= 2:
                        v.wait_ge(tp.sem, NCH * (ti - 1))
                    v.tensor_tensor(out=sz[:], in0=sff[b][:], in1=sxm[b][:],
                                    op=ALU.add).then_inc(zq.sem, 1)
                    v.wait_ge(zq.sem, 2 * ti + 1)
                    stepv(v.tensor_reduce(out=sst[:, 0:1], in_=sz[:], axis=AX.X, op=ALU.add))
                    v.wait_ge(aq.sem, 2 * ti + 1)
                    stepv(v.tensor_reduce(out=sst[:, 1:2], in_=ssq[:], axis=AX.X, op=ALU.add))
                    stepv(v.tensor_scalar(out=sst[:, 2:4], in0=sst[:, 0:2], scalar1=1.0 / D,
                                          scalar2=None, op0=ALU.mult))
                    stepv(v.tensor_tensor(out=sst[:, 4:5], in0=sst[:, 2:3], in1=sst[:, 2:3],
                                          op=ALU.mult))
                    stepv(v.tensor_tensor(out=sst[:, 4:5], in0=sst[:, 3:4], in1=sst[:, 4:5],
                                          op=ALU.subtract))
                    v.tensor_scalar(out=sst[:, 4:5], in0=sst[:, 4:5], scalar1=1e-5,
                                    scalar2=None, op0=ALU.add).then_inc(vq.sem, 1)
                    vq.n += 1
                    v.wait_ge(aq.sem, 2 * ti + 2)
                    stepv(v.reciprocal(out=sst[:, 6:7], in_=sst[:, 5:6]))
                    stepv(v.tensor_scalar(out=sst[:, 3:4], in0=sst[:, 2:3], scalar1=-1.0,
                                          scalar2=None, op0=ALU.mult))
                    stepv(v.tensor_scalar(out=sz[:], in0=sz[:], scalar1=sst[:, 3:4],
                                          scalar2=None, op0=ALU.add))
                    stepv(v.scalar_tensor_tensor(out=sz[:], in0=sz[:], scalar=sst[:, 6:7],
                                                 in1=sgb[:], op0=ALU.mult, op1=ALU.mult))
                    v.tensor_tensor(out=szb[:], in0=sz[:], in1=sbb[:],
                                    op=ALU.add).then_inc(zq.sem, 1)
                    for c in range(NCH):
                        gc = ti * NCH + c
                        v.wait_ge(tp.sem, gc + 1)
                        if c == 0 and ti >= 2:
                            v.wait_ge(st[ti % 2].sem, 16 * (ti // 2))
                        v.tensor_tensor(out=sy_[ti % 2][:, c, :], in0=ps_t[gc % 2][:],
                                        in1=sxr[b][:, c, :], op=ALU.add).then_inc(ev.sem, 1)

            @blk.gpsimd
            def _(g):
                for ti in range(NTI):
                    g.wait_ge(ev.sem, NCH * (ti + 1))
                    g.dma_start(out=fm_view(xres_dst, 0, NCH, ti * 128, 128),
                                in_=sy_[ti % 2][:]).then_inc(st[ti % 2].sem, 16)
                g.wait_ge(st[0].sem, 16 * (NTI // 2))
                g.wait_ge(st[1].sem, 16 * (NTI // 2))
        phase_end(nc, pp)


def emit_l1ffn(nc, IN, SC, xres_src, xres_dst, tag):
    q_bf = SC[f"l1_q"].ap(); k_bf = SC[f"l1_k"].ap(); va_bf = SC[f"l1_va"].ap()
    attn_bf = SC[f"l1_attn"].ap(); t1 = SC[f"l1_t1"].ap()
    t1n_bf = SC[f"l1_t1nb"].ap(); t1n_f = SC[f"l1_t1nf"].ap()
    keys = SC[f"l1_keys"].ap(); emask = SC[f"l1_emask"].ap(); ffacc = SC[f"l1_ffacc"].ap()
    xn_bf = SC["xn_bf"].ap(); xn_f = SC["xn_f32"].ap()
    emit_proj(nc, IN, tag + "q", SC["Wq_bf"].ap(), D, D, xn_bf, q_bf, BF16,
              bias=IN["bq"], post_scale=0.125)
    emit_proj(nc, IN, tag + "k", SC["Wk_bf"].ap(), D, D, xn_bf, k_bf, BF16,
              bias=IN["bk"])
    emit_vaug(nc, IN, tag + "v", SC["Wv_bf"].ap(), xn_bf, IN["bv"], va_bf)
    den = SC[f"l1_den"].ap()
    emit_l1attn(nc, IN, tag + "a", q_bf, k_bf, va_bf, attn_bf, den)
    emit_proj(nc, IN, tag + "o", SC["Wo_bf"].ap(), D, D, attn_bf, t1, F32,
              bias=IN["bo"], res_f32=xn_f, div_den=den)
    emit_ln(nc, IN, SC, t1, want_f32=True, tag=tag + "l1",
            gamma=IN["ln1_g"], beta=IN["ln1_b"], out_bf=t1n_bf, out_f32=t1n_f)
    emit_proj(nc, IN, tag + "r", IN["router_w"], D, 1, t1n_f, keys, F32,
              bias=IN["router_b"], f32_in=True)
    xtm = SC["l1_xtm"].ap(); fftm = SC["l1_fftm"].ap(); idxl = SC["l1_idx"].ap()
    emit_x2tm(nc, IN, tag + "x", t1n_f, xtm)
    emit_router_masks(nc, IN, tag + "m", keys, emask, idxl)
    emit_moe_routed(nc, IN, tag + "e", xtm, idxl, SC["W1_bf"].ap(), IN["b1"],
                    SC["W2_bf"].ap(), IN["b2"], fftm, SC["ident"].ap())
    emit_l1ffn_epi(nc, IN, tag + "p", xtm, fftm, IN["ln2_g"], IN["ln2_b"],
                   xres_src, xres_dst, SC["ident"].ap())





def emit_fm2tm(nc, IN, tag, src_fm, R, dst_tm, ident_bf, col0=0):
    RCH = R // 128 if R >= 128 else 1
    rr = min(R, 128)
    NTOK = L // 128
    with ExitStack() as ctx:
        pp = P(nc, ctx)
        s_id = pp.sbuf(f"{tag}_id", [128, 128], BF16)
        s_in = [pp.sbuf(f"{tag}_i{b}", [rr, 128], BF16) for b in range(2)]
        s_out = [pp.sbuf(f"{tag}_o{b}", [128, R], BF16) for b in range(2)]
        ps_t = pp.psum(f"{tag}_ps", [128, 128], BF16)
        cc = pp.ctr(f"{tag}_cc")
        ld = [pp.ctr(f"{tag}_ld{b}") for b in range(2)]
        e2 = pp.ctr(f"{tag}_e2"); v2 = pp.ctr(f"{tag}_v2")
        st = [pp.ctr(f"{tag}_st{b}") for b in range(2)]
        steps = [(i, rc) for i in range(NTOK) for rc in range(RCH)]

        with nc.Block() as blk:
            @blk.sync
            def _(sy):
                sy.dma_start(out=s_id[:], in_=ident_bf).then_inc(cc.sem, 16)
                for n, (i, rc) in enumerate(steps):
                    if n >= 2:
                        sy.wait_ge(e2.sem, n - 1)
                    sy.dma_start(out=s_in[n % 2][:],
                                 in_=src_fm[rc * 128:rc * 128 + rr, i * 128:(i + 1) * 128]
                                 ).then_inc(ld[n % 2].sem, 16)

            @blk.tensor
            def _(t):
                t.wait_ge(cc.sem, 16)
                for n, (i, rc) in enumerate(steps):
                    t.wait_ge(ld[n % 2].sem, 16 * (n // 2 + 1))
                    if n >= 1:
                        t.wait_ge(v2.sem, n)
                    t.transpose(ps_t[:, :rr], in_=s_in[n % 2][:], identity=s_id[0:rr, 0:rr]
                                ).then_inc(e2.sem, 1)

            @blk.vector
            def _(v):
                for n, (i, rc) in enumerate(steps):
                    v.wait_ge(e2.sem, n + 1)
                    if n >= 2 * RCH:
                        v.wait_ge(st[(n // RCH) % 2].sem, 16 * (n // (2 * RCH)))
                    v.tensor_copy(out=s_out[(n // RCH) % 2][:, rc * 128:rc * 128 + rr],
                                  in_=ps_t[:, :rr]).then_inc(v2.sem, 1)

            @blk.gpsimd
            def _(g):
                for i in range(NTOK):
                    g.wait_ge(v2.sem, RCH * (i + 1))
                    g.dma_start(out=dst_tm[i * 128:(i + 1) * 128, col0:col0 + R],
                                in_=s_out[i % 2][:]).then_inc(st[i % 2].sem, 16)
                g.wait_ge(st[0].sem, 16 * ((NTOK + 1) // 2))
                g.wait_ge(st[1].sem, 16 * (NTOK // 2))
        phase_end(nc, pp)


def emit_tm_proj(nc, IN, tag, W_bf, K, M, x_fm, dst_tm, col0):
    KCH = K // 128
    NM = (M + 511) // 512
    NTOK = L // 128
    with ExitStack() as ctx:
        pp = P(nc, ctx)
        sW = pp.sbuf(f"{tag}_W", [128, KCH, M], BF16)
        sx = [pp.sbuf(f"{tag}_x{b}", [128, KCH, 128], BF16) for b in range(2)]
        so = [pp.sbuf(f"{tag}_o{b}", [128, M], BF16) for b in range(2)]
        ps = [pp.psum(f"{tag}_p{b}", [128, min(M, 512)], F32) for b in range(2)]
        cc = pp.ctr(f"{tag}_cc")
        ld = [pp.ctr(f"{tag}_ld{b}") for b in range(2)]
        mm = pp.ctr(f"{tag}_mm"); ev = pp.ctr(f"{tag}_ev")
        st = [pp.ctr(f"{tag}_st{b}") for b in range(2)]

        with nc.Block() as blk:
            @blk.sync
            def _(sy):
                sy.dma_start(out=sW[:], in_=W_bf.rearrange("(c p) m -> p c m", p=128)
                             ).then_inc(cc.sem, 16)
                for i in range(NTOK):
                    if i >= 2:
                        sy.wait_ge(st[i % 2].sem, 16 * (i // 2))
                    sy.dma_start(out=sx[i % 2][:], in_=fm_view(x_fm, 0, KCH, i * 128, 128)
                                 ).then_inc(ld[i % 2].sem, 16)

            @blk.tensor
            def _(t):
                t.wait_ge(cc.sem, 16)
                for i in range(NTOK):
                    t.wait_ge(ld[i % 2].sem, 16 * (i // 2 + 1))
                    for nm in range(NM):
                        tot = i * NM + nm
                        cw = min(512, M - nm * 512)
                        if tot >= 2:
                            t.wait_ge(ev.sem, tot - 1)
                        for kc in range(KCH):
                            inst = t.matmul(ps[tot % 2][:, :cw],
                                            lhsT=sx[i % 2][:, kc, :],
                                            rhs=sW[:, kc, nm * 512:nm * 512 + cw],
                                            start=(kc == 0), stop=(kc == KCH - 1))
                        inst.then_inc(mm.sem, 1)

            @blk.vector
            def _(v):
                for i in range(NTOK):
                    if i >= 2:
                        v.wait_ge(st[i % 2].sem, 16 * (i // 2))
                    for nm in range(NM):
                        tot = i * NM + nm
                        cw = min(512, M - nm * 512)
                        v.wait_ge(mm.sem, tot + 1)
                        v.tensor_copy(out=so[i % 2][:, nm * 512:nm * 512 + cw],
                                      in_=ps[tot % 2][:, :cw]).then_inc(ev.sem, 1)

            @blk.gpsimd
            def _(g):
                for i in range(NTOK):
                    g.wait_ge(ev.sem, NM * (i + 1))
                    g.dma_start(out=dst_tm[i * 128:(i + 1) * 128, col0:col0 + M],
                                in_=so[i % 2][:]).then_inc(st[i % 2].sem, 16)
                g.wait_ge(st[0].sem, 16 * ((NTOK + 1) // 2))
                g.wait_ge(st[1].sem, 16 * (NTOK // 2))
        phase_end(nc, pp)


def emit_l0_tables(nc, IN, SC, tag):
    kr = SC["l0_krr"].ap().rearrange("(s d) t -> d s t", s=2)

    def half_ap(dram, sh):
        return dram.rearrange("(h x) t -> h x t", x=32)[:, sh * 16:(sh + 1) * 16, :]



    qr = SC["l0_qrr"].ap()
    s96 = float(96.0 ** -0.5)
    with ExitStack() as ctx:
        pp = P(nc, ctx)
        sd = pp.sbuf(f"{tag}_d", [16, 8], F32)
        sbias = pp.sbuf(f"{tag}_bi", [128, 2], F32)
        skr = pp.sbuf(f"{tag}_kr", [16, 2, L], F32)
        stk = pp.sbuf(f"{tag}_tk", [16, 2, L], F32)
        stq = pp.sbuf(f"{tag}_tq", [128, 2, L], F32)
        sqr = pp.sbuf(f"{tag}_qr", [128, 2, L], F32)
        stt_ = pp.sbuf(f"{tag}_t", [128, 2, L], F32)
        sqo = pp.sbuf(f"{tag}_qo", [128, L], F32)
        srep = pp.sbuf(f"{tag}_rp", [16, 128], F32)
        sdq = pp.sbuf(f"{tag}_dq", [128, 2], F32)
        ps_r = pp.psum(f"{tag}_pr", [128, 2], F32)
        rp = pp.ctr(f"{tag}_rpc")
        cc = pp.ctr(f"{tag}_cc"); vq = pp.ctr(f"{tag}_vq"); aq = pp.ctr(f"{tag}_aq")
        dn = pp.ctr(f"{tag}_dn"); so_ = pp.ctr(f"{tag}_so")
        bi = pp.ctr(f"{tag}_bi"); ck2 = pp.ctr(f"{tag}_k2")

        with nc.Block() as blk:
            @blk.sync
            def _(sy):
                sy.dma_start(out=sd[:, 0:1], in_=IN["raw_delta"].rearrange("(p a) -> p a", a=1)
                             ).then_inc(cc.sem, 16)
                sy.dma_start(out=stk[:, 0, :], in_=IN["tab_kc"]).then_inc(cc.sem, 16)
                sy.dma_start(out=stk[:, 1, :], in_=IN["tab_ks"]).then_inc(cc.sem, 16)
                sy.dma_start(out=stq[:, 0, :], in_=IN["tab_qc"]).then_inc(cc.sem, 16)
                sy.dma_start(out=stq[:, 1, :], in_=IN["tab_qs"]).then_inc(cc.sem, 16)
                sy.dma_start(out=srep[:], in_=IN["rep16"]).then_inc(cc.sem, 16)

            @blk.gpsimd
            def _(g):
                g.dma_start(out=skr[:], in_=kr).then_inc(ck2.sem, 16)
                for sh in range(2):
                    hap = half_ap(qr, sh)
                    for h in range(8):
                        g.dma_start(out=sqr[h * 16:(h + 1) * 16, sh, :], in_=hap[h]
                                    ).then_inc(ck2.sem, 16)
                g.wait_ge(dn.sem, 2)
                g.dma_start(out=SC["l0_mfm"].ap().rearrange("(s d) t -> d s t", s=2),
                            in_=skr[:]).then_inc(so_.sem, 16)
                g.wait_ge(dn.sem, 3)
                g.dma_start(out=SC["l0_krope"].ap().rearrange("(s d) t -> d s t", s=2),
                            in_=stt_[0:16, :, :]).then_inc(so_.sem, 16)
                g.wait_ge(dn.sem, 4)
                for sh in range(2):
                    hap = half_ap(SC["l0_qmu"].ap(), sh)
                    for h in range(8):
                        g.dma_start(out=hap[h], in_=sqr[h * 16:(h + 1) * 16, sh, :]
                                    ).then_inc(so_.sem, 16)
                g.wait_ge(dn.sem, 5)
                hap = half_ap(SC["l0_qband"].ap(), 0)
                for h in range(8):
                    g.dma_start(out=hap[h], in_=sqo[h * 16:(h + 1) * 16, :]
                                ).then_inc(so_.sem, 16)
                g.wait_ge(dn.sem, 6)
                hap = half_ap(SC["l0_qband"].ap(), 1)
                for h in range(8):
                    g.dma_start(out=hap[h], in_=sqo[h * 16:(h + 1) * 16, :]
                                ).then_inc(so_.sem, 16)
                g.wait_ge(so_.sem, 16 * (2 + 16 + 8 + 8))

            @blk.tensor
            def _(t):
                t.wait_ge(cc.sem, 96)
                t.wait_ge(rp.sem, 1)
                t.matmul(ps_r[:], lhsT=srep[:], rhs=sd[:, 5:7], start=True, stop=True
                         ).then_inc(rp.sem, 1)

            @blk.scalar
            def _(s):
                def steps(inst):
                    inst.then_inc(aq.sem, 1)
                    aq.n += 1
                    s.wait_ge(aq.sem, aq.n)
                s.wait_ge(cc.sem, 96)
                s.wait_ge(ck2.sem, 16 * 17)
                s.wait_ge(bi.sem, 1)
                steps(s.activation(out=sd[:, 1:2], in_=sd[:, 0:1], func=AF.Sigmoid))
                s.wait_ge(dn.sem, 1)
                steps(s.activation(out=sd[:, 3:4], in_=sd[:, 2:3], func=AF.Sin))
                steps(s.activation(out=sd[:, 4:5], in_=sd[:, 2:3], func=AF.Sin,
                                   bias=sbias[0:16, 1:2], scale=1.0))
                steps(s.activation(out=stt_[0:16, :, :], in_=skr[:], func=AF.Abs))
                steps(s.activation(out=stt_[0:16, :, :], in_=stt_[0:16, :, :],
                                   func=AF.Exp, scale=-1.0))
                steps(s.activation(out=stt_[0:16, :, :], in_=stt_[0:16, :, :],
                                   func=AF.Ln, bias=sbias[0:16, 0:1]))
                steps(s.activation(out=skr[:], in_=skr[:], func=AF.Relu))
                s.wait_ge(dn.sem, 3)
                s.wait_ge(so_.sem, 32)
                steps(s.activation(out=stt_[:], in_=sqr[:], func=AF.Abs))
                steps(s.activation(out=stt_[:], in_=stt_[:], func=AF.Exp, scale=-1.0))
                steps(s.activation(out=stt_[:], in_=stt_[:], func=AF.Ln, bias=sbias[:, 0:1]))
                steps(s.activation(out=sqr[:], in_=sqr[:], func=AF.Relu))

            @blk.vector
            def _(v):
                def stepv(inst):
                    inst.then_inc(vq.sem, 1)
                    vq.n += 1
                    v.wait_ge(vq.sem, vq.n)
                v.memset(sbias[:, 0:1], 1.0)
                v.memset(sbias[:, 1:2], math.pi / 2).then_inc(bi.sem, 1)
                v.wait_ge(aq.sem, 1)
                v.tensor_scalar(out=sd[:, 2:3], in0=sd[:, 1:2], scalar1=-math.pi,
                                scalar2=None, op0=ALU.mult).then_inc(dn.sem, 1)
                v.wait_ge(aq.sem, 3)
                stepv(v.tensor_tensor(out=sd[:, 5:6], in0=sd[:, 3:4], in1=sd[:, 3:4],
                                      op=ALU.mult))
                stepv(v.tensor_scalar(out=sd[:, 5:6], in0=sd[:, 5:6], scalar1=-2.0,
                                      scalar2=1.0, op0=ALU.mult, op1=ALU.add))
                stepv(v.tensor_tensor(out=sd[:, 6:7], in0=sd[:, 3:4], in1=sd[:, 4:5],
                                      op=ALU.mult))
                stepv(v.tensor_scalar(out=sd[:, 6:7], in0=sd[:, 6:7], scalar1=2.0,
                                      scalar2=None, op0=ALU.mult))
                v.tensor_copy(out=sdq[0:16, 0:2], in_=sd[:, 5:7]).then_inc(rp.sem, 1)
                v.wait_ge(rp.sem, 2)
                stepv(v.tensor_copy(out=sdq[:], in_=ps_r[:]))

                def rotate(tab, rows, scr, dsc):
                    tc = tab[0:rows, 0, :]; tsn = tab[0:rows, 1, :]
                    s1 = scr[0:rows, 0, :]; s2 = scr[0:rows, 1, :]
                    cosd = dsc[0:rows, 0:1]; sind = dsc[0:rows, 1:2]
                    stepv(v.tensor_scalar(out=s1, in0=tc, scalar1=sind,
                                          scalar2=None, op0=ALU.mult))
                    stepv(v.tensor_scalar(out=tc, in0=tc, scalar1=cosd,
                                          scalar2=None, op0=ALU.mult))
                    stepv(v.tensor_scalar(out=s2, in0=tsn, scalar1=sind,
                                          scalar2=None, op0=ALU.mult))
                    stepv(v.tensor_scalar(out=tsn, in0=tsn, scalar1=cosd,
                                          scalar2=None, op0=ALU.mult))
                    stepv(v.tensor_tensor(out=tc, in0=tc, in1=s2, op=ALU.subtract))
                    stepv(v.tensor_tensor(out=tsn, in0=tsn, in1=s1, op=ALU.add))

                v.wait_ge(aq.sem, 7)
                stepv(v.tensor_tensor(out=skr[:], in0=skr[:], in1=stt_[0:16, :, :],
                                      op=ALU.add))
                v.tensor_copy(out=sd[:, 7:8], in_=sd[:, 5:6]).then_inc(dn.sem, 1)
                rotate(stk, 16, stt_[0:16, :, :], sdq)
                m1 = skr[:, 0, :]; m2 = skr[:, 1, :]
                crot = stk[:, 0, :]; srt = stk[:, 1, :]
                stepv(v.tensor_tensor(out=stt_[0:16, 0, :], in0=m1, in1=crot, op=ALU.mult))
                stepv(v.tensor_tensor(out=stt_[0:16, 1, :], in0=m2, in1=srt, op=ALU.mult))
                stepv(v.tensor_tensor(out=stt_[0:16, 0, :], in0=stt_[0:16, 0, :],
                                      in1=stt_[0:16, 1, :], op=ALU.subtract))
                stepv(v.tensor_tensor(out=stt_[0:16, 1, :], in0=m2, in1=crot, op=ALU.mult))
                stepv(v.tensor_tensor(out=sqo[0:16, :], in0=m1, in1=srt, op=ALU.mult))
                stepv(v.tensor_tensor(out=stt_[0:16, 1, :], in0=stt_[0:16, 1, :],
                                      in1=sqo[0:16, :], op=ALU.add))
                v.tensor_copy(out=sd[:, 7:8], in_=sd[:, 5:6]).then_inc(dn.sem, 1)
                v.wait_ge(aq.sem, 11)
                stepv(v.tensor_tensor(out=sqr[:], in0=sqr[:], in1=stt_[:], op=ALU.add))
                stepv(v.tensor_scalar(out=sqr[:], in0=sqr[:], scalar1=0.10206207261596575,
                                      scalar2=None, op0=ALU.mult))
                v.tensor_copy(out=sd[:, 7:8], in_=sd[:, 5:6]).then_inc(dn.sem, 1)
                rotate(stq, 128, stt_, sdq)
                cq = stq[:, 0, :]; sq_ = stq[:, 1, :]
                mu1 = sqr[:, 0, :]; mu2 = sqr[:, 1, :]
                stepv(v.tensor_tensor(out=sqo[:], in0=mu1, in1=cq, op=ALU.mult))
                stepv(v.tensor_tensor(out=stt_[:, 0, :], in0=mu2, in1=sq_, op=ALU.mult))
                stepv(v.tensor_tensor(out=sqo[:], in0=sqo[:], in1=stt_[:, 0, :],
                                      op=ALU.subtract))
                v.tensor_copy(out=sd[:, 7:8], in_=sd[:, 5:6]).then_inc(dn.sem, 1)
                v.wait_ge(so_.sem, 16 * (2 + 16 + 8))
                stepv(v.tensor_tensor(out=sqo[:], in0=mu1, in1=sq_, op=ALU.mult))
                stepv(v.tensor_tensor(out=stt_[:, 0, :], in0=mu2, in1=cq, op=ALU.mult))
                stepv(v.tensor_tensor(out=sqo[:], in0=sqo[:], in1=stt_[:, 0, :],
                                      op=ALU.add))
                v.tensor_copy(out=sd[:, 7:8], in_=sd[:, 5:6]).then_inc(dn.sem, 1)
        phase_end(nc, pp)

    with ExitStack() as ctx:
        pp = P(nc, ctx)
        tbuf = pp.sbuf(f"{tag}_cp", [128, L], BF16)
        lc = pp.ctr(f"{tag}_lc"); sc_ = pp.ctr(f"{tag}_sc")
        jobs = []
        for h in range(8):
            jobs.append((SC["l0_k"].ap()[h * 64:(h + 1) * 64, :],
                         SC["l0_Kcat"].ap()[h * 96:h * 96 + 64, :], 64))
            jobs.append((SC["l0_krope"].ap(),
                         SC["l0_Kcat"].ap()[h * 96 + 64:h * 96 + 96, :], 32))
            jobs.append((SC["l0_qc"].ap()[h * 64:(h + 1) * 64, :],
                         SC["l0_Qcat"].ap()[h * 96:h * 96 + 64, :], 64))
            jobs.append((SC["l0_qband"].ap()[h * 32:(h + 1) * 32, :],
                         SC["l0_Qcat"].ap()[h * 96 + 64:h * 96 + 96, :], 32))
        with nc.Block() as blk:
            @blk.gpsimd
            def _(g):
                for n, (src, dst, rows) in enumerate(jobs):
                    if n >= 1:
                        g.wait_ge(sc_.sem, 16 * n)
                    g.dma_start(out=tbuf[0:rows, :], in_=src).then_inc(lc.sem, 16)
                    g.wait_ge(lc.sem, 16 * (n + 1))
                    g.dma_start(out=dst, in_=tbuf[0:rows, :]).then_inc(sc_.sem, 16)
                g.wait_ge(sc_.sem, 16 * len(jobs))
        phase_end(nc, pp)


def emit_l0_core1(nc, IN, SC, tag):
    NTI = L // 128
    with ExitStack() as ctx:
        pp = P(nc, ctx)
        sQi = pp.sbuf(f"{tag}_qi", [64, 4, L], BF16)
        sKi = pp.sbuf(f"{tag}_ki", [64, L], BF16)
        sPr = pp.sbuf(f"{tag}_pr", [128, L], F32)
        sLv = pp.sbuf(f"{tag}_lv", [128, L], F32)
        sNG = pp.sbuf(f"{tag}_ng", [128, TT], F32)
        sW4 = pp.sbuf(f"{tag}_w4", [128, 4], F32)
        sw1 = pp.sbuf(f"{tag}_w1", [1, 4], F32)
        siop = pp.sbuf(f"{tag}_io", [128, 1], F32)
        ones_r = pp.sbuf(f"{tag}_onr", [1, 128], F32)
        srel = [pp.sbuf(f"{tag}_re{b}", [128, TT], F32) for b in range(2)]
        sI = pp.sbuf(f"{tag}_I", [128, L], F32)
        sCa = pp.sbuf(f"{tag}_Ca", [128, L], F32)
        sms = pp.sbuf(f"{tag}_ms", [128, 3 * TT + 4], F32)
        smki = pp.sbuf(f"{tag}_mi", [128, TT], I32)
        smli = pp.sbuf(f"{tag}_li", [128, TT], I32)
        sVal = pp.sbuf(f"{tag}_va", [128, 32], F32)
        sIdxU = pp.sbuf(f"{tag}_iu", [128, 32], U32)
        sIdxI = [pp.sbuf(f"{tag}_ii{b}", [128, 32], I32) for b in range(2)]
        ps_I = [pp.psum(f"{tag}_pI{b}", [128, TT], F32) for b in range(2)]
        ps_w = pp.psum(f"{tag}_pw", [128, 4], F32)
        cc = pp.ctr(f"{tag}_cc"); bw = pp.ctr(f"{tag}_bw")
        mm = pp.ctr(f"{tag}_mm"); rl = pp.ctr(f"{tag}_rl"); vq = pp.ctr(f"{tag}_vq")
        cb = pp.ctr(f"{tag}_cb"); tk = pp.ctr(f"{tag}_tk")
        st = [pp.ctr(f"{tag}_st{b}") for b in range(2)]

        with nc.Block() as blk:
            @blk.sync
            def _(sy):
                sy.dma_start(out=sQi[:], in_=SC["l0_qi"].ap().rearrange("(h p) t -> p h t", p=64)
                             ).then_inc(cc.sem, 16)
                sy.dma_start(out=sKi[:], in_=SC["l0_ki"].ap()).then_inc(cc.sem, 16)
                sy.dma_start(out=sPr[:], in_=IN["iota_row"]).then_inc(cc.sem, 16)
                sy.dma_start(out=sLv[:], in_=IN["ladv"]).then_inc(cc.sem, 16)
                sy.dma_start(out=sw1[:], in_=IN["idx_w"].rearrange("(a w) -> a w", a=1)
                             ).then_inc(cc.sem, 16)
                sy.dma_start(out=ones_r[:], in_=IN["ones_row"]).then_inc(cc.sem, 16)
                sy.dma_start(out=siop[:], in_=IN["iota_p"]).then_inc(cc.sem, 16)

            def ncs(ti):
                return min(8, ti // 4 + 1)
            triples = [(ti, sc, h) for ti in range(NTI)
                       for sc in range(ncs(ti)) for h in range(4)]

            @blk.tensor
            def _(t):
                t.wait_ge(cc.sem, 112)
                t.matmul(ps_w[:], lhsT=ones_r[:], rhs=sw1[:], start=True, stop=True
                         ).then_inc(bw.sem, 1)
                for g, (ti, sc, h) in enumerate(triples):
                    if g >= 2:
                        t.wait_ge(rl.sem, g - 1)
                    t.matmul(ps_I[g % 2][:],
                             lhsT=sQi[:, h, ti * 128:(ti + 1) * 128],
                             rhs=sKi[:, sc * TT:(sc + 1) * TT],
                             start=True, stop=True).then_inc(mm.sem, 1)

            REL_MARK = {}

            @blk.vector
            def _(v):
                def stepv(inst):
                    inst.then_inc(vq.sem, 1)
                    vq.n += 1
                    v.wait_ge(vq.sem, vq.n)
                v.memset(sNG[:], NEG)
                v.wait_ge(bw.sem, 1)
                v.tensor_copy(out=sW4[:], in_=ps_w[:]).then_inc(bw.sem, 1)
                v.wait_ge(bw.sem, 2)
                gg = 0
                for ti in range(NTI):
                    t0 = ti * 128
                    stepv(v.tensor_scalar(out=sms[:, 3 * TT:3 * TT + 1], in0=siop[:],
                                          scalar1=float(t0 - 64), scalar2=None, op0=ALU.add))
                    stepv(v.tensor_scalar(out=sms[:, 3 * TT + 1:3 * TT + 2], in0=siop[:],
                                          scalar1=float(t0), scalar2=63.0,
                                          op0=ALU.add, op1=ALU.max))
                    for sc in range(ncs(ti)):
                        for h in range(4):
                            g = gg
                            gg += 1
                            v.wait_ge(rl.sem, g + 1)
                            dst = sI[:, sc * TT:(sc + 1) * TT]
                            if h == 0:
                                stepv(v.tensor_scalar(out=dst, in0=srel[g % 2][:],
                                                      scalar1=sW4[:, 0:1], scalar2=None,
                                                      op0=ALU.mult))
                            else:
                                stepv(v.scalar_tensor_tensor(out=dst, in0=srel[g % 2][:],
                                                             scalar=sW4[:, h:h + 1], in1=dst,
                                                             op0=ALU.mult, op1=ALU.add))
                            REL_MARK[g] = vq.n
                        stepv(v.tensor_scalar(out=smki[:], in0=sPr[:, sc * TT:(sc + 1) * TT],
                                              scalar1=sms[:, 3 * TT:3 * TT + 1], scalar2=None,
                                              op0=ALU.is_le))
                        stepv(v.tensor_scalar(out=smli[:], in0=sPr[:, sc * TT:(sc + 1) * TT],
                                              scalar1=sms[:, 3 * TT + 1:3 * TT + 2], scalar2=None,
                                              op0=ALU.is_gt))
                        stepv(v.tensor_copy(out=sms[:, 2 * TT:3 * TT], in_=sNG[:]))
                        stepv(v.copy_predicated(out=sms[:, 2 * TT:3 * TT],
                                                mask=smli[:],
                                                data=sLv[:, sc * TT:(sc + 1) * TT]))
                        stepv(v.tensor_copy(out=sCa[:, sc * TT:(sc + 1) * TT],
                                            in_=sms[:, 2 * TT:3 * TT]))
                        stepv(v.copy_predicated(out=sCa[:, sc * TT:(sc + 1) * TT],
                                                mask=smki[:],
                                                data=sI[:, sc * TT:(sc + 1) * TT]))
                    W = ncs(ti) * TT
                    if ti >= 2:
                        v.wait_ge(st[ti % 2].sem, 16 * (ti // 2))
                    for r in range(4):
                        stepv(v.max(out=sVal[:, r * 8:(r + 1) * 8], in_=sCa[:, :W]))
                        stepv(v.max_index(out=sIdxU[:, r * 8:(r + 1) * 8],
                                          in_max=sVal[:, r * 8:(r + 1) * 8],
                                          in_values=sCa[:, :W]))
                        stepv(v.match_replace(out=sCa[:, :W],
                                              in_to_replace=sVal[:, r * 8:(r + 1) * 8],
                                              in_values=sCa[:, :W], imm_value=NEG))
                    stepv(v.tensor_copy(out=sIdxI[ti % 2][:], in_=sIdxU[:]))
                    v.tensor_copy(out=sms[:, 3 * TT + 3:3 * TT + 4],
                                  in_=sVal[:, 0:1]).then_inc(tk.sem, 1)

            @blk.scalar
            def _(s):
                for g, (ti, sc, h) in enumerate(triples):
                    s.wait_ge(mm.sem, g + 1)
                    if g >= 2:
                        s.wait_ge(vq.sem, REL_MARK[g - 2])
                    s.activation(out=srel[g % 2][:], in_=ps_I[g % 2][:],
                                 func=AF.Relu).then_inc(rl.sem, 1)

            @blk.gpsimd
            def _(g):
                for ti in range(NTI):
                    g.wait_ge(tk.sem, ti + 1)
                    g.dma_start(out=SC["l0_idx"].ap()[ti * 128:(ti + 1) * 128, :],
                                in_=sIdxI[ti % 2][:]).then_inc(st[ti % 2].sem, 16)
                g.wait_ge(st[0].sem, 16 * (NTI // 2))
                g.wait_ge(st[1].sem, 16 * (NTI // 2))
        phase_end(nc, pp)


def emit_l0_core2(nc, IN, SC, tag):
    NTI = L // 128
    kmv = SC["l0_kmv"].ap()
    with ExitStack() as ctx:
        pp = P(nc, ctx)
        s_id = pp.sbuf(f"{tag}_id", [128, 128], BF16)
        smA = pp.sbuf(f"{tag}_mA", [128, 192], F32)
        smB = pp.sbuf(f"{tag}_mB", [128, 192], F32)
        sIdx = [pp.sbuf(f"{tag}_ix{b}", [128, 32], I32) for b in range(2)]
        sQct = [pp.sbuf(f"{tag}_qc{b}", [128, TT], BF16) for b in range(2)]
        sQmt = [pp.sbuf(f"{tag}_qm{b}", [128, 256], BF16) for b in range(2)]
        sTc = [pp.sbuf(f"{tag}_tc{b}", [128, TT], BF16) for b in range(2)]
        sTs = [pp.sbuf(f"{tag}_ts{b}", [128, TT], BF16) for b in range(2)]
        sQc8 = [pp.sbuf(f"{tag}_q8{b}", [96, 8, 128], BF16) for b in range(2)]
        sKc8 = [pp.sbuf(f"{tag}_k8{b}", [96, 8, 192], BF16) for b in range(2)]
        sVp = [pp.sbuf(f"{tag}_vp{b}", [128, 2, TT], BF16) for b in range(2)]
        sG = [pp.sbuf(f"{tag}_G{b}", [128, 32, 1056], BF16) for b in range(2)]
        sMg = pp.sbuf(f"{tag}_Mg", [128, 32, 32], BF16)
        sAB = pp.sbuf(f"{tag}_AB", [128, 32, 32], BF16)
        sABf = pp.sbuf(f"{tag}_ABf", [128, 32, 32], BF16)
        sKp = pp.sbuf(f"{tag}_Kp", [128, 32, 64], BF16)
        sT32 = pp.sbuf(f"{tag}_T32", [128, 32], F32)
        sSel = pp.sbuf(f"{tag}_S", [128, 8, 32], F32)
        sPs = pp.sbuf(f"{tag}_Ps", [128, 32], BF16)
        sPb = [pp.sbuf(f"{tag}_Pb{b}", [128, 192], BF16) for b in range(2)]
        sPbT = pp.sbuf(f"{tag}_PbT", [128, 2, 128], BF16)
        sSv = pp.sbuf(f"{tag}_Sv", [128, 64], F32)
        sden = pp.sbuf(f"{tag}_dn", [128, 4], F32)
        sAt = pp.sbuf(f"{tag}_At", [128, TT], BF16)
        sAf = [pp.sbuf(f"{tag}_Af{b}", [128, 4, 128], BF16) for b in range(2)]
        ps_b = [pp.psum(f"{tag}_pb{b}", [128, 192], F32) for b in range(2)]
        ps_t = [pp.psum(f"{tag}_pt{b}", [128, 128], BF16) for b in range(2)]
        ps_o = [pp.psum(f"{tag}_po{b}", [128, 64], F32) for b in range(2)]
        cc = pp.ctr(f"{tag}_cc")
        ld = [pp.ctr(f"{tag}_ld{b}") for b in range(2)]
        ldg = [pp.ctr(f"{tag}_lg{b}") for b in range(2)]
        ga = [pp.ctr(f"{tag}_ga{b}") for b in range(2)]
        vq = pp.ctr(f"{tag}_vq")
        pb = pp.ctr(f"{tag}_pbc"); nm = pp.ctr(f"{tag}_nm"); ve = pp.ctr(f"{tag}_ve")
        tc = pp.ctr(f"{tag}_tcc"); po = pp.ctr(f"{tag}_poc"); ev = pp.ctr(f"{tag}_ev")
        ft = pp.ctr(f"{tag}_ft"); fc = pp.ctr(f"{tag}_fc")
        st = [pp.ctr(f"{tag}_st{b}") for b in range(2)]

        with nc.Block() as blk:
            @blk.sync
            def _(sy):
                sy.dma_start(out=s_id[:], in_=SC["ident"].ap()).then_inc(cc.sem, 16)
                sy.dma_start(out=smA[:], in_=IN["bmaskA"]).then_inc(cc.sem, 16)
                sy.dma_start(out=smB[:], in_=IN["bmaskB"]).then_inc(cc.sem, 16)
                for ti in range(NTI):
                    t0 = ti * 128
                    P0 = max(0, t0 - 64)
                    b = ti % 2
                    if ti >= 2:
                        sy.wait_ge(ev.sem, 8 * (ti - 1))
                        sy.wait_ge(ga[b].sem, 32 * 16 * (ti // 2))
                    sy.dma_start(out=sIdx[b][:], in_=SC["l0_idx"].ap()[t0:t0 + 128, :]
                                 ).then_inc(ld[b].sem, 16)
                    sy.dma_start(out=sQct[b][:], in_=SC["l0_qct"].ap()[t0:t0 + 128, :]
                                 ).then_inc(ld[b].sem, 16)
                    sy.dma_start(out=sQmt[b][:], in_=SC["l0_qmut"].ap()[t0:t0 + 128, :]
                                 ).then_inc(ld[b].sem, 16)
                    sy.dma_start(out=sQc8[b][:],
                                 in_=SC["l0_Qcat"].ap()[:, t0:t0 + 128].rearrange(
                                     "(h p) t -> p h t", p=96)).then_inc(ld[b].sem, 16)
                    sy.dma_start(out=sKc8[b][:],
                                 in_=SC["l0_Kcat"].ap()[:, P0:P0 + 192].rearrange(
                                     "(h p) t -> p h t", p=96)).then_inc(ld[b].sem, 16)
                    sy.dma_start(out=sVp[b][:, 0, :],
                                 in_=kmv[P0:P0 + 128, 544:1056]).then_inc(ld[b].sem, 16)
                    sy.dma_start(out=sVp[b][0:64, 1, :],
                                 in_=kmv[P0 + 128:P0 + 192, 544:1056]).then_inc(ld[b].sem, 16)

            @blk.gpsimd
            def _(g):
                for ti in range(NTI):
                    b = ti % 2
                    g.wait_ge(ld[b].sem, 112 * (ti // 2 + 1))
                    if ti >= 2:
                        g.wait_ge(ev.sem, 8 * (ti - 1))
                    g.dma_start(out=sTc[b][:], in_=IN["tselc"][ti * 128:ti * 128 + 128, :]
                                ).then_inc(ldg[b].sem, 16)
                    g.dma_start(out=sTs[b][:], in_=IN["tsels"][ti * 128:ti * 128 + 128, :]
                                ).then_inc(ldg[b].sem, 16)
                    for j in range(32):
                        g.indirect_dma_start(
                            out=sG[b][:, j, :], out_offset=None,
                            in_=kmv, in_offset=IndirectOffsetOnAxis(sIdx[b][:, j:j + 1], 0),
                        ).then_inc(ga[b].sem, 16)
                    if ti >= 1:
                        g.wait_ge(fc.sem, 4 * ti)
                        g.dma_start(out=fm_view(SC["l0_attnf"].ap(), 0, 4, (ti - 1) * 128, 128),
                                    in_=sAf[(ti - 1) % 2][:]).then_inc(st[(ti - 1) % 2].sem, 16)
                g.wait_ge(fc.sem, 4 * NTI)
                g.dma_start(out=fm_view(SC["l0_attnf"].ap(), 0, 4, (NTI - 1) * 128, 128),
                            in_=sAf[(NTI - 1) % 2][:]).then_inc(st[(NTI - 1) % 2].sem, 16)
                g.wait_ge(st[0].sem, 16 * (NTI // 2))
                g.wait_ge(st[1].sem, 16 * (NTI // 2))

            @blk.tensor
            def _(t):
                t.wait_ge(cc.sem, 48)
                for ti in range(NTI):
                    b = ti % 2
                    t.wait_ge(ld[b].sem, 112 * (ti // 2 + 1))
                    if ti >= 1:
                        t.wait_ge(fc.sem, 4 * ti)
                    for h in range(8):
                        g8 = ti * 8 + h
                        if g8 >= 2:
                            t.wait_ge(ve.sem, 2 * (g8 - 1))
                        t.matmul(ps_b[h % 2][:], lhsT=sQc8[b][:, h, :], rhs=sKc8[b][:, h, :],
                                 start=True, stop=True).then_inc(pb.sem, 1)
                        t.wait_ge(ve.sem, 2 * g8 + 2)
                        if g8 >= 1:
                            t.wait_ge(tc.sem, 2 * g8)
                        t.transpose(ps_t[0][:], in_=sPb[h % 2][:, 0:128], identity=s_id[:])
                        t.transpose(ps_t[1][0:64, 0:128], in_=sPb[h % 2][:, 128:192],
                                    identity=s_id[:]).then_inc(ft.sem, 1)
                        t.wait_ge(tc.sem, 2 * (g8 + 1))
                        if g8 >= 2:
                            t.wait_ge(ev.sem, g8 - 1)
                        t.matmul(ps_o[h % 2][:], lhsT=sPbT[:, 0, :],
                                 rhs=sVp[b][:, 0, h * 64:(h + 1) * 64], start=True, stop=False)
                        t.matmul(ps_o[h % 2][:], lhsT=sPbT[0:64, 1, :],
                                 rhs=sVp[b][0:64, 1, h * 64:(h + 1) * 64], start=False,
                                 stop=True).then_inc(po.sem, 1)
                    t.wait_ge(ev.sem, 8 * (ti + 1))
                    t.wait_ge(tc.sem, 2 * 8 * (ti + 1))
                    for q4 in range(4):
                        if q4 >= 2:
                            t.wait_ge(fc.sem, 4 * ti + q4 - 1)
                        t.transpose(ps_t[q4 % 2][:], in_=sAt[:, q4 * 128:(q4 + 1) * 128],
                                    identity=s_id[:]).then_inc(ft.sem, 1)

            @blk.scalar
            def _(s):
                for ti in range(NTI):
                    for h in range(8):
                        g8 = ti * 8 + h
                        s.wait_ge(nm.sem, g8 + 1)
                        if g8 >= 2:
                            s.wait_ge(tc.sem, 2 * (g8 - 1))
                        if g8 >= 1:
                            s.wait_ge(ev.sem, g8)
                        s.activation(out=sPb[h % 2][:], in_=ps_b[h % 2][:], func=AF.Exp,
                                     bias=sden[:, 2:3], scale=1.0)
                        s.activation(out=sPs[:], in_=sSel[:, h, :], func=AF.Exp,
                                     bias=sden[:, 2:3], scale=1.0).then_inc(ve.sem, 2)

            @blk.vector
            def _(v):
                def stepv(inst):
                    inst.then_inc(vq.sem, 1)
                    vq.n += 1
                    v.wait_ge(vq.sem, vq.n)
                for ti in range(NTI):
                    b = ti % 2
                    v.wait_ge(ga[b].sem, 32 * 16 * (ti // 2 + 1))
                    v.wait_ge(ld[b].sem, 112 * (ti // 2 + 1))
                    v.wait_ge(ldg[b].sem, 32 * (ti // 2 + 1))
                    stepv(v.tensor_copy(out=sMg[:], in_=sG[b][:, :, 0:32]))
                    tcv = sTc[b][:].rearrange("p (j d) -> p j d", d=16)
                    tsv = sTs[b][:].rearrange("p (j d) -> p j d", d=16)
                    m1 = sMg[:, :, 0:16]; m2 = sMg[:, :, 16:32]
                    stepv(v.tensor_tensor(out=sABf[:, :, 0:16], in0=tcv, in1=m1, op=ALU.mult))
                    stepv(v.tensor_tensor(out=sABf[:, :, 16:32], in0=tsv, in1=m2, op=ALU.mult))
                    stepv(v.tensor_tensor(out=sAB[:, :, 0:16], in0=sABf[:, :, 0:16],
                                          in1=sABf[:, :, 16:32], op=ALU.add))
                    stepv(v.tensor_tensor(out=sABf[:, :, 0:16], in0=tcv, in1=m2, op=ALU.mult))
                    stepv(v.tensor_tensor(out=sABf[:, :, 16:32], in0=tsv, in1=m1, op=ALU.mult))
                    stepv(v.tensor_tensor(out=sABf[:, :, 0:16], in0=sABf[:, :, 0:16],
                                          in1=sABf[:, :, 16:32], op=ALU.subtract))
                    stepv(v.tensor_copy(out=sAB[:, :, 16:32], in_=sABf[:, :, 0:16]))
                    for h in range(8):
                        qm = sQmt[b][:, h * 32:(h + 1) * 32].rearrange(
                            "p (a d) -> p a d", a=1).to_broadcast([128, 32, 32])
                        stepv(v.tensor_tensor(out=sABf[:], in0=sAB[:], in1=qm, op=ALU.mult))
                        stepv(v.tensor_reduce(out=sSel[:, h, :], in_=sABf[:], axis=AX.X,
                                              op=ALU.add))
                        qc = sQct[b][:, h * 64:(h + 1) * 64].rearrange(
                            "p (a d) -> p a d", a=1).to_broadcast([128, 32, 64])
                        stepv(v.tensor_tensor(out=sKp[:],
                                              in0=sG[b][:, :, 32 + h * 64:32 + (h + 1) * 64],
                                              in1=qc, op=ALU.mult))
                        stepv(v.tensor_reduce(out=sT32[:], in_=sKp[:], axis=AX.X, op=ALU.add))
                        stepv(v.tensor_tensor(out=sSel[:, h, :], in0=sSel[:, h, :],
                                              in1=sT32[:], op=ALU.add))
                    for h in range(8):
                        g8 = ti * 8 + h
                        v.wait_ge(pb.sem, g8 + 1)
                        msk = smB if ti == 0 else smA
                        stepv(v.tensor_tensor(out=ps_b[h % 2][:], in0=ps_b[h % 2][:],
                                              in1=msk[:], op=ALU.add))
                        stepv(v.tensor_reduce(out=sden[:, 0:1], in_=ps_b[h % 2][:],
                                              axis=AX.X, op=ALU.max))
                        stepv(v.tensor_reduce(out=sden[:, 1:2], in_=sSel[:, h, :],
                                              axis=AX.X, op=ALU.max))
                        stepv(v.tensor_tensor(out=sden[:, 0:1], in0=sden[:, 0:1],
                                              in1=sden[:, 1:2], op=ALU.max))
                        v.tensor_scalar(out=sden[:, 2:3], in0=sden[:, 0:1], scalar1=-1.0,
                                        scalar2=None, op0=ALU.mult).then_inc(nm.sem, 1)
                        v.wait_ge(ve.sem, 2 * g8 + 2)
                        stepv(v.tensor_reduce(out=sden[:, 0:1], in_=sPb[h % 2][:],
                                              axis=AX.X, op=ALU.add))
                        stepv(v.tensor_reduce(out=sden[:, 1:2], in_=sPs[:],
                                              axis=AX.X, op=ALU.add))
                        stepv(v.tensor_tensor(out=sden[:, 0:1], in0=sden[:, 0:1],
                                              in1=sden[:, 1:2], op=ALU.add))
                        stepv(v.reciprocal(out=sden[:, 3:4], in_=sden[:, 0:1]))
                        v.wait_ge(ft.sem, 12 * ti + h + 1)
                        stepv(v.tensor_copy(out=sPbT[:, 0, :], in_=ps_t[0][:]))
                        v.tensor_copy(out=sPbT[0:64, 1, :], in_=ps_t[1][0:64, :]
                                      ).then_inc(tc.sem, 2)
                        ps_bc = sPs[:].rearrange("p (j a) -> p j a", a=1).to_broadcast(
                            [128, 32, 64])
                        stepv(v.tensor_tensor(out=sKp[:],
                                              in0=sG[b][:, :, 544 + h * 64:544 + (h + 1) * 64],
                                              in1=ps_bc, op=ALU.mult))
                        stepv(v.tensor_reduce(out=sSv[:],
                                              in_=sKp[:].rearrange("p j d -> p d j"),
                                              axis=AX.X, op=ALU.add))
                        v.wait_ge(po.sem, g8 + 1)
                        stepv(v.tensor_tensor(out=sSv[:], in0=sSv[:],
                                              in1=ps_o[h % 2][:], op=ALU.add))
                        v.tensor_scalar(out=sAt[:, h * 64:(h + 1) * 64], in0=sSv[:],
                                        scalar1=sden[:, 3:4], scalar2=None, op0=ALU.mult
                                        ).then_inc(ev.sem, 1)
                    for q4 in range(4):
                        v.wait_ge(ft.sem, 12 * ti + 8 + q4 + 1)
                        if ti >= 2:
                            v.wait_ge(st[ti % 2].sem, 16 * (ti // 2))
                        v.tensor_copy(out=sAf[ti % 2][:, q4, :], in_=ps_t[q4 % 2][:]
                                      ).then_inc(fc.sem, 1)
        phase_end(nc, pp)


def emit_l0attn_block(nc, IN, SC, xres_src, xres_dst, tag):
    xn = SC["xn_bf"].ap()
    emit_proj(nc, IN, tag + "ck", SC["W_dkv_bf"].ap(), D, 256, xn, SC["l0_ckv"].ap(), BF16)
    emit_proj(nc, IN, tag + "kf", SC["W_uk_bf"].ap(), 256, 512, SC["l0_ckv"].ap(), SC["l0_k"].ap(), BF16)
    emit_proj(nc, IN, tag + "cq", SC["W_dq_bf"].ap(), D, 512, xn, SC["l0_cq"].ap(), BF16)
    emit_proj(nc, IN, tag + "qc", SC["W_uq_bf"].ap(), 512, 512, SC["l0_cq"].ap(),
              SC["l0_qc"].ap(), BF16, post_scale=float(96.0 ** -0.5))
    emit_proj(nc, IN, tag + "qr", SC["W_qr_bf"].ap(), 512, 256, SC["l0_cq"].ap(), SC["l0_qrr"].ap(), BF16)
    emit_proj(nc, IN, tag + "kr", SC["W_kr_bf"].ap(), D, 32, xn, SC["l0_krr"].ap(), BF16)
    emit_proj(nc, IN, tag + "qi", SC["idx_wq_bf"].ap(), D, 256, xn, SC["l0_qi"].ap(), BF16)
    emit_proj(nc, IN, tag + "ki", SC["idx_wk_bf"].ap(), D, 64, xn, SC["l0_ki"].ap(), BF16)
    emit_l0_tables(nc, IN, SC, tag + "tb")
    emit_tm_proj(nc, IN, tag + "kt", SC["W_uk_bf"].ap(), 256, 512, SC["l0_ckv"].ap(),
                 SC["l0_kmv"].ap(), 32)
    emit_tm_proj(nc, IN, tag + "vt", SC["W_uv_bf"].ap(), 256, 512, SC["l0_ckv"].ap(),
                 SC["l0_kmv"].ap(), 544)
    emit_fm2tm(nc, IN, tag + "mt", SC["l0_mfm"].ap(), 32, SC["l0_kmv"].ap(),
               SC["ident"].ap(), col0=0)
    emit_fm2tm(nc, IN, tag + "qt", SC["l0_qc"].ap(), 512, SC["l0_qct"].ap(),
               SC["ident"].ap(), col0=0)
    emit_fm2tm(nc, IN, tag + "mu", SC["l0_qmu"].ap(), 256, SC["l0_qmut"].ap(),
               SC["ident"].ap(), col0=0)
    emit_l0_core1(nc, IN, SC, tag + "c1")
    emit_l0_core2(nc, IN, SC, tag + "c2")
    emit_proj(nc, IN, tag + "ou", SC["W_out_bf"].ap(), 512, D, SC["l0_attnf"].ap(),
              xres_dst, F32, res_f32=xres_src)



def emit_l1attn_block(nc, IN, SC, xres_src, xres_dst, tag):
    lat_bf = SC["a_lat"].ap(); va2 = SC["a_va"].ap(); y_bf = SC["a_y"].ap()
    xn_bf = SC["xn_bf"].ap()
    emit_proj(nc, IN, tag + "l", SC["aWbT_bf"].ap(), D, 64, xn_bf, lat_bf, BF16)
    emit_vaug(nc, IN, tag + "v", SC["aWvT_bf"].ap(), xn_bf, IN["abv"], va2,
              nh=1, hd=1024)

    NKT = L // 128
    with ExitStack() as ctx:
        pp = P(nc, ctx)
        ones_r = pp.sbuf(f"{tag}_onr", [1, 128], F32)
        ones_cb = pp.sbuf(f"{tag}_ocb", [128, 1], BF16)
        ones_cf = pp.sbuf(f"{tag}_ocf", [128, 1], F32)
        sLat = pp.sbuf(f"{tag}_lat", [128, L], BF16)
        sVa = pp.sbuf(f"{tag}_va", [128, NKT, 1025], BF16)
        scm = pp.sbuf(f"{tag}_cm", [128, 4, TT], F32)
        ssq = pp.sbuf(f"{tag}_sq", [64, TT], F32)
        snr = pp.sbuf(f"{tag}_nr", [1, 8], F32)
        sBn = pp.sbuf(f"{tag}_Bn", [128, 1], F32)
        sPall = pp.sbuf(f"{tag}_P", [128, NKT, TT], BF16)
        sY = [pp.sbuf(f"{tag}_Y{b}", [128, NCH, TT], BF16) for b in range(2)]
        snb = pp.sbuf(f"{tag}_nb2", [128, TT], F32)
        srd = pp.sbuf(f"{tag}_rd", [1, TT], F32)
        ps_g = [pp.psum(f"{tag}_pg{b}", [128, 2, TT], F32) for b in range(2)]
        ps_d = pp.psum(f"{tag}_pd", [1, TT], F32)
        ps_nb = pp.psum(f"{tag}_nb", [128, TT], F32)
        ps_B = pp.psum(f"{tag}_pB", [128, 1], F32)
        cc = pp.ctr(f"{tag}_cc"); z1 = pp.ctr(f"{tag}_z1")
        nq = pp.ctr(f"{tag}_nq"); np_ = pp.ctr(f"{tag}_np"); vq = pp.ctr(f"{tag}_vq")
        b1 = pp.ctr(f"{tag}_b1")
        ms = pp.ctr(f"{tag}_ms"); mk = pp.ctr(f"{tag}_mk"); ex = pp.ctr(f"{tag}_ex")
        pv = pp.ctr(f"{tag}_pv"); dn = pp.ctr(f"{tag}_dn"); rc = pp.ctr(f"{tag}_rc")
        bc = pp.ctr(f"{tag}_bc"); eo = pp.ctr(f"{tag}_eo")
        nbq = pp.ctr(f"{tag}_nbq")
        st = [pp.ctr(f"{tag}_st{b}") for b in range(2)]
        base_of = lambda qt: sum(2 * q + 2 for q in range(qt))
        NGT = base_of(8)

        with nc.Block() as blk:
            @blk.sync
            def _(sy):
                sy.dma_start(out=ones_r[:], in_=IN["ones_row"]).then_inc(cc.sem, 16)
                sy.dma_start(out=ones_cb[:], in_=IN["ones_col_bf"]).then_inc(cc.sem, 16)
                sy.dma_start(out=ones_cf[:], in_=IN["ones_col"]).then_inc(cc.sem, 16)
                sy.dma_start(out=scm[:], in_=IN["cmask"].rearrange("(o p) t -> p o t", p=128)
                             ).then_inc(cc.sem, 16)
                sy.wait_ge(z1.sem, 1)
                sy.dma_start(out=sLat[0:64, :], in_=lat_bf).then_inc(cc.sem, 16)
                sy.dma_start(out=sVa[:], in_=va2.rearrange("(kt p) c -> p kt c", p=128)
                             ).then_inc(cc.sem, 16)

            @blk.scalar
            def _(s):
                s.wait_ge(cc.sem, 96)
                for i in range(NT):
                    if i >= 1:
                        s.wait_ge(np_.sem, i)
                    s.activation(out=ssq[:], in_=sLat[0:64, i * TT:(i + 1) * TT],
                                 func=AF.Square).then_inc(nq.sem, 1)
                s.wait_ge(b1.sem, 3)
                for qt in range(8):
                    ngr = 2 * qt + 2
                    base_g = base_of(qt)
                    if qt >= 1:
                        s.wait_ge(pv.sem, NCH * qt)
                    for g in range(ngr):
                        if g < ngr - 2:
                            s.wait_ge(ms.sem, base_g + g + 1)
                        else:
                            s.wait_ge(mk.sem, 2 * qt + (g - (ngr - 2)) + 1)
                        s.activation(out=sPall[:, 2 * g:2 * g + 2, :], in_=ps_g[g % 2][:],
                                     func=AF.Exp, bias=sBn[:], scale=0.125
                                     ).then_inc(ex.sem, 1)

            @blk.tensor
            def _(t):
                t.wait_ge(cc.sem, 96)
                for i in range(NT):
                    t.wait_ge(nq.sem, i + 1)
                    if i >= 1:
                        t.wait_ge(vq.sem, 2 * i)
                    t.matmul(ps_d[:], lhsT=ones_cf[0:64, :], rhs=ssq[:],
                             start=True, stop=True).then_inc(np_.sem, 1)
                t.wait_ge(b1.sem, 1)
                t.matmul(ps_B[:], lhsT=ones_r[:], rhs=snr[:, 2:3],
                         start=True, stop=True).then_inc(b1.sem, 1)
                for qt in range(8):
                    nkt = 4 * qt + 4
                    ngr = nkt // 2
                    base_g = base_of(qt)
                    if qt >= 1:
                        t.wait_ge(eo.sem, NCH * qt)
                    for g in range(ngr):
                        G = base_g + g
                        if G >= 2:
                            t.wait_ge(ex.sem, G - 1)
                        for j in range(2):
                            kt = 2 * g + j
                            inst = t.matmul(ps_g[g % 2][:, j, :],
                                            lhsT=sLat[:, kt * 128:(kt + 1) * 128],
                                            rhs=sLat[:, qt * TT:(qt + 1) * TT],
                                            start=True, stop=True)
                        inst.then_inc(ms.sem, 1)
                    t.wait_ge(ex.sem, base_g + ngr)
                    if qt == 0:
                        t.wait_ge(vq.sem, 17)
                    else:
                        t.wait_ge(rc.sem, 2 * qt - 1)
                    for kt in range(nkt):
                        inst = t.matmul(ps_d[:], lhsT=ones_cb[:], rhs=sPall[:, kt, :],
                                        start=(kt == 0), stop=(kt == nkt - 1))
                    inst.then_inc(dn.sem, 1)
                    t.wait_ge(rc.sem, 2 * qt + 2)
                    if qt >= 1:
                        t.wait_ge(nbq.sem, qt)
                    t.matmul(ps_nb[:], lhsT=ones_r[:], rhs=srd[:],
                             start=True, stop=True).then_inc(bc.sem, 1)
                    for mc in range(NCH):
                        b = mc % 4
                        if mc >= 4:
                            t.wait_ge(eo.sem, qt * NCH + (mc - 4) + 1)
                        for kt in range(nkt):
                            inst = t.matmul(ps_g[b // 2][:, b % 2, :],
                                            lhsT=sVa[:, kt, mc * 128:(mc + 1) * 128],
                                            rhs=sPall[:, kt, :],
                                            start=(kt == 0), stop=(kt == nkt - 1))
                        inst.then_inc(pv.sem, 1)

            @blk.vector
            def _(v):
                def stepv(inst):
                    inst.then_inc(vq.sem, 1)
                    vq.n += 1
                    v.wait_ge(vq.sem, vq.n)
                v.memset(sLat[64:128, :], 0.0).then_inc(z1.sem, 1)
                stepv(v.memset(snr[:], 0.0))
                for i in range(NT):
                    v.wait_ge(np_.sem, i + 1)
                    stepv(v.tensor_reduce(out=snr[:, 1:2], in_=ps_d[:], axis=AX.X,
                                          op=ALU.max))
                    stepv(v.tensor_tensor(out=snr[:, 0:1], in0=snr[:, 0:1],
                                          in1=snr[:, 1:2], op=ALU.max))
                v.tensor_scalar(out=snr[:, 2:3], in0=snr[:, 0:1], scalar1=-0.125,
                                scalar2=None, op0=ALU.mult).then_inc(b1.sem, 1)
                v.wait_ge(b1.sem, 2)
                v.tensor_copy(out=sBn[:], in_=ps_B[:]).then_inc(b1.sem, 1)
                for qt in range(8):
                    ngr = 2 * qt + 2
                    base_g = base_of(qt)
                    for gi, g in enumerate([ngr - 2, ngr - 1]):
                        v.wait_ge(ms.sem, base_g + g + 1)
                        v.tensor_tensor(out=ps_g[g % 2][:, 0, :], in0=ps_g[g % 2][:, 0, :],
                                        in1=scm[:, 2 * gi, :], op=ALU.add)
                        v.tensor_tensor(out=ps_g[g % 2][:, 1, :], in0=ps_g[g % 2][:, 1, :],
                                        in1=scm[:, 2 * gi + 1, :], op=ALU.add
                                        ).then_inc(mk.sem, 1)
                    v.wait_ge(dn.sem, qt + 1)
                    v.tensor_copy(out=srd[:], in_=ps_d[:]).then_inc(rc.sem, 1)
                    v.wait_ge(rc.sem, 2 * qt + 1)
                    v.reciprocal(out=srd[:], in_=srd[:]).then_inc(rc.sem, 1)
                    v.wait_ge(bc.sem, qt + 1)
                    v.tensor_copy(out=snb[:], in_=ps_nb[:]).then_inc(nbq.sem, 1)
                    v.wait_ge(nbq.sem, qt + 1)
                    for mc in range(NCH):
                        v.wait_ge(pv.sem, qt * NCH + mc + 1)
                        if qt >= 2:
                            v.wait_ge(st[qt % 2].sem, 16 * (qt // 2))
                        b = mc % 4
                        v.tensor_tensor(out=sY[qt % 2][:, mc, :],
                                        in0=ps_g[b // 2][:, b % 2, :], in1=snb[:],
                                        op=ALU.mult).then_inc(eo.sem, 1)

            @blk.gpsimd
            def _(g):
                for qt in range(8):
                    g.wait_ge(eo.sem, NCH * (qt + 1))
                    g.dma_start(out=fm_view(y_bf, 0, NCH, qt * TT, TT), in_=sY[qt % 2][:]
                                ).then_inc(st[qt % 2].sem, 16)
                g.wait_ge(st[0].sem, 16 * 4)
                g.wait_ge(st[1].sem, 16 * 4)
        phase_end(nc, pp)

    emit_proj(nc, IN, tag + "o", SC["aWoT_bf"].ap(), D, D, y_bf, xres_dst, F32,
              bias=IN["abo"], res_f32=xres_src)



L1W = ["Wq", "Wk", "Wv", "Wo"]
_L0_SHAPES = {"W_dkv": (D, 256), "W_uk": (256, 512), "W_uv": (256, 512),
              "W_dq": (D, 512), "W_uq": (512, 512), "W_qr": (512, 256),
              "W_kr": (D, 32), "W_out": (512, D),
              "idx_wq": (D, 256), "idx_wk": (D, 64)}


def nc_shape_of(w):
    return _L0_SHAPES[w]


def build_program(stages=("ffn0a",)):
    nc = bass.Bass("TRN2", target_bir_lowering=False)
    IN = {}

    def ein(name, shape, dt=F32):
        IN[name] = nc.dram_tensor(name, list(shape), dt, kind="ExternalInput").ap()

    ein("x", (D, L))
    ein("w_freq_real", (D, D)); ein("w_freq_imag", (D, D)); ein("ffn_bias", (D,))
    ein("Fre", (D, D)); ein("Fim_neg", (D, D))
    ein("Gre", (D, D)); ein("Gim", (D, D)); ein("Gim_neg", (D, D))
    ein("ident_f32", (128, 128))
    ein("ones_col", (128, 1)); ein("ones_row", (1, 128))
    for w in L1W:
        ein(w, (D, D))
    ein("bq", (D,)); ein("bk", (D,)); ein("bv", (D,)); ein("bo", (D,))
    ein("router_w", (D, 1)); ein("router_b", (1,))
    ein("W1", (4 * D, 4096)); ein("b1", (4 * 4096,))
    ein("W2", (4 * 4096, D)); ein("b2", (4 * D,))
    ein("ln1_g", (D,)); ein("ln1_b", (D,)); ein("ln2_g", (D,)); ein("ln2_b", (D,))
    ein("aWb", (64, D)); ein("aWv", (D, D)); ein("abv", (D,))
    ein("aWo", (D, D)); ein("abo", (D,))
    ein("cmask", (512, TT)); ein("ones_col_bf", (128, 1), BF16)
    for w, sh in [("W_dkv", (D, 256)), ("W_uk", (256, 512)), ("W_uv", (256, 512)),
                  ("W_dq", (D, 512)), ("W_uq", (512, 512)), ("W_qr", (512, 256)),
                  ("W_kr", (D, 32)), ("W_out", (512, D)),
                  ("idx_wq", (D, 256)), ("idx_wk", (D, 64))]:
        ein(w, sh)
    ein("idx_w", (4,)); ein("raw_delta", (16,))
    ein("iota_row", (128, L)); ein("ladv", (128, L)); ein("iota_p", (128, 1))
    ein("tab_kc", (16, L)); ein("tab_ks", (16, L))
    ein("tab_qc", (128, L)); ein("tab_qs", (128, L))
    ein("tselc", (L, 512)); ein("tsels", (L, 512))
    ein("bmaskA", (128, 192)); ein("bmaskB", (128, 192)); ein("rep16", (16, 128))
    ein("sel2", (16, 8 * 128))
    ein("colsel", (128, 32 * 32)); ein("rowsel", (32, 32 * 128))
    ein("striu34", (34, 32)); ein("triuI", (128, 128))
    ein("iota_tok", (128, 32), I32); ein("ce4m1", (2, 4))
    ein("oobfill", (1, 4 * MOE_CAP), I32)

    y = nc.dram_tensor("y", [D, L], F32, kind="ExternalOutput").ap()

    SC = {}

    def scr(name, shape, dt):
        SC[name] = nc.dram_tensor("sc_" + name, list(shape), dt, kind="Internal")

    for n in ["Fre", "Fim_neg", "Gre", "Gim", "Gim_neg",
              "wreT", "wimT", "WGre", "WGim", "M"]:
        scr(n + "_bf", (D, D), BF16)
    scr("ident", (128, 128), BF16)
    scr("xn_bf", (D, L), BF16)
    scr("xn_f32", (D, L), F32)
    scr("xres_a", (D, L), F32)
    scr("xres_b", (D, L), F32)
    need_l1 = any(st.startswith("l1ffn") for st in stages)
    need_l1a = any(st.startswith("l1attn") for st in stages)
    need_l0a = any(st.startswith("l0attn") for st in stages)
    L0W = ["W_dkv", "W_uk", "W_uv", "W_dq", "W_uq", "W_qr", "W_kr", "W_out",
           "idx_wq", "idx_wk"]
    if need_l0a:
        for w in L0W:
            scr(w + "_bf", nc_shape_of(w), BF16)
        scr("l0_ckv", (256, L), BF16); scr("l0_k", (512, L), BF16)
        scr("l0_cq", (512, L), BF16); scr("l0_qc", (512, L), BF16)
        scr("l0_qrr", (256, L), BF16); scr("l0_krr", (32, L), BF16)
        scr("l0_qi", (256, L), BF16); scr("l0_ki", (64, L), BF16)
        scr("l0_mfm", (32, L), BF16); scr("l0_krope", (32, L), BF16)
        scr("l0_qmu", (256, L), BF16); scr("l0_qband", (256, L), BF16)
        scr("l0_qct", (L, 512), BF16); scr("l0_qmut", (L, 256), BF16)
        scr("l0_Kcat", (768, L), BF16); scr("l0_Qcat", (768, L), BF16)
        scr("l0_kmv", (L, 1056), BF16)
        scr("l0_idx", (L, 32), I32)
        scr("l0_attnf", (512, L), BF16)
    if need_l1a:
        scr("aWbT_bf", (D, 64), BF16)
        scr("aWvT_bf", (D, D), BF16)
        scr("aWoT_bf", (D, D), BF16)
        scr("a_lat", (64, L), BF16)
        scr("a_va", (L, 1025), BF16)
        scr("a_y", (D, L), BF16)
    if need_l1:
        for w in L1W:
            scr(w + "_bf", (D, D), BF16)
        scr("rw_bf", (D, 1), BF16)
        scr("W1_bf", (4 * D, 4096), BF16)
        scr("W2_bf", (4 * 4096, D), BF16)
        scr("l1_q", (D, L), BF16); scr("l1_k", (D, L), BF16)
        scr("l1_va", (L, 16 * 65), BF16)
        scr("l1_attn", (D, L), BF16)
        scr("l1_t1", (D, L), F32)
        scr("l1_den", (16, L), F32)
        scr("l1_xtm", (L + 128, D), BF16)
        scr("l1_fftm", (L + 128, D), BF16)
        scr("l1_idx", (4 * MOE_CAP, 1), I32)
        scr("l1_t1nb", (D, L), BF16); scr("l1_t1nf", (D, L), F32)
        scr("l1_keys", (1, L), F32); scr("l1_emask", (4, L), F32)
        scr("l1_ffacc", (D, L), F32)

    with ExitStack() as c0:
        p0 = P(nc, c0)
        s_id = p0.sbuf("id0", [128, 128], BF16)
        l0 = p0.ctr("id0_l")
        with nc.Block() as blk:
            @blk.gpsimd
            def _(g):
                g.dma_start(out=s_id[:], in_=IN["ident_f32"]).then_inc(l0.sem, 16)
                g.wait_ge(l0.sem, 16)
                g.dma_start(out=SC["ident"].ap(), in_=s_id[:]).then_inc(l0.sem, 32)
                g.wait_ge(l0.sem, 48)
        phase_end(nc, p0)

    cast_jobs = [(IN[n], SC[n + "_bf"].ap())
                 for n in ["Fre", "Fim_neg", "Gre", "Gim", "Gim_neg"]]
    if need_l1:
        cast_jobs += [(IN[w], SC[w + "_bf"].ap()) for w in L1W]
        cast_jobs += [(IN["router_w"], SC["rw_bf"].ap()),
                      (IN["W1"], SC["W1_bf"].ap()), (IN["W2"], SC["W2_bf"].ap())]
    if need_l0a:
        cast_jobs += [(IN[w], SC[w + "_bf"].ap()) for w in L0W]
    emit_cast_bf16(nc, cast_jobs, "cc")
    tjobs = [(IN["w_freq_real"], SC["wreT_bf"].ap()),
             (IN["w_freq_imag"], SC["wimT_bf"].ap())]
    if need_l1a:
        tjobs += [(IN["aWb"], SC["aWbT_bf"].ap()),
                  (IN["aWv"], SC["aWvT_bf"].ap()),
                  (IN["aWo"], SC["aWoT_bf"].ap())]
    emit_transpose_bf16(nc, tjobs, IN["ident_f32"], "tw")
    emit_mm1024(nc, [(SC["wreT_bf"].ap(), SC["Gre_bf"].ap()),
                     (SC["wimT_bf"].ap(), SC["Gim_neg_bf"].ap())], SC["WGre_bf"].ap(), "m1")
    emit_mm1024(nc, [(SC["wreT_bf"].ap(), SC["Gim_bf"].ap()),
                     (SC["wimT_bf"].ap(), SC["Gre_bf"].ap())], SC["WGim_bf"].ap(), "m2")
    emit_mm1024(nc, [(SC["Fre_bf"].ap(), SC["WGre_bf"].ap()),
                     (SC["Fim_neg_bf"].ap(), SC["WGim_bf"].ap())], SC["M_bf"].ap(), "m3")

    cur = IN["x"]
    nxt = [SC["xres_a"].ap(), SC["xres_b"].ap()]
    for si, stg in enumerate(stages):
        dst = nxt[si % 2]
        wf32 = stg.startswith("l1ffn")
        emit_ln(nc, IN, SC, cur, want_f32=wf32, tag=f"ln{si}")
        if stg.startswith("ffn0"):
            emit_ffn0(nc, IN, SC, cur, dst, tag=f"ff{si}")
        elif stg.startswith("l1ffn"):
            emit_l1ffn(nc, IN, SC, cur, dst, tag=f"s{si}")
        elif stg.startswith("l1attn"):
            emit_l1attn_block(nc, IN, SC, cur, dst, tag=f"a{si}")
        elif stg.startswith("l0attn"):
            emit_l0attn_block(nc, IN, SC, cur, dst, tag=f"z{si}")
        else:
            raise ValueError(stg)
        cur = dst

    if DEBUG_COPY is not None:
        src_t = SC[DEBUG_COPY]
        R = src_t.shape[0] if hasattr(src_t, "shape") else src_t.ap().tensor.shape[0]
        dsrc = src_t.ap()
        rch = min(NCH, (R + 127) // 128)
    else:
        dsrc = cur
        rch = NCH
    with ExitStack() as c9:
        p9 = P(nc, c9)
        t9 = p9.sbuf("fin", [128, rch, TT], F32)
        l9 = p9.ctr("fin_l"); s9 = p9.ctr("fin_s")
        with nc.Block() as blk:
            @blk.gpsimd
            def _(sy):
                for i in range(NT):
                    if i >= 1:
                        sy.wait_ge(s9.sem, 16 * i)
                    sy.dma_start(out=t9[:], in_=fm_view(dsrc, 0, rch, i * TT, TT)
                                 ).then_inc(l9.sem, 16)
                    sy.wait_ge(l9.sem, 16 * (i + 1))
                    sy.dma_start(out=fm_view(y, 0, rch, i * TT, TT), in_=t9[:]
                                 ).then_inc(s9.sem, 16)
                sy.wait_ge(s9.sem, 16 * NT)
    return nc



def host_constants():
    d = np.arange(D)
    ang = 2 * np.pi * np.outer(d, d) / D
    C = {}
    C["Fre"] = np.cos(ang).astype(np.float32)
    C["Fim_neg"] = np.sin(ang).astype(np.float32)
    C["Gre"] = (np.cos(ang) / D).astype(np.float32)
    C["Gim"] = (np.sin(ang) / D).astype(np.float32)
    C["Gim_neg"] = (-np.sin(ang) / D).astype(np.float32)
    C["ident_f32"] = np.eye(128, dtype=np.float32)
    C["ones_col"] = np.ones((128, 1), np.float32)
    C["ones_row"] = np.ones((1, 128), np.float32)
    C["ones_col_bf"] = np.ones((128, 1), ml_dtypes.bfloat16)
    cm = np.zeros((4, 128, TT), np.float32)
    for o in range(4):
        kp = np.arange(128)[:, None] + 128 * o
        qf = np.arange(TT)[None, :]
        cm[o] = np.where(qf >= kp, 0.0, NEG)
    C["cmask"] = cm.reshape(512, TT)
    th = (1.0 / 10000.0 ** (2.0 * np.arange(16) / 32.0)).astype(np.float64)
    p = np.arange(L)
    C["tab_kc"] = np.cos(np.outer(th, p)).astype(np.float32)
    C["tab_ks"] = np.sin(np.outer(th, p)).astype(np.float32)
    tq = np.maximum(p, 2 * p - 63)
    C["tab_qc"] = np.tile(np.cos(np.outer(th, tq)).astype(np.float32), (8, 1))
    C["tab_qs"] = np.tile(np.sin(np.outer(th, tq)).astype(np.float32), (8, 1))
    jj = np.arange(64, 96)
    rel = p[:, None] - jj[None, :]
    ang = rel[:, :, None] * th[None, None, :]
    C["tselc"] = np.cos(ang).astype(np.float32).reshape(L, 512)
    C["tsels"] = np.sin(ang).astype(np.float32).reshape(L, 512)
    C["iota_row"] = np.tile(p.astype(np.float32), (128, 1))
    C["ladv"] = np.tile((-LADC - p * LADD).astype(np.float32), (128, 1))
    C["iota_p"] = np.arange(128, dtype=np.float32).reshape(128, 1)
    r = np.arange(128)[:, None]; c = np.arange(192)[None, :]
    C["bmaskA"] = np.where((c >= r + 1) & (c <= r + 64), 0.0, NEG).astype(np.float32)
    b0 = np.maximum(0, r - 63)
    C["bmaskB"] = np.where((c >= b0) & (c <= b0 + 63), 0.0, NEG).astype(np.float32)
    C["rep16"] = (np.arange(128)[None, :] % 16 == np.arange(16)[:, None]).astype(np.float32)
    sel2 = np.zeros((16, 8, 128), np.float32)
    for kc in range(8):
        for p in range(128):
            sel2[2 * kc + p // 64, kc, p] = 1.0
    C["sel2"] = sel2.reshape(16, 8 * 128)
    CAPm = 4 * MOE_CAP
    C["colsel"] = np.repeat((np.arange(32)[None, :, None] == np.arange(32)[None, None, :])
                            .astype(np.float32), 128, axis=0).reshape(128, 32 * 32)
    C["rowsel"] = np.repeat((np.arange(32)[:, None, None] == np.arange(32)[None, :, None])
                            .astype(np.float32), 128, axis=2).reshape(32, 32 * 128)
    su = np.zeros((34, 32), np.float32)
    su[:32, :] = (np.arange(32)[:, None] < np.arange(32)[None, :]).astype(np.float32)
    su[32, :] = 1.0
    su[33, :] = 1.0
    C["striu34"] = su
    C["triuI"] = (np.arange(128)[:, None] <= np.arange(128)[None, :]).astype(np.float32)
    C["iota_tok"] = (np.arange(32)[None, :] * 128
                     + np.arange(128)[:, None]).astype(np.int32)
    C["ce4m1"] = np.stack([np.arange(4, dtype=np.float32) * MOE_CAP,
                           np.full(4, -1.0, np.float32)])
    C["oobfill"] = np.full((1, CAPm), L, np.int32)
    return C


def make_in_map(b, inputs, C, stages=("ffn0a",)):
    f = lambda k: np.ascontiguousarray(np.asarray(inputs[k], np.float32))
    m = dict(C)
    m["x"] = np.ascontiguousarray(np.asarray(inputs["L0_attn_x"][b], np.float32).T)
    m["w_freq_real"] = f("L0_ffn_w_freq_real")
    m["w_freq_imag"] = f("L0_ffn_w_freq_imag")
    m["ffn_bias"] = f("L0_ffn_bias")
    if any(st.startswith("l1ffn") for st in stages):
        m["Wq"] = f("L1_ffn_Wq"); m["bq"] = f("L1_ffn_bq")
        m["Wk"] = f("L1_ffn_Wk"); m["bk"] = f("L1_ffn_bk")
        m["Wv"] = f("L1_ffn_Wv"); m["bv"] = f("L1_ffn_bv")
        m["Wo"] = f("L1_ffn_Wo"); m["bo"] = f("L1_ffn_bo")
        m["router_w"] = f("L1_ffn_router_w"); m["router_b"] = f("L1_ffn_router_b")
        m["W1"] = f("L1_ffn_W1").reshape(4 * D, 4096)
        m["b1"] = f("L1_ffn_b1").reshape(-1)
        m["W2"] = f("L1_ffn_W2").reshape(4 * 4096, D)
        m["b2"] = f("L1_ffn_b2").reshape(-1)
        m["ln1_g"] = f("L1_ffn_ln1_g"); m["ln1_b"] = f("L1_ffn_ln1_b")
        m["ln2_g"] = f("L1_ffn_ln2_g"); m["ln2_b"] = f("L1_ffn_ln2_b")
    if any(st.startswith("l1attn") for st in stages):
        m["aWb"] = f("L1_attn_W_basis")
        m["aWv"] = f("L1_attn_W_v"); m["abv"] = f("L1_attn_b_v")
        m["aWo"] = f("L1_attn_W_o"); m["abo"] = f("L1_attn_b_o")
    else:
        m["aWb"] = np.zeros((64, D), np.float32)
        m["aWv"] = np.zeros((D, D), np.float32); m["abv"] = np.zeros((D,), np.float32)
        m["aWo"] = np.zeros((D, D), np.float32); m["abo"] = np.zeros((D,), np.float32)
    if any(st.startswith("l0attn") for st in stages):
        m["W_dkv"] = f("L0_attn_W_dkv"); m["W_uk"] = f("L0_attn_W_uk")
        m["W_uv"] = f("L0_attn_W_uv"); m["W_dq"] = f("L0_attn_W_dq")
        m["W_uq"] = f("L0_attn_W_uq"); m["W_qr"] = f("L0_attn_W_qr")
        m["W_kr"] = f("L0_attn_W_kr"); m["W_out"] = f("L0_attn_W_out")
        m["idx_wq"] = f("L0_attn_idx_wq"); m["idx_wk"] = f("L0_attn_idx_wk")
        m["idx_w"] = f("L0_attn_idx_w"); m["raw_delta"] = f("L0_attn_raw_delta")
    else:
        for w, sh in _L0_SHAPES.items():
            m[w] = np.zeros(sh, np.float32)
        m["idx_w"] = np.zeros((4,), np.float32)
        m["raw_delta"] = np.zeros((16,), np.float32)
    if not any(st.startswith("l1ffn") for st in stages):
        for w in L1W:
            m[w] = np.zeros((D, D), np.float32)
        for bnm in ["bq", "bk", "bv", "bo"]:
            m[bnm] = np.zeros((D,), np.float32)
        m["router_w"] = np.zeros((D, 1), np.float32)
        m["router_b"] = np.zeros((1,), np.float32)
        m["W1"] = np.zeros((4 * D, 4096), np.float32)
        m["b1"] = np.zeros((4 * 4096,), np.float32)
        m["W2"] = np.zeros((4 * 4096, D), np.float32)
        m["b2"] = np.zeros((4 * D,), np.float32)
        for g in ["ln1_g", "ln1_b", "ln2_g", "ln2_b"]:
            m[g] = np.zeros((D,), np.float32)
    return m


FULL_STAGES = ("ffn0a", "l0attn", "ffn0b", "l1ffn_a", "l1attn", "l1ffn_b")


def kernel(**inputs):
    C = host_constants()
    nc = build_program(stages=FULL_STAGES)
    in_maps = [make_in_map(b, inputs, C, FULL_STAGES) for b in range(8)]
    res = run_bass_kernel_spmd(nc, in_maps, core_ids=list(range(8)))
    return np.stack([np.asarray(r["y"], np.float32).T for r in res.results], axis=0)


# version 08
# speedup vs baseline: 1.0186x; 1.0186x over previous
import math
import numpy as np
import ml_dtypes
from contextlib import ExitStack

import concourse.bass as bass
import concourse.mybir as mybir
from concourse.bass_utils import run_bass_kernel_spmd
from concourse.bass_core_rust import IndirectOffsetOnAxis

F32 = mybir.dt.float32
BF16 = mybir.dt.bfloat16
I32 = mybir.dt.int32
U32 = mybir.dt.uint32
AF = mybir.ActivationFunctionType
ALU = mybir.AluOpType
AX = mybir.AxisListType

L = 4096
D = 1024
NCH = D // 128
TT = 512
NT = L // TT
NEG = -3.0e38
LADC = 1.0e30
LADD = 1.0e24
N_PHASES_DEFAULT = 99
DEBUG_COPY = None


class Ctr:
    def __init__(self, nc, name):
        self.sem = nc.alloc_semaphore(name)
        self.n = 0


class P:
    def __init__(self, nc, ctx):
        self.nc = nc
        self.ctx = ctx
        self.sems = []

    def sbuf(self, name, shape, dt):
        return self.ctx.enter_context(self.nc.sbuf_tensor(name, shape, dt))

    def psum(self, name, shape, dt=F32):
        return self.ctx.enter_context(self.nc.psum_tensor(name, shape, dt))

    def ctr(self, name):
        c = Ctr(self.nc, name)
        self.sems.append(c.sem)
        return c


def phase_end(nc, pp):
    nc.all_engine_barrier()
    if pp.sems:
        nc.clear_and_free_semaphores(pp.sems)
        pp.sems = []
    nc.all_engine_barrier()


def fm_view(dram_ap, c0, nch, t0, nt):
    return dram_ap[c0 * 128:(c0 + nch) * 128, t0:t0 + nt].rearrange(
        "(c p) t -> p c t", p=128)



def emit_cast_bf16(nc, jobs, tag):
    steps = []
    for src, dst in jobs:
        R, C = src.tensor.shape[-2], src.tensor.shape[-1]
        for r in range((R + 127) // 128):
            steps.append((src, dst, r, min(128, R - r * 128), C))
    NB = 4
    with ExitStack() as ctx:
        pp = P(nc, ctx)
        sf = [pp.sbuf(f"{tag}_f{b}", [128, 4096], F32) for b in range(NB)]
        sb = [pp.sbuf(f"{tag}_b{b}", [128, 4096], BF16) for b in range(NB)]
        ld = [pp.ctr(f"{tag}_ld{b}") for b in range(NB)]
        st = [pp.ctr(f"{tag}_st{b}") for b in range(NB)]
        ck = pp.ctr(f"{tag}_ck")
        n = len(steps)
        ldc = [0] * NB
        stc = [0] * NB

        def loader(eng, par):
            for i, (src, dst, r, rows, C) in enumerate(steps):
                b = i % NB
                if b % 2 != par:
                    continue
                if i >= NB:
                    eng.wait_ge(ck.sem, i - NB + 1)
                eng.dma_start(out=sf[b][:rows, :C],
                              in_=src[r * 128:r * 128 + rows, :]).then_inc(ld[b].sem, 16)

        with nc.Block() as blk:
            @blk.sync
            def _(sy):
                loader(sy, 0)

            @blk.scalar
            def _(sc):
                loader(sc, 1)

            @blk.vector
            def _(v):
                for i, (src, dst, r, rows, C) in enumerate(steps):
                    b = i % NB
                    ldc[b] += 1
                    v.wait_ge(ld[b].sem, 16 * ldc[b])
                    if i >= NB:
                        v.wait_ge(st[b].sem, 16 * (i // NB))
                    v.tensor_copy(out=sb[b][:rows, :C], in_=sf[b][:rows, :C]
                                  ).then_inc(ck.sem, 1)

            @blk.gpsimd
            def _(g):
                for i, (src, dst, r, rows, C) in enumerate(steps):
                    b = i % NB
                    g.wait_ge(ck.sem, i + 1)
                    g.dma_start(out=dst[r * 128:r * 128 + rows, :],
                                in_=sb[b][:rows, :C]).then_inc(st[b].sem, 16)
                for b in range(NB):
                    g.wait_ge(st[b].sem, 16 * len([i for i in range(n) if i % NB == b]))
        phase_end(nc, pp)


def emit_transpose_bf16(nc, jobs, ident_bf_dram, tag):
    with ExitStack() as ctx:
        pp = P(nc, ctx)
        s_id = pp.sbuf(f"{tag}_id", [128, 128], F32)
        s_in = [pp.sbuf(f"{tag}_in{b}", [128, 128], F32) for b in range(2)]
        s_out = [pp.sbuf(f"{tag}_out{b}", [128, 128], BF16) for b in range(2)]
        ps_t = [pp.psum(f"{tag}_ps{b}", [128, 128], F32) for b in range(2)]
        lid = pp.ctr(f"{tag}_li")
        l2 = [pp.ctr(f"{tag}_l{b}") for b in range(2)]
        s2 = [pp.ctr(f"{tag}_s{b}") for b in range(2)]
        e2 = pp.ctr(f"{tag}_e"); v2 = pp.ctr(f"{tag}_v")
        steps = []
        for src, dst in jobs:
            R, C = src.tensor.shape[-2], src.tensor.shape[-1]
            for r in range((R + 127) // 128):
                for c in range(C // 128):
                    steps.append((src, dst, r, c, min(128, R - r * 128)))
        with nc.Block() as blk:
            @blk.sync
            def _(sy):
                sy.dma_start(out=s_id[:], in_=ident_bf_dram).then_inc(lid.sem, 16)
                for i, (src, dst, r, c, rows) in enumerate(steps):
                    if i >= 2:
                        sy.wait_ge(e2.sem, i - 1)
                    sy.dma_start(out=s_in[i % 2][:rows, :],
                                 in_=src[r * 128:r * 128 + rows, c * 128:(c + 1) * 128]
                                 ).then_inc(l2[i % 2].sem, 16)

            @blk.tensor
            def _(t):
                t.wait_ge(lid.sem, 16)
                for i, (src, dst, r, c, rows) in enumerate(steps):
                    t.wait_ge(l2[i % 2].sem, 16 * (i // 2 + 1))
                    if i >= 2:
                        t.wait_ge(v2.sem, i - 1)
                    t.transpose(ps_t[i % 2][:, :rows], in_=s_in[i % 2][:rows, :],
                                identity=s_id[0:rows, 0:rows]).then_inc(e2.sem, 1)

            @blk.vector
            def _(v):
                for i, (src, dst, r, c, rows) in enumerate(steps):
                    v.wait_ge(e2.sem, i + 1)
                    if i >= 2:
                        v.wait_ge(s2[i % 2].sem, 16 * (i // 2))
                    v.tensor_copy(out=s_out[i % 2][:, :rows], in_=ps_t[i % 2][:, :rows]
                                  ).then_inc(v2.sem, 1)

            @blk.gpsimd
            def _(g):
                for i, (src, dst, r, c, rows) in enumerate(steps):
                    g.wait_ge(v2.sem, i + 1)
                    g.dma_start(out=dst[c * 128:(c + 1) * 128, r * 128:r * 128 + rows],
                                in_=s_out[i % 2][:, :rows]).then_inc(s2[i % 2].sem, 16)
                g.wait_ge(s2[0].sem, 16 * ((len(steps) + 1) // 2))
                g.wait_ge(s2[1].sem, 16 * (len(steps) // 2))
        phase_end(nc, pp)


def emit_mm1024(nc, pairs, dst_dram, tag):
    npair = len(pairs)
    with ExitStack() as ctx:
        pp = P(nc, ctx)
        sL = [pp.sbuf(f"{tag}_L{i}", [128, NCH, D], BF16) for i in range(npair)]
        sR = [[pp.sbuf(f"{tag}_R{b}{i}", [128, NCH, TT], BF16) for i in range(npair)]
              for b in range(2)]
        sOb = pp.sbuf(f"{tag}_O", [128, TT], BF16)
        ps = [pp.psum(f"{tag}_P{b}", [128, TT], F32) for b in range(2)]
        lw = pp.ctr(f"{tag}_lw"); lr = [pp.ctr(f"{tag}_lr{b}") for b in range(2)]
        mm = pp.ctr(f"{tag}_mm"); ev = pp.ctr(f"{tag}_ev"); so = pp.ctr(f"{tag}_so")
        with nc.Block() as blk:
            @blk.sync
            def _(sy):
                for i, (lhsT, rhs) in enumerate(pairs):
                    sy.dma_start(out=sL[i][:],
                                 in_=lhsT.rearrange("(c p) m -> p c m", p=128)
                                 ).then_inc(lw.sem, 16)
                for nt in range(2):
                    for i, (lhsT, rhs) in enumerate(pairs):
                        sy.dma_start(out=sR[nt][i][:],
                                     in_=fm_view(rhs, 0, NCH, nt * TT, TT)
                                     ).then_inc(lr[nt].sem, 16)

            @blk.tensor
            def _(t):
                t.wait_ge(lw.sem, 16 * npair)
                for nt in range(2):
                    t.wait_ge(lr[nt].sem, 16 * npair)
                    for mc in range(8):
                        b = mc % 2
                        tot = nt * 8 + mc
                        if tot >= 2:
                            t.wait_ge(ev.sem, tot - 1)
                        for i in range(npair):
                            for kc in range(NCH):
                                inst = t.matmul(ps[b][:],
                                                lhsT=sL[i][:, kc, mc * 128:(mc + 1) * 128],
                                                rhs=sR[nt][i][:, kc, :],
                                                start=(i == 0 and kc == 0),
                                                stop=(i == npair - 1 and kc == NCH - 1))
                        inst.then_inc(mm.sem, 1)

            @blk.vector
            def _(v):
                for tot in range(16):
                    v.wait_ge(mm.sem, tot + 1)
                    if tot >= 1:
                        v.wait_ge(so.sem, 16 * tot)
                    v.tensor_copy(out=sOb[:], in_=ps[tot % 2][:]).then_inc(ev.sem, 1)

            @blk.scalar
            def _(s):
                for nt in range(2):
                    for mc in range(8):
                        tot = nt * 8 + mc
                        s.wait_ge(ev.sem, tot + 1)
                        s.dma_start(out=dst_dram[mc * 128:(mc + 1) * 128, nt * TT:(nt + 1) * TT],
                                    in_=sOb[:]).then_inc(so.sem, 16)
                s.wait_ge(so.sem, 16 * 16)
        phase_end(nc, pp)



def emit_ln(nc, IN, SC, src_dram, want_f32, tag, gamma=None, beta=None,
            out_bf=None, out_f32=None, res_src=None, res_dst=None, res_scale=0.5):
    if out_bf is None:
        out_bf = SC["xn_bf"].ap()
    if out_f32 is None:
        out_f32 = SC["xn_f32"].ap()
    with ExitStack() as ctx:
        pp = P(nc, ctx)
        sx = [pp.sbuf(f"{tag}_x{b}", [128, NCH, TT], F32) for b in range(2)]
        ssq = pp.sbuf(f"{tag}_sq", [128, NCH, TT], F32)
        sap = pp.sbuf(f"{tag}_ap", [128, NCH, TT], F32)
        sst = pp.sbuf(f"{tag}_st", [1, 4 * TT], F32)
        sab = pp.sbuf(f"{tag}_ab", [1, 2 * TT], F32)
        sxn = [pp.sbuf(f"{tag}_o{b}", [128, NCH, TT], BF16) for b in range(2)]
        sxf = [pp.sbuf(f"{tag}_f{b}", [128, NCH, TT], F32) for b in range(2)] if want_f32 else None
        want_res = res_dst is not None
        sgb = pp.sbuf(f"{tag}_gb", [128, 2 * NCH], F32) if gamma is not None else None
        srs = [pp.sbuf(f"{tag}_rs{b}", [128, NCH, TT], F32) for b in range(2)] if want_res else None
        sro = [pp.sbuf(f"{tag}_ro{b}", [128, NCH, TT], F32) for b in range(2)] if want_res else None
        ones_c = pp.sbuf(f"{tag}_onc", [128, 1], F32)
        ones_r = pp.sbuf(f"{tag}_onr", [1, 128], F32)
        ps_s = pp.psum(f"{tag}_ps", [1, 2 * TT], F32)
        ps_a = pp.psum(f"{tag}_pa", [128, TT], F32)
        ps_b = pp.psum(f"{tag}_pb", [128, TT], F32)
        cst = pp.ctr(f"{tag}_cs")
        ld = [pp.ctr(f"{tag}_ld{b}") for b in range(2)]
        vq = pp.ctr(f"{tag}_vq")
        sq = pp.ctr(f"{tag}_sq"); pe = pp.ctr(f"{tag}_pe")
        dv = pp.ctr(f"{tag}_dv"); bc = pp.ctr(f"{tag}_bc"); ap = pp.ctr(f"{tag}_aq")
        st = [pp.ctr(f"{tag}_s2{b}") for b in range(2)]
        nsp = 1 if res_dst is not None else (2 if want_f32 else 1)
        ncst_tot = 64 if gamma is not None else 32

        with nc.Block() as blk:
            @blk.sync
            def _(sy):
                sy.dma_start(out=ones_c[:], in_=IN["ones_col"]).then_inc(cst.sem, 16)
                sy.dma_start(out=ones_r[:], in_=IN["ones_row"]).then_inc(cst.sem, 16)
                ncst = 2
                if gamma is not None:
                    sy.dma_start(out=sgb[:, 0:NCH], in_=gamma.rearrange("(c p) -> p c", p=128),
                                 allow_slow_non_contiguous=True).then_inc(cst.sem, 16)
                    sy.dma_start(out=sgb[:, NCH:2 * NCH], in_=beta.rearrange("(c p) -> p c", p=128),
                                 allow_slow_non_contiguous=True).then_inc(cst.sem, 16)
                    ncst = 4
                for i in range(NT):
                    if i >= 2:
                        sy.wait_ge(ap.sem, i - 1)
                    sy.dma_start(out=sx[i % 2][:],
                                 in_=fm_view(src_dram, 0, NCH, i * TT, TT)
                                 ).then_inc(ld[i % 2].sem, 16)
                    if want_res:
                        sy.dma_start(out=srs[i % 2][:],
                                     in_=fm_view(res_src, 0, NCH, i * TT, TT)
                                     ).then_inc(ld[i % 2].sem, 16)

            nldp = 32 if want_res else 16

            @blk.scalar
            def _(s):
                for i in range(NT):
                    s.wait_ge(ld[i % 2].sem, nldp * (i // 2 + 1))
                    if i >= 1:
                        s.wait_ge(pe.sem, 2 * i)
                    s.activation(out=ssq[:], in_=sx[i % 2][:], func=AF.Square
                                 ).then_inc(sq.sem, 1)
                    s.wait_ge(dv.sem, 2 * i + 1)
                    s.activation(out=sst[:, 3 * TT:4 * TT], in_=sst[:, 2 * TT:3 * TT],
                                 func=AF.Sqrt).then_inc(sq.sem, 1)

            @blk.tensor
            def _(t):
                t.wait_ge(cst.sem, ncst_tot)
                for i in range(NT):
                    t.wait_ge(ld[i % 2].sem, nldp * (i // 2 + 1))
                    t.wait_ge(sq.sem, 2 * i + 1)
                    if i >= 1:
                        t.wait_ge(dv.sem, 2 * i - 1)
                    for kc in range(NCH):
                        t.matmul(ps_s[:, 0:TT], lhsT=ones_c[:], rhs=sx[i % 2][:, kc, :],
                                 start=(kc == 0), stop=(kc == NCH - 1))
                    for kc in range(NCH):
                        inst = t.matmul(ps_s[:, TT:2 * TT], lhsT=ones_c[:], rhs=ssq[:, kc, :],
                                        start=(kc == 0), stop=(kc == NCH - 1))
                    inst.then_inc(pe.sem, 2)
                    t.wait_ge(dv.sem, 2 * i + 2)
                    t.matmul(ps_a[:], lhsT=ones_r[:], rhs=sab[:, 0:TT], start=True, stop=True)
                    t.matmul(ps_b[:], lhsT=ones_r[:], rhs=sab[:, TT:2 * TT], start=True, stop=True
                             ).then_inc(bc.sem, 1)

            @blk.vector
            def _(v):
                def step(inst):
                    inst.then_inc(vq.sem, 1)
                    vq.n += 1
                    v.wait_ge(vq.sem, vq.n)
                v.wait_ge(cst.sem, ncst_tot)
                for i in range(NT):
                    v.wait_ge(pe.sem, 2 * (i + 1))
                    step(v.tensor_scalar(out=sst[:, 0:TT], in0=ps_s[:, 0:TT], scalar1=1.0 / D,
                                         scalar2=None, op0=ALU.mult))
                    step(v.tensor_scalar(out=sst[:, TT:2 * TT], in0=ps_s[:, TT:2 * TT],
                                         scalar1=1.0 / D, scalar2=None, op0=ALU.mult))
                    step(v.tensor_tensor(out=sst[:, 2 * TT:3 * TT], in0=sst[:, 0:TT],
                                         in1=sst[:, 0:TT], op=ALU.mult))
                    step(v.tensor_tensor(out=sst[:, 2 * TT:3 * TT], in0=sst[:, TT:2 * TT],
                                         in1=sst[:, 2 * TT:3 * TT], op=ALU.subtract))
                    v.tensor_scalar(out=sst[:, 2 * TT:3 * TT], in0=sst[:, 2 * TT:3 * TT],
                                    scalar1=1e-5, scalar2=None, op0=ALU.add).then_inc(dv.sem, 1)
                    v.wait_ge(sq.sem, 2 * i + 2)
                    step(v.reciprocal(out=sab[:, 0:TT], in_=sst[:, 3 * TT:4 * TT]))
                    step(v.tensor_tensor(out=sab[:, TT:2 * TT], in0=sst[:, 0:TT],
                                         in1=sab[:, 0:TT], op=ALU.mult))
                    v.tensor_scalar(out=sab[:, TT:2 * TT], in0=sab[:, TT:2 * TT],
                                    scalar1=-1.0, scalar2=None, op0=ALU.mult).then_inc(dv.sem, 1)
                    v.wait_ge(bc.sem, i + 1)
                    if i >= 2:
                        v.wait_ge(st[i % 2].sem, 16 * nsp * (i // 2))
                    for kc in range(NCH):
                        step(v.tensor_tensor(out=sap[:, kc, :], in0=sx[i % 2][:, kc, :],
                                             in1=ps_a[:], op=ALU.mult))
                        if gamma is None:
                            inst = v.tensor_tensor(out=sxn[i % 2][:, kc, :], in0=sap[:, kc, :],
                                                   in1=ps_b[:], op=ALU.add)
                            if want_f32:
                                inst = v.tensor_tensor(out=sxf[i % 2][:, kc, :], in0=sap[:, kc, :],
                                                       in1=ps_b[:], op=ALU.add)
                            if want_res:
                                step(v.tensor_tensor(out=sap[:, kc, :], in0=sap[:, kc, :],
                                                     in1=ps_b[:], op=ALU.add))
                        else:
                            step(v.tensor_tensor(out=sap[:, kc, :], in0=sap[:, kc, :],
                                                 in1=ps_b[:], op=ALU.add))
                            if want_res:
                                step(v.tensor_scalar(out=sap[:, kc, :], in0=sap[:, kc, :],
                                                     scalar1=sgb[:, kc:kc + 1],
                                                     scalar2=sgb[:, NCH + kc:NCH + kc + 1],
                                                     op0=ALU.mult, op1=ALU.add))
                            else:
                                inst = v.tensor_scalar(out=sxn[i % 2][:, kc, :], in0=sap[:, kc, :],
                                                       scalar1=sgb[:, kc:kc + 1],
                                                       scalar2=sgb[:, NCH + kc:NCH + kc + 1],
                                                       op0=ALU.mult, op1=ALU.add)
                                if want_f32:
                                    inst = v.tensor_scalar(out=sxf[i % 2][:, kc, :], in0=sap[:, kc, :],
                                                           scalar1=sgb[:, kc:kc + 1],
                                                           scalar2=sgb[:, NCH + kc:NCH + kc + 1],
                                                           op0=ALU.mult, op1=ALU.add)
                        if want_res:
                            inst = v.scalar_tensor_tensor(out=sro[i % 2][:, kc, :],
                                                          in0=sap[:, kc, :], scalar=res_scale,
                                                          in1=srs[i % 2][:, kc, :],
                                                          op0=ALU.mult, op1=ALU.add)
                    inst.then_inc(ap.sem, 1)

            @blk.gpsimd
            def _(g):
                for i in range(NT):
                    g.wait_ge(ap.sem, i + 1)
                    if want_res:
                        g.dma_start(out=fm_view(res_dst, 0, NCH, i * TT, TT),
                                    in_=sro[i % 2][:]).then_inc(st[i % 2].sem, 16)
                    else:
                        g.dma_start(out=fm_view(out_bf, 0, NCH, i * TT, TT),
                                    in_=sxn[i % 2][:]).then_inc(st[i % 2].sem, 16)
                        if want_f32:
                            g.dma_start(out=fm_view(out_f32, 0, NCH, i * TT, TT),
                                        in_=sxf[i % 2][:]).then_inc(st[i % 2].sem, 16)
                g.wait_ge(st[0].sem, 16 * nsp * ((NT + 1) // 2))
                g.wait_ge(st[1].sem, 16 * nsp * (NT // 2))
        phase_end(nc, pp)



def emit_ffn0(nc, IN, SC, xres_src, xres_dst, tag):
    with ExitStack() as ctx:
        pp = P(nc, ctx)
        sM = pp.sbuf(f"{tag}_M", [128, NCH, D], BF16)
        sb = pp.sbuf(f"{tag}_b", [128, NCH], F32)
        sxn = [pp.sbuf(f"{tag}_x{b}", [128, NCH, TT], BF16) for b in range(2)]
        sxr = [pp.sbuf(f"{tag}_r{b}", [128, NCH, TT], F32) for b in range(2)]
        syo = [pp.sbuf(f"{tag}_y{b}", [128, NCH, TT], F32) for b in range(2)]
        ps = [pp.psum(f"{tag}_p{b}", [128, TT], F32) for b in range(4)]
        cst = pp.ctr(f"{tag}_cs")
        ld = [pp.ctr(f"{tag}_ld{b}") for b in range(2)]
        mm = pp.ctr(f"{tag}_mm")
        vq = pp.ctr(f"{tag}_vq")
        ev = pp.ctr(f"{tag}_ev")
        st = [pp.ctr(f"{tag}_st{b}") for b in range(2)]

        with nc.Block() as blk:
            @blk.sync
            def _(sy):
                sy.dma_start(out=sM[:], in_=SC["M_bf"].ap().rearrange("(c p) m -> p c m", p=128)
                             ).then_inc(cst.sem, 16)
                sy.dma_start(out=sb[:], in_=IN["ffn_bias"].rearrange("(c p) -> p c", p=128),
                             allow_slow_non_contiguous=True).then_inc(cst.sem, 16)
                for i in range(NT):
                    if i >= 2:
                        sy.wait_ge(st[i % 2].sem, 16 * NCH * (i // 2))
                    sy.dma_start(out=sxn[i % 2][:],
                                 in_=fm_view(SC["xn_bf"].ap(), 0, NCH, i * TT, TT)
                                 ).then_inc(ld[i % 2].sem, 16)
                    sy.dma_start(out=sxr[i % 2][:],
                                 in_=fm_view(xres_src, 0, NCH, i * TT, TT)).then_inc(ld[i % 2].sem, 16)

            @blk.tensor
            def _(t):
                t.wait_ge(cst.sem, 32)
                for i in range(NT):
                    t.wait_ge(ld[i % 2].sem, 32 * (i // 2 + 1))
                    for mc in range(NCH):
                        b = mc % 4
                        tot = i * NCH + mc
                        if tot >= 4:
                            t.wait_ge(ev.sem, tot - 3)
                        for kc in range(NCH):
                            inst = t.matmul(ps[b][:], lhsT=sM[:, kc, mc * 128:(mc + 1) * 128],
                                            rhs=sxn[i % 2][:, kc, :],
                                            start=(kc == 0), stop=(kc == NCH - 1))
                        inst.then_inc(mm.sem, 1)

            @blk.vector
            def _(v):
                for i in range(NT):
                    for mc in range(NCH):
                        b = mc % 4
                        tot = i * NCH + mc
                        v.wait_ge(mm.sem, tot + 1)
                        if i >= 2:
                            v.wait_ge(st[i % 2].sem, 16 * NCH * (i // 2))
                        v.tensor_scalar(out=syo[i % 2][:, mc, :], in0=ps[b][:],
                                        scalar1=sb[:, mc:mc + 1], scalar2=0.5,
                                        op0=ALU.add, op1=ALU.mult).then_inc(vq.sem, 1)
                        vq.n += 1
                        v.wait_ge(vq.sem, vq.n)
                        v.tensor_tensor(out=syo[i % 2][:, mc, :], in0=syo[i % 2][:, mc, :],
                                        in1=sxr[i % 2][:, mc, :], op=ALU.add).then_inc(ev.sem, 1)

            @blk.gpsimd
            def _(g):
                for i in range(NT):
                    for mc in range(NCH):
                        g.wait_ge(ev.sem, i * NCH + mc + 1)
                        g.dma_start(out=xres_dst[mc * 128:(mc + 1) * 128, i * TT:(i + 1) * TT],
                                    in_=syo[i % 2][:, mc, :]).then_inc(st[i % 2].sem, 16)
                g.wait_ge(st[0].sem, 16 * NCH * ((NT + 1) // 2))
                g.wait_ge(st[1].sem, 16 * NCH * (NT // 2))
        phase_end(nc, pp)





def emit_proj(nc, IN, tag, W_bf, K, M, x_bf, out_dram, out_dt, bias=None,
              post_scale=None, res_f32=None, f32_in=False, div_den=None):
    KCH, MCH = K // 128, (M + 127) // 128
    mrows = min(128, M)

    def mr(mc):
        return min(128, M - mc * 128)

    def psc(mc):
        if isinstance(post_scale, (list, tuple)):
            return post_scale[mc]
        return post_scale
    DTI = F32 if f32_in else BF16
    with ExitStack() as ctx:
        pp = P(nc, ctx)
        sW = pp.sbuf(f"{tag}_W", [128, KCH, M], DTI)
        sB = pp.sbuf(f"{tag}_b", [mrows, MCH], F32) if bias is not None else None
        sx = [pp.sbuf(f"{tag}_x{b}", [128, KCH, TT], DTI) for b in range(2)]
        sr = [pp.sbuf(f"{tag}_r{b}", [mrows, MCH, TT], F32) for b in range(2)] if res_f32 is not None else None
        so = [pp.sbuf(f"{tag}_o{b}", [mrows, MCH, TT], out_dt) for b in range(2)]
        ps = [pp.psum(f"{tag}_p{b}", [mrows, TT], F32) for b in range(4)]
        cst = pp.ctr(f"{tag}_cs")
        ld = [pp.ctr(f"{tag}_ld{b}") for b in range(2)]
        mm = pp.ctr(f"{tag}_mm"); vq = pp.ctr(f"{tag}_vq")
        ev = pp.ctr(f"{tag}_ev")
        st = [pp.ctr(f"{tag}_st{b}") for b in range(2)]
        nldp = 32 if res_f32 is not None else 16
        ncst = 32 if bias is not None else 16
        if div_den is not None:
            srdn = pp.sbuf(f"{tag}_rdn", [16, L], F32)
            ssel2 = pp.sbuf(f"{tag}_sl2", [16, 8, 128], F32)
            ps_db = [pp.psum(f"{tag}_db{b}", [128, TT], F32) for b in range(2)]
            dnl = pp.ctr(f"{tag}_dnl"); dnr = pp.ctr(f"{tag}_dnr")
            dbb = pp.ctr(f"{tag}_dbb"); dvx = pp.ctr(f"{tag}_dvx")

        with nc.Block() as blk:
            @blk.sync
            def _(sy):
                if div_den is not None:
                    sy.dma_start(out=srdn[:], in_=div_den).then_inc(dnl.sem, 16)
                    sy.dma_start(out=ssel2[:], in_=IN["sel2"].rearrange("j (c p) -> j c p", p=128)
                                 ).then_inc(dnl.sem, 16)
                sy.dma_start(out=sW[:], in_=W_bf.rearrange("(c p) m -> p c m", p=128),
                             allow_slow_non_contiguous=(M < 128)).then_inc(cst.sem, 16)
                if bias is not None:
                    sy.dma_start(out=sB[:], in_=bias.rearrange("(c p) -> p c", p=mrows),
                                 allow_slow_non_contiguous=True).then_inc(cst.sem, 16)
                for i in range(NT):
                    if i >= 2:
                        sy.wait_ge(st[i % 2].sem, 16 * MCH * (i // 2))
                    sy.dma_start(out=sx[i % 2][:], in_=fm_view(x_bf, 0, KCH, i * TT, TT)
                                 ).then_inc(ld[i % 2].sem, 16)
                    if res_f32 is not None:
                        sy.dma_start(out=sr[i % 2][:],
                                     in_=fm_view(res_f32, 0, MCH, i * TT, TT)[:mrows]
                                     ).then_inc(ld[i % 2].sem, 16)

            @blk.tensor
            def _(t):
                t.wait_ge(cst.sem, ncst)
                if div_den is not None:
                    t.wait_ge(dnl.sem, 32)
                    t.wait_ge(dnr.sem, 1)
                for i in range(NT):
                    t.wait_ge(ld[i % 2].sem, nldp * (i // 2 + 1))
                    if div_den is not None:
                        for kc in range(KCH):
                            gdb = i * KCH + kc
                            if gdb >= 2:
                                t.wait_ge(dvx.sem, gdb - 1)
                            t.matmul(ps_db[gdb % 2][:], lhsT=ssel2[:, kc, :],
                                     rhs=srdn[:, i * TT:(i + 1) * TT],
                                     start=True, stop=True).then_inc(dbb.sem, 1)
                        t.wait_ge(dvx.sem, i * KCH + KCH)
                    for mc in range(MCH):
                        tot = i * MCH + mc
                        b = tot % 4
                        if tot >= 4:
                            t.wait_ge(ev.sem, tot - 3)
                        for kc in range(KCH):
                            inst = t.matmul(ps[b][0:mr(mc), :],
                                            lhsT=sW[:, kc, mc * 128:mc * 128 + mr(mc)],
                                            rhs=sx[i % 2][:, kc, :],
                                            start=(kc == 0), stop=(kc == KCH - 1))
                        inst.then_inc(mm.sem, 1)

            @blk.vector
            def _(v):
                if div_den is not None:
                    v.wait_ge(dnl.sem, 32)
                    v.reciprocal(out=srdn[:], in_=srdn[:]).then_inc(dnr.sem, 1)
                for i in range(NT):
                    if div_den is not None:
                        for kc in range(KCH):
                            gdb = i * KCH + kc
                            v.wait_ge(dbb.sem, gdb + 1)
                            v.tensor_tensor(out=sx[i % 2][:, kc, :], in0=sx[i % 2][:, kc, :],
                                            in1=ps_db[gdb % 2][:], op=ALU.mult
                                            ).then_inc(dvx.sem, 1)
                    for mc in range(MCH):
                        tot = i * MCH + mc
                        b = tot % 4
                        v.wait_ge(mm.sem, tot + 1)
                        if i >= 2:
                            v.wait_ge(st[i % 2].sem, 16 * MCH * (i // 2))
                        dst = so[i % 2][0:mr(mc), mc, :]
                        pss = ps[b][0:mr(mc), :]
                        if res_f32 is not None:
                            if bias is not None:
                                v.tensor_scalar(out=dst, in0=pss,
                                                scalar1=sB[:, mc:mc + 1], scalar2=None,
                                                op0=ALU.add).then_inc(vq.sem, 1)
                                vq.n += 1
                                v.wait_ge(vq.sem, vq.n)
                                v.tensor_tensor(out=dst, in0=dst, in1=sr[i % 2][:, mc, :],
                                                op=ALU.add).then_inc(ev.sem, 1)
                            else:
                                v.tensor_tensor(out=dst, in0=pss, in1=sr[i % 2][:, mc, :],
                                                op=ALU.add).then_inc(ev.sem, 1)
                        elif bias is not None and post_scale is not None:
                            v.tensor_scalar(out=dst, in0=pss, scalar1=sB[:, mc:mc + 1],
                                            scalar2=psc(mc), op0=ALU.add, op1=ALU.mult
                                            ).then_inc(ev.sem, 1)
                        elif post_scale is not None:
                            v.tensor_scalar(out=dst, in0=pss, scalar1=psc(mc),
                                            scalar2=None, op0=ALU.mult).then_inc(ev.sem, 1)
                        elif bias is not None:
                            v.tensor_scalar(out=dst, in0=pss, scalar1=sB[:, mc:mc + 1],
                                            scalar2=None, op0=ALU.add).then_inc(ev.sem, 1)
                        else:
                            v.tensor_copy(out=dst, in_=pss).then_inc(ev.sem, 1)

            @blk.gpsimd
            def _(g):
                for i in range(NT):
                    for mc in range(MCH):
                        g.wait_ge(ev.sem, i * MCH + mc + 1)
                        g.dma_start(
                            out=out_dram[mc * 128:mc * 128 + mr(mc), i * TT:(i + 1) * TT],
                            in_=so[i % 2][0:mr(mc), mc, :]).then_inc(st[i % 2].sem, 16)
                g.wait_ge(st[0].sem, 16 * MCH * ((NT + 1) // 2))
                g.wait_ge(st[1].sem, 16 * MCH * (NT // 2))
        phase_end(nc, pp)


def emit_vaug(nc, IN, tag, Wv_bf, x_bf, bias, out_dram, nh=16, hd=64):
    M = nh * hd
    CW = hd + 1
    NTOK = L // 128
    wide = hd >= 512
    HPH = 1 if wide else 512 // hd
    nhalf = M // 512
    with ExitStack() as ctx:
        pp = P(nc, ctx)
        sW = pp.sbuf(f"{tag}_W", [128, NCH, M], BF16)
        sbv = pp.sbuf(f"{tag}_bv", [1, M], F32)
        sbb = pp.sbuf(f"{tag}_bb", [128, M], F32)
        ones_r = pp.sbuf(f"{tag}_onr", [1, 128], F32)
        sx = [pp.sbuf(f"{tag}_x{b}", [128, NCH, 128], BF16) for b in range(2)]
        so = [pp.sbuf(f"{tag}_o{b}", [128, nh * CW], BF16) for b in range(2)]
        ps = [pp.psum(f"{tag}_p{b}", [128, 512], F32) for b in range(2)]
        psb = pp.psum(f"{tag}_pb", [128, 512], F32)
        cst = pp.ctr(f"{tag}_cs"); bb = pp.ctr(f"{tag}_bb")
        ld = [pp.ctr(f"{tag}_ld{b}") for b in range(2)]
        mm = pp.ctr(f"{tag}_mm"); ev = pp.ctr(f"{tag}_ev"); msk = pp.ctr(f"{tag}_mk")
        st = [pp.ctr(f"{tag}_st{b}") for b in range(2)]

        with nc.Block() as blk:
            @blk.sync
            def _(sy):
                sy.dma_start(out=sW[:], in_=Wv_bf.rearrange("(c p) m -> p c m", p=128)
                             ).then_inc(cst.sem, 16)
                sy.dma_start(out=sbv[:], in_=bias.rearrange("(a m) -> a m", a=1)
                             ).then_inc(cst.sem, 16)
                sy.dma_start(out=ones_r[:], in_=IN["ones_row"]).then_inc(cst.sem, 16)
                for i in range(NTOK):
                    if i >= 2:
                        sy.wait_ge(st[i % 2].sem, 16 * (i // 2))
                    sy.dma_start(out=sx[i % 2][:],
                                 in_=fm_view(x_bf, 0, NCH, i * 128, 128)
                                 ).then_inc(ld[i % 2].sem, 16)

            @blk.tensor
            def _(t):
                t.wait_ge(cst.sem, 48)
                for half in range(nhalf):
                    t.matmul(psb[:], lhsT=ones_r[:], rhs=sbv[:, half * 512:(half + 1) * 512],
                             start=True, stop=True).then_inc(bb.sem, 1)
                    t.wait_ge(bb.sem, 2 * half + 2)
                for i in range(NTOK):
                    t.wait_ge(ld[i % 2].sem, 16 * (i // 2 + 1))
                    for half in range(nhalf):
                        tot = nhalf * i + half
                        if tot >= 2:
                            t.wait_ge(ev.sem, tot - 1)
                        for kc in range(NCH):
                            inst = t.matmul(ps[tot % 2][:],
                                            lhsT=sx[i % 2][:, kc, :],
                                            rhs=sW[:, kc, half * 512:(half + 1) * 512],
                                            start=(kc == 0), stop=(kc == NCH - 1))
                        inst.then_inc(mm.sem, 1)

            @blk.vector
            def _(v):
                for half in range(nhalf):
                    v.wait_ge(bb.sem, 2 * half + 1)
                    v.tensor_copy(out=sbb[:, half * 512:(half + 1) * 512], in_=psb[:]
                                  ).then_inc(bb.sem, 1)
                for i in range(NTOK):
                    if i >= 2:
                        v.wait_ge(st[i % 2].sem, 16 * (i // 2))
                    v.memset(so[i % 2][:].rearrange("p (h c) -> p h c", c=CW)[:, :, hd:hd + 1],
                             1.0).then_inc(msk.sem, 1)
                    for half in range(nhalf):
                        tot = nhalf * i + half
                        v.wait_ge(mm.sem, tot + 1)
                        if wide:
                            out_sl = so[i % 2][:, half * 512:(half + 1) * 512]
                            in1_sl = sbb[:, half * 512:(half + 1) * 512]
                            v.tensor_tensor(out=out_sl, in0=ps[tot % 2][:], in1=in1_sl,
                                            op=ALU.add).then_inc(ev.sem, 1)
                        else:
                            v.tensor_tensor(
                                out=so[i % 2][:].rearrange("p (h c) -> p h c", c=CW)[:, half * HPH:(half + 1) * HPH, 0:hd],
                                in0=ps[tot % 2][:].rearrange("p (h c) -> p h c", c=hd),
                                in1=sbb[:, half * 512:(half + 1) * 512].rearrange("p (h c) -> p h c", c=hd),
                                op=ALU.add).then_inc(ev.sem, 1)

            @blk.gpsimd
            def _(g):
                for i in range(NTOK):
                    g.wait_ge(ev.sem, nhalf * (i + 1))
                    g.wait_ge(msk.sem, i + 1)
                    g.dma_start(out=out_dram[i * 128:(i + 1) * 128, :],
                                in_=so[i % 2][:]).then_inc(st[i % 2].sem, 16)
                g.wait_ge(st[0].sem, 16 * ((NTOK + 1) // 2))
                g.wait_ge(st[1].sem, 16 * (NTOK // 2))
        phase_end(nc, pp)


def emit_l1attn(nc, IN, tag, q_bf, k_bf, vaug_bf, out_dram, den_dram):
    NH, HD = 16, 64
    NG = 16
    with ExitStack() as ctx:
        pp = P(nc, ctx)
        ones_c = pp.sbuf(f"{tag}_onc", [128, 1], F32)
        ones_r = pp.sbuf(f"{tag}_onr", [1, 128], F32)
        sBn = pp.sbuf(f"{tag}_Bn", [128, NH], F32)
        cc = pp.ctr(f"{tag}_cc")

        c1s = ExitStack()
        p1s = P(nc, c1s)
        sqk = p1s.sbuf(f"{tag}_nx", [128, NCH, TT], BF16)
        ssq = p1s.sbuf(f"{tag}_nsq", [128, NCH, TT], F32)
        snr = p1s.sbuf(f"{tag}_nr", [1, 96], F32)
        ps_n = [p1s.psum(f"{tag}_pn{b}", [1, TT], F32) for b in range(2)]
        ps_B = p1s.psum(f"{tag}_pB", [128, NH], F32)
        l1 = pp.ctr(f"{tag}_l1"); a1 = pp.ctr(f"{tag}_a1")
        p1 = pp.ctr(f"{tag}_p1"); vq = pp.ctr(f"{tag}_vq"); b1 = pp.ctr(f"{tag}_b1")

        with nc.Block() as blk:
            @blk.sync
            def _(sy):
                sy.dma_start(out=ones_c[:], in_=IN["ones_row"]).then_inc(cc.sem, 16)
                sy.dma_start(out=ones_r[:], in_=IN["ones_row"]).then_inc(cc.sem, 16)
                for j in range(2 * NT):
                    src = [q_bf, k_bf][j // NT]
                    if j >= 1:
                        sy.wait_ge(a1.sem, j)
                    sy.dma_start(out=sqk[:], in_=fm_view(src, 0, NCH, (j % NT) * TT, TT)
                                 ).then_inc(l1.sem, 16)

            @blk.scalar
            def _(s):
                for j in range(2 * NT):
                    s.wait_ge(l1.sem, 16 * (j + 1))
                    if j >= 1:
                        s.wait_ge(p1.sem, j * NH)
                    s.activation(out=ssq[:], in_=sqk[:], func=AF.Square).then_inc(a1.sem, 1)
                s.wait_ge(b1.sem, 1)
                s.activation(out=snr[:, 48:64], in_=snr[:, 32:48], func=AF.Sqrt
                             ).then_inc(b1.sem, 1)

            @blk.tensor
            def _(t):
                t.wait_ge(cc.sem, 32)
                for j in range(2 * NT):
                    t.wait_ge(a1.sem, j + 1)
                    for h in range(NH):
                        g = j * NH + h
                        hc, ho = (h * HD) // 128, (h * HD) % 128
                        if g >= 2:
                            t.wait_ge(vq.sem, 2 * g - 1)
                        t.matmul(ps_n[g % 2][:], lhsT=ones_c[ho:ho + HD, :],
                                 rhs=ssq[ho:ho + HD, hc, :],
                                 start=True, stop=True).then_inc(p1.sem, 1)
                t.wait_ge(b1.sem, 3)
                t.matmul(ps_B[:], lhsT=ones_r[:], rhs=snr[:, 64:80],
                         start=True, stop=True).then_inc(b1.sem, 1)

            @blk.vector
            def _(v):
                def stepv(inst):
                    inst.then_inc(vq.sem, 1)
                    vq.n += 1
                    v.wait_ge(vq.sem, vq.n)
                stepv(v.memset(snr[:], 0.0))
                for j in range(2 * NT):
                    for h in range(NH):
                        g = j * NH + h
                        col = (j // NT) * NH + h
                        v.wait_ge(p1.sem, g + 1)
                        stepv(v.tensor_reduce(out=snr[:, 80:81], in_=ps_n[g % 2][:],
                                              axis=AX.X, op=ALU.max))
                        stepv(v.tensor_tensor(out=snr[:, col:col + 1], in0=snr[:, col:col + 1],
                                              in1=snr[:, 80:81], op=ALU.max))
                v.tensor_tensor(out=snr[:, 32:48], in0=snr[:, 0:16],
                                in1=snr[:, 16:32], op=ALU.mult).then_inc(b1.sem, 1)
                v.wait_ge(b1.sem, 2)
                v.tensor_scalar(out=snr[:, 64:80], in0=snr[:, 48:64],
                                scalar1=-1.0, scalar2=None, op0=ALU.mult).then_inc(b1.sem, 1)
                v.wait_ge(b1.sem, 4)
                v.tensor_copy(out=sBn[:], in_=ps_B[:]).then_inc(b1.sem, 1)

        nc.all_engine_barrier()
        c1s.close()

        sK = [pp.sbuf(f"{tag}_K{b}", [128, L], BF16) for b in range(3)]
        sV = [pp.sbuf(f"{tag}_V{b}", [128, 32, HD + 1], BF16) for b in range(3)]
        sq = [pp.sbuf(f"{tag}_q{b}", [128, TT], BF16) for b in range(2)]
        sP = [pp.sbuf(f"{tag}_P{b}", [128, 2, TT], BF16) for b in range(3)]
        sAo = [pp.sbuf(f"{tag}_A{b}", [64, TT], BF16) for b in range(2)]
        sDe = [pp.sbuf(f"{tag}_D{b}", [1, TT], F32) for b in range(2)]
        ps_s = [pp.psum(f"{tag}_ss{b}", [128, 2, TT], F32) for b in range(3)]
        ps_o = [pp.psum(f"{tag}_po{b}", [65, TT], F32) for b in range(2)]
        z1 = pp.ctr(f"{tag}_z1")
        kvld = [pp.ctr(f"{tag}_kv{b}") for b in range(3)]
        qld = [pp.ctr(f"{tag}_ql{b}") for b in range(2)]
        ms = pp.ctr(f"{tag}_ms"); ex = pp.ctr(f"{tag}_ex"); pv = pp.ctr(f"{tag}_pv")
        oc = pp.ctr(f"{tag}_oc")
        stq = [pp.ctr(f"{tag}_sq{b}") for b in range(2)]
        NU = NH * 8

        with nc.Block() as blk:
            @blk.vector
            def _(v):
                for b in range(3):
                    v.memset(sK[b][64:128, :], 0.0).then_inc(z1.sem, 1)
                v.memset(sq[0][64:128, :], 0.0).then_inc(z1.sem, 1)
                v.memset(sq[1][64:128, :], 0.0).then_inc(z1.sem, 1)
                for u in range(NU):
                    v.wait_ge(pv.sem, NG * (u + 1))
                    if u >= 2:
                        v.wait_ge(stq[u % 2].sem, 32 * (u // 2))
                    v.tensor_copy(out=sAo[u % 2][:], in_=ps_o[u % 2][0:64, :])
                    v.tensor_copy(out=sDe[u % 2][:], in_=ps_o[u % 2][64:65, :]
                                  ).then_inc(oc.sem, 1)

            @blk.sync
            def _(sy):
                sy.wait_ge(z1.sem, 5)
                for h in range(NH):
                    for qt in range(8):
                        u = h * 8 + qt
                        if u >= 2:
                            sy.wait_ge(stq[u % 2].sem, 32 * (u // 2))
                        sy.dma_start(out=sq[u % 2][0:64, :],
                                     in_=q_bf[h * 64:(h + 1) * 64, qt * TT:(qt + 1) * TT]
                                     ).then_inc(qld[u % 2].sem, 16)

            @blk.scalar
            def _(s):
                s.wait_ge(b1.sem, 5)
                for u in range(NU):
                    h = u // 8
                    for g in range(NG):
                        G = u * NG + g
                        s.wait_ge(ms.sem, G + 1)
                        if G >= 3:
                            s.wait_ge(pv.sem, G - 2)
                        s.activation(out=sP[G % 3][:], in_=ps_s[G % 3][:],
                                     func=AF.Exp, bias=sBn[:, h:h + 1], scale=1.0
                                     ).then_inc(ex.sem, 1)

            @blk.tensor
            def _(t):
                t.wait_ge(b1.sem, 5)
                for u in range(NU):
                    h, qt = u // 8, u % 8
                    t.wait_ge(kvld[h % 3].sem, 32 * (h // 3 + 1))
                    t.wait_ge(qld[u % 2].sem, 16 * (u // 2 + 1))
                    if u >= 2:
                        t.wait_ge(oc.sem, u - 1)
                    for g in range(NG):
                        G = u * NG + g
                        if G >= 3:
                            t.wait_ge(ex.sem, G - 2)
                        for j in range(2):
                            kt = 2 * g + j
                            inst = t.matmul(ps_s[G % 3][:, j, :],
                                            lhsT=sK[h % 3][:, kt * 128:(kt + 1) * 128],
                                            rhs=sq[u % 2][:],
                                            start=True, stop=True)
                        inst.then_inc(ms.sem, 1)
                        if g >= 1:
                            t.wait_ge(ex.sem, G)
                            for j in range(2):
                                kt = 2 * (g - 1) + j
                                inst = t.matmul(ps_o[u % 2][:], lhsT=sV[h % 3][:, kt, :],
                                                rhs=sP[(G - 1) % 3][:, j, :],
                                                start=(g - 1 == 0 and j == 0), stop=False)
                            inst.then_inc(pv.sem, 1)
                    Gl = u * NG + NG - 1
                    t.wait_ge(ex.sem, Gl + 1)
                    for j in range(2):
                        kt = 2 * (NG - 1) + j
                        inst = t.matmul(ps_o[u % 2][:], lhsT=sV[h % 3][:, NG * 2 - 2 + j, :],
                                        rhs=sP[Gl % 3][:, j, :],
                                        start=False, stop=(j == 1))
                    inst.then_inc(pv.sem, 1)

            @blk.gpsimd
            def _(g_e):
                def ldkv(h):
                    g_e.dma_start(out=sK[h % 3][0:64, :], in_=k_bf[h * 64:(h + 1) * 64, :]
                                  ).then_inc(kvld[h % 3].sem, 16)
                    g_e.dma_start(out=sV[h % 3][:],
                                  in_=vaug_bf[:, h * 65:(h + 1) * 65].rearrange(
                                      "(kt p) c -> p kt c", p=128)
                                  ).then_inc(kvld[h % 3].sem, 16)
                g_e.wait_ge(z1.sem, 5)
                ldkv(0); ldkv(1)
                for u in range(NU):
                    h, qt = u // 8, u % 8
                    if qt == 0 and h + 2 < NH:
                        if h >= 1:
                            g_e.wait_ge(oc.sem, 8 * h)
                        ldkv(h + 2)
                    g_e.wait_ge(oc.sem, u + 1)
                    g_e.dma_start(out=out_dram[h * 64:(h + 1) * 64, qt * TT:(qt + 1) * TT],
                                  in_=sAo[u % 2][:]).then_inc(stq[u % 2].sem, 16)
                    g_e.dma_start(out=den_dram[h:h + 1, qt * TT:(qt + 1) * TT],
                                  in_=sDe[u % 2][:]).then_inc(stq[u % 2].sem, 16)
                g_e.wait_ge(stq[0].sem, 32 * (NU // 2))
                g_e.wait_ge(stq[1].sem, 32 * (NU // 2))
        phase_end(nc, pp)


def emit_router_masks(nc, IN, tag, keys_dram, emask_dram, idx_dram):
    CAP = MOE_CAP
    NTI = L // 128
    with ExitStack() as ctx:
        pp = P(nc, ctx)
        sk = pp.sbuf(f"{tag}_k", [1, L], F32)
        si = pp.sbuf(f"{tag}_i", [1, L], I32)
        st_ = pp.sbuf(f"{tag}_t", [1, L], F32)
        sl = pp.sbuf(f"{tag}_l", [1, L], F32)
        sm = pp.sbuf(f"{tag}_m", [1, 4 * L], F32)
        semk = pp.sbuf(f"{tag}_em", [4, L], F32)
        s_idf = pp.sbuf(f"{tag}_idf", [4, 4], F32)
        sCsel = pp.sbuf(f"{tag}_cs", [128, 32, 32], F32)
        sRsel = pp.sbuf(f"{tag}_rs", [32, 32, 128], F32)
        sStriu = pp.sbuf(f"{tag}_su", [34, 32], F32)
        sTriu = pp.sbuf(f"{tag}_tu", [128, 128], F32)
        sIotaT = pp.sbuf(f"{tag}_it", [128, 32], I32)
        sOh = pp.sbuf(f"{tag}_oh", [128, 32, 4], F32)
        sCum = pp.sbuf(f"{tag}_cu", [128, 32, 4], F32)
        sS34 = pp.sbuf(f"{tag}_s34", [34, 4], F32)
        sOff = pp.sbuf(f"{tag}_of", [32, 4], F32)
        sT4 = pp.sbuf(f"{tag}_t4", [128, 4], F32)
        sSlotF = pp.sbuf(f"{tag}_sf", [128, 1], F32)
        sSlotI = pp.sbuf(f"{tag}_si", [128, 32], I32)
        ps_a = pp.psum(f"{tag}_pa", [128, 4], F32)
        ps_c = pp.psum(f"{tag}_pc", [128, 4], F32)
        ps_S = pp.psum(f"{tag}_pS", [32, 4], F32)
        ps_of = pp.psum(f"{tag}_pf", [32, 4], F32)
        ps_b2 = pp.psum(f"{tag}_pb", [128, 4], F32)
        l2 = pp.ctr(f"{tag}_l2"); vq = pp.ctr(f"{tag}_vq"); dn = pp.ctr(f"{tag}_dn")
        dm = pp.ctr(f"{tag}_dm"); cl = pp.ctr(f"{tag}_cl"); eml = pp.ctr(f"{tag}_el")
        pa = pp.ctr(f"{tag}_pac"); ohc = pp.ctr(f"{tag}_ohc")
        pc = pp.ctr(f"{tag}_pcc"); cmc = pp.ctr(f"{tag}_cmc")
        ac = pp.ctr(f"{tag}_ac"); s34 = pp.ctr(f"{tag}_s34c"); of2 = pp.ctr(f"{tag}_of2")
        pb2 = pp.ctr(f"{tag}_pb2"); s2a = pp.ctr(f"{tag}_s2a"); slc = pp.ctr(f"{tag}_slc")
        pf = pp.ctr(f"{tag}_pf2"); sc2 = pp.ctr(f"{tag}_sc2")

        with nc.Block() as blk:
            @blk.sync
            def _(sy):
                sy.dma_start(out=sk[:], in_=keys_dram).then_inc(l2.sem, 16)
                sy.dma_start(out=s_idf[:], in_=IN["ident_f32"][0:4, 0:4]).then_inc(cl.sem, 16)
                sy.dma_start(out=sCsel[:],
                             in_=IN["colsel"].rearrange("p (a q) -> p a q", q=32)
                             ).then_inc(cl.sem, 16)
                sy.dma_start(out=sRsel[:],
                             in_=IN["rowsel"].rearrange("p (a q) -> p a q", q=128)
                             ).then_inc(cl.sem, 16)
                sy.dma_start(out=sStriu[:], in_=IN["striu34"]).then_inc(cl.sem, 16)
                sy.dma_start(out=sTriu[:], in_=IN["triuI"]).then_inc(cl.sem, 16)
                sy.dma_start(out=sIotaT[:], in_=IN["iota_tok"]).then_inc(cl.sem, 16)
                sy.dma_start(out=sS34[32:34, :], in_=IN["ce4m1"]).then_inc(cl.sem, 16)
                sy.wait_ge(dm.sem, 16)
                sy.dma_start(out=semk[:], in_=emask_dram).then_inc(eml.sem, 16)

            @blk.vector
            def _(v):
                def stepv(inst):
                    inst.then_inc(vq.sem, 1)
                    vq.n += 1
                    v.wait_ge(vq.sem, vq.n)
                v.wait_ge(l2.sem, 16)
                stepv(v.tensor_copy(out=si[:], in_=sk[:]))
                stepv(v.tensor_copy(out=st_[:], in_=si[:]))
                stepv(v.tensor_tensor(out=sl[:], in0=sk[:], in1=st_[:], op=ALU.is_lt))
                stepv(v.tensor_tensor(out=st_[:], in0=st_[:], in1=sl[:], op=ALU.subtract))
                stepv(v.tensor_copy(out=si[:], in_=st_[:]))
                stepv(v.tensor_scalar(out=si[:], in0=si[:], scalar1=3, scalar2=None,
                                      op0=ALU.bitwise_and))
                stepv(v.tensor_copy(out=st_[:], in_=si[:]))
                for e in range(4):
                    inst = v.tensor_scalar(out=sm[:, e * L:(e + 1) * L], in0=st_[:],
                                           scalar1=float(e), scalar2=None, op0=ALU.is_equal)
                inst.then_inc(dn.sem, 1)
                for ti in range(NTI):
                    v.wait_ge(pa.sem, ti + 1)
                    v.tensor_copy(out=sOh[:, ti, :], in_=ps_a[:]).then_inc(ohc.sem, 1)
                    v.wait_ge(pc.sem, ti + 1)
                    v.tensor_copy(out=sCum[:, ti, :], in_=ps_c[:]).then_inc(cmc.sem, 1)
                v.wait_ge(ac.sem, 1)
                v.tensor_copy(out=sS34[0:32, :], in_=ps_S[:]).then_inc(s34.sem, 1)
                v.wait_ge(of2.sem, 1)
                v.tensor_copy(out=sOff[:], in_=ps_of[:]).then_inc(of2.sem, 1)
                for ti in range(NTI):
                    v.wait_ge(pb2.sem, ti + 1)
                    v.tensor_tensor(out=sT4[:], in0=ps_b2[:], in1=sCum[:, ti, :],
                                    op=ALU.add).then_inc(s2a.sem, 1)
                    v.wait_ge(s2a.sem, ti + 1)
                    stepv(v.tensor_tensor(out=sT4[:], in0=sT4[:], in1=sOh[:, ti, :],
                                          op=ALU.mult))
                    stepv(v.tensor_reduce(out=sSlotF[:], in_=sT4[:], axis=AX.X, op=ALU.add))
                    v.tensor_copy(out=sSlotI[:, ti:ti + 1], in_=sSlotF[:]
                                  ).then_inc(slc.sem, 1)

            @blk.tensor
            def _(t):
                t.wait_ge(cl.sem, 112)
                t.wait_ge(eml.sem, 16)
                for ti in range(NTI):
                    if ti >= 1:
                        t.wait_ge(ohc.sem, ti)
                    t.transpose(ps_a[:], in_=semk[:, ti * 128:(ti + 1) * 128],
                                identity=s_idf[:]).then_inc(pa.sem, 1)
                    t.wait_ge(ohc.sem, ti + 1)
                    if ti >= 1:
                        t.wait_ge(cmc.sem, ti)
                    t.matmul(ps_c[:], lhsT=sTriu[:], rhs=sOh[:, ti, :],
                             start=True, stop=True).then_inc(pc.sem, 1)
                    inst = t.matmul(ps_S[:], lhsT=sCsel[:, ti, :], rhs=sOh[:, ti, :],
                                    start=(ti == 0), stop=(ti == NTI - 1))
                inst.then_inc(ac.sem, 1)
                t.wait_ge(s34.sem, 1)
                t.matmul(ps_of[:], lhsT=sStriu[:], rhs=sS34[:],
                         start=True, stop=True).then_inc(of2.sem, 1)
                t.wait_ge(of2.sem, 2)
                for ti in range(NTI):
                    if ti >= 1:
                        t.wait_ge(s2a.sem, ti)
                    t.matmul(ps_b2[:], lhsT=sRsel[:, ti, :], rhs=sOff[:],
                             start=True, stop=True).then_inc(pb2.sem, 1)

            @blk.gpsimd
            def _(g):
                g.wait_ge(dn.sem, 1)
                g.dma_start(out=emask_dram.rearrange("e t -> (e t)").rearrange(
                    "(a n) -> a n", a=1), in_=sm[:]).then_inc(dm.sem, 16)
                g.dma_start(out=idx_dram.rearrange("n b -> b (n)"),
                            in_=IN["oobfill"]).then_inc(pf.sem, 16)
                g.wait_ge(pf.sem, 16)
                for ti in range(NTI):
                    g.wait_ge(slc.sem, ti + 1)
                    g.wait_ge(sc2.sem, 16 * ti)
                    g.indirect_dma_start(
                        out=idx_dram,
                        out_offset=IndirectOffsetOnAxis(sSlotI[:, ti:ti + 1], 0),
                        in_=sIotaT[:, ti:ti + 1], in_offset=None,
                    ).then_inc(sc2.sem, 16)
                g.wait_ge(sc2.sem, 16 * NTI)
        phase_end(nc, pp)


def emit_moe(nc, IN, tag, x_bf, emask_dram, W1_bf, b1_f, W2_bf, b2_f, ffacc_dram):
    FF = 4096
    FCH = FF // 128
    with ExitStack() as ctx:
        pp = P(nc, ctx)
        ones_r = pp.sbuf(f"{tag}_onr", [1, 128], F32)
        sW1 = pp.sbuf(f"{tag}_W1", [128, NCH, FF], BF16)
        sW2 = pp.sbuf(f"{tag}_W2", [128, FCH, D], BF16)
        sb1 = pp.sbuf(f"{tag}_b1", [128, FCH], F32)
        sb2 = pp.sbuf(f"{tag}_b2", [128, NCH], F32)
        smr = [pp.sbuf(f"{tag}_mr{b}", [1, TT], F32) for b in range(2)]
        smb = pp.sbuf(f"{tag}_mb", [128, TT], F32)
        sx = [pp.sbuf(f"{tag}_x{b}", [128, NCH, TT], BF16) for b in range(2)]
        sh = pp.sbuf(f"{tag}_h", [128, FCH, TT], BF16)
        syo = [pp.sbuf(f"{tag}_y{b}", [128, TT], F32) for b in range(4)]
        ps1 = [pp.psum(f"{tag}_pa{b}", [128, TT], F32) for b in range(4)]
        ps2 = [pp.psum(f"{tag}_pc{b}", [128, TT], F32) for b in range(2)]
        ps_m = pp.psum(f"{tag}_pm", [128, TT], F32)
        cc = pp.ctr(f"{tag}_cc")
        wl = pp.ctr(f"{tag}_wl")
        xl = [pp.ctr(f"{tag}_xl{b}") for b in range(2)]
        m1 = pp.ctr(f"{tag}_m1"); hq = pp.ctr(f"{tag}_hq")
        m2 = pp.ctr(f"{tag}_m2"); ev = pp.ctr(f"{tag}_ev")
        mb = pp.ctr(f"{tag}_mb"); mv = pp.ctr(f"{tag}_mv")
        st = [pp.ctr(f"{tag}_st{b}") for b in range(4)]
        ml = [pp.ctr(f"{tag}_ml{b}") for b in range(2)]

        with nc.Block() as blk:
            @blk.sync
            def _(sy):
                sy.dma_start(out=ones_r[:], in_=IN["ones_row"]).then_inc(cc.sem, 16)
                for e in range(4):
                    if e >= 1:
                        sy.wait_ge(m1.sem, FCH * NT * e)
                        sy.wait_ge(m2.sem, NCH * NT * e)
                        sy.wait_ge(ev.sem, NCH * NT * e)
                        sy.wait_ge(hq.sem, FCH * NT * e)
                    sy.dma_start(out=sW1[:],
                                 in_=W1_bf[e * D:(e + 1) * D, :].rearrange("(c p) m -> p c m", p=128)
                                 ).then_inc(wl.sem, 16)
                    sy.dma_start(out=sW2[:],
                                 in_=W2_bf[e * FF:(e + 1) * FF, :].rearrange("(c p) m -> p c m", p=128)
                                 ).then_inc(wl.sem, 16)
                    sy.dma_start(out=sb1[:],
                                 in_=b1_f[e * FF:(e + 1) * FF].rearrange("(c p) -> p c", p=128),
                                 allow_slow_non_contiguous=True).then_inc(wl.sem, 16)
                    sy.dma_start(out=sb2[:],
                                 in_=b2_f[e * D:(e + 1) * D].rearrange("(c p) -> p c", p=128),
                                 allow_slow_non_contiguous=True).then_inc(wl.sem, 16)
                    for i in range(NT):
                        u = e * NT + i
                        if u >= 2:
                            sy.wait_ge(m1.sem, FCH * (u - 1))
                            sy.wait_ge(mb.sem, u - 1)

                        sy.dma_start(out=smr[u % 2][:],
                                     in_=emask_dram[e:e + 1, i * TT:(i + 1) * TT]
                                     ).then_inc(ml[u % 2].sem, 16)
                        sy.dma_start(out=sx[u % 2][:], in_=fm_view(x_bf, 0, NCH, i * TT, TT)
                                     ).then_inc(xl[u % 2].sem, 16)

            @blk.tensor
            def _(t):
                t.wait_ge(cc.sem, 16)
                for e in range(4):
                    t.wait_ge(wl.sem, 64 * (e + 1))
                    for i in range(NT):
                        u = e * NT + i
                        t.wait_ge(xl[u % 2].sem, 16 * (u // 2 + 1))
                        t.wait_ge(ml[u % 2].sem, 16 * (u // 2 + 1))
                        if u >= 1:
                            t.wait_ge(mv.sem, u)
                        t.matmul(ps_m[:], lhsT=ones_r[:], rhs=smr[u % 2][:],
                                 start=True, stop=True).then_inc(mb.sem, 1)
                        for fc in range(FCH):
                            b = fc % 4
                            tot1 = u * FCH + fc
                            if tot1 >= 4:
                                t.wait_ge(hq.sem, tot1 - 3)
                            for kc in range(NCH):
                                inst = t.matmul(ps1[b][:],
                                                lhsT=sW1[:, kc, fc * 128:(fc + 1) * 128],
                                                rhs=sx[u % 2][:, kc, :],
                                                start=(kc == 0), stop=(kc == NCH - 1))
                            inst.then_inc(m1.sem, 1)
                        t.wait_ge(hq.sem, (u + 1) * FCH)
                        for mc in range(NCH):
                            b2b = mc % 2
                            tot2 = u * NCH + mc
                            if tot2 >= 2:
                                t.wait_ge(ev.sem, tot2 - 1)
                            for kc in range(FCH):
                                inst = t.matmul(ps2[b2b][:],
                                                lhsT=sW2[:, kc, mc * 128:(mc + 1) * 128],
                                                rhs=sh[:, kc, :],
                                                start=(kc == 0), stop=(kc == FCH - 1))
                            inst.then_inc(m2.sem, 1)

            @blk.scalar
            def _(s):
                for e in range(4):
                    for i in range(NT):
                        u = e * NT + i
                        if u >= 1:
                            s.wait_ge(m2.sem, NCH * u)
                        for fc in range(FCH):
                            b = fc % 4
                            s.wait_ge(m1.sem, u * FCH + fc + 1)
                            s.activation(out=sh[:, fc, :], in_=ps1[b][:], func=AF.Relu,
                                         bias=sb1[:, fc:fc + 1], scale=1.0).then_inc(hq.sem, 1)

            @blk.vector
            def _(v):
                for e in range(4):
                    for i in range(NT):
                        u = e * NT + i
                        v.wait_ge(mb.sem, u + 1)
                        if u >= 1:
                            v.wait_ge(ev.sem, NCH * u)
                        v.tensor_copy(out=smb[:], in_=ps_m[:]).then_inc(mv.sem, 1)
                        for mc in range(NCH):
                            b2b = mc % 2
                            tot2 = u * NCH + mc
                            v.wait_ge(m2.sem, tot2 + 1)
                            if tot2 >= 4:
                                v.wait_ge(st[tot2 % 4].sem, 16 * (tot2 // 4))
                            if mc == 0:
                                v.wait_ge(mv.sem, u + 1)
                            v.scalar_tensor_tensor(out=syo[tot2 % 4][:], in0=ps2[b2b][:],
                                                   scalar=sb2[:, mc:mc + 1], in1=smb[:],
                                                   op0=ALU.add, op1=ALU.mult).then_inc(ev.sem, 1)

            @blk.gpsimd
            def _(g):
                for e in range(4):
                    for i in range(NT):
                        u = e * NT + i
                        for mc in range(NCH):
                            tot2 = u * NCH + mc
                            g.wait_ge(ev.sem, tot2 + 1)
                            g.dma_start(
                                out=ffacc_dram[mc * 128:(mc + 1) * 128, i * TT:(i + 1) * TT],
                                in_=syo[tot2 % 4][:], accum_op=ALU.add
                                ).then_inc(st[tot2 % 4].sem, 16)
                for b in range(4):
                    g.wait_ge(st[b].sem, 16 * (4 * NT * NCH // 4))
        phase_end(nc, pp)



def emit_x2tm(nc, IN, tag, src_fm_f32, dst_tm_bf):
    with ExitStack() as ctx:
        pp = P(nc, ctx)
        szr = pp.sbuf(f"{tag}_zr", [128, D], BF16)
        zq = pp.ctr(f"{tag}_zq"); zs = pp.ctr(f"{tag}_zs")
        s_id = pp.sbuf(f"{tag}_id", [128, 128], F32)
        sxi = [pp.sbuf(f"{tag}_x{b}", [128, NCH, 128], F32) for b in range(2)]
        sob = [pp.sbuf(f"{tag}_o{b}", [128, NCH, 128], BF16) for b in range(2)]
        ps_t = [pp.psum(f"{tag}_p{b}", [128, 128], F32) for b in range(2)]
        cc = pp.ctr(f"{tag}_cc")
        ld = [pp.ctr(f"{tag}_ld{b}") for b in range(2)]
        tp = pp.ctr(f"{tag}_tp"); cp = pp.ctr(f"{tag}_cp")
        st = [pp.ctr(f"{tag}_st{b}") for b in range(2)]
        NTI = L // 128

        with nc.Block() as blk:
            @blk.sync
            def _(sy):
                sy.dma_start(out=s_id[:], in_=IN["ident_f32"]).then_inc(cc.sem, 16)
                for ti in range(NTI):
                    if ti >= 2:
                        sy.wait_ge(tp.sem, NCH * (ti - 1))
                    sy.dma_start(out=sxi[ti % 2][:],
                                 in_=fm_view(src_fm_f32, 0, NCH, ti * 128, 128)
                                 ).then_inc(ld[ti % 2].sem, 16)

            @blk.tensor
            def _(t):
                t.wait_ge(cc.sem, 16)
                for ti in range(NTI):
                    t.wait_ge(ld[ti % 2].sem, 16 * (ti // 2 + 1))
                    for c in range(NCH):
                        gc = ti * NCH + c
                        if gc >= 2:
                            t.wait_ge(cp.sem, gc - 1)
                        t.transpose(ps_t[gc % 2][:], in_=sxi[ti % 2][:, c, :],
                                    identity=s_id[:]).then_inc(tp.sem, 1)

            @blk.vector
            def _(v):
                v.memset(szr[:], 0.0).then_inc(zq.sem, 1)
                for ti in range(NTI):
                    if ti >= 2:
                        v.wait_ge(st[ti % 2].sem, 16 * (ti // 2))
                    for c in range(NCH):
                        gc = ti * NCH + c
                        v.wait_ge(tp.sem, gc + 1)
                        v.tensor_copy(out=sob[ti % 2][:, c, :], in_=ps_t[gc % 2][:]
                                      ).then_inc(cp.sem, 1)

            @blk.gpsimd
            def _(g):
                g.wait_ge(zq.sem, 1)
                g.dma_start(out=dst_tm_bf[L:L + 128, :], in_=szr[:]).then_inc(zs.sem, 16)
                for ti in range(NTI):
                    g.wait_ge(cp.sem, NCH * (ti + 1))
                    g.dma_start(out=dst_tm_bf[ti * 128:(ti + 1) * 128, :],
                                in_=sob[ti % 2][:].rearrange("p c q -> p (c q)")
                                ).then_inc(st[ti % 2].sem, 16)
                g.wait_ge(st[0].sem, 16 * (NTI // 2))
                g.wait_ge(st[1].sem, 16 * (NTI // 2))
                g.wait_ge(zs.sem, 16)
        phase_end(nc, pp)


MOE_CAP = 1536


def emit_moe_routed(nc, IN, tag, xtm_bf, idx_dram, W1_bf, b1_f, W2_bf, b2_f, ff_tm, ident_bf):
    FF = 4096
    FCH = FF // 128
    TPE = MOE_CAP // TT
    NGT = 4 * TPE
    with ExitStack() as ctx:
        pp = P(nc, ctx)
        s_id = pp.sbuf(f"{tag}_id", [128, 128], BF16)
        sW1 = pp.sbuf(f"{tag}_W1", [128, NCH, FF], BF16)
        sW2 = pp.sbuf(f"{tag}_W2", [128, FCH, D], BF16)
        sb1 = pp.sbuf(f"{tag}_b1", [128, FCH], F32)
        sb2 = pp.sbuf(f"{tag}_b2", [128, NCH], F32)
        sIg = pp.sbuf(f"{tag}_ig", [128, NGT * 4], I32)
        sGx = [pp.sbuf(f"{tag}_g{b}", [128, 4, D], BF16) for b in range(2)]
        sXc = pp.sbuf(f"{tag}_xc", [128, NCH, TT], BF16)
        sh = pp.sbuf(f"{tag}_h", [128, FCH, TT], BF16)
        syo = [pp.sbuf(f"{tag}_y{b}", [128, TT], BF16) for b in range(2)]
        sYt = [pp.sbuf(f"{tag}_t{b}", [128, 4, D], BF16) for b in range(2)]
        ps1 = [pp.psum(f"{tag}_pa{b}", [128, TT], F32) for b in range(2)]
        ps2 = [pp.psum(f"{tag}_pc{b}", [128, TT], F32) for b in range(2)]
        ps_ti = [pp.psum(f"{tag}_pi{b}", [128, 128], BF16) for b in range(2)]
        ps_to = [pp.psum(f"{tag}_po{b}", [128, 128], BF16) for b in range(2)]
        cc = pp.ctr(f"{tag}_cc"); wl = pp.ctr(f"{tag}_wl"); zz = pp.ctr(f"{tag}_zz")
        ga = [pp.ctr(f"{tag}_ga{b}") for b in range(2)]
        tit = pp.ctr(f"{tag}_tt"); tic = pp.ctr(f"{tag}_tc")
        m1 = pp.ctr(f"{tag}_m1"); hq = pp.ctr(f"{tag}_hq"); m2 = pp.ctr(f"{tag}_m2")
        ev2 = pp.ctr(f"{tag}_ev"); tot = pp.ctr(f"{tag}_to"); toc = pp.ctr(f"{tag}_oc")
        sc = [pp.ctr(f"{tag}_sc{b}") for b in range(2)]

        with nc.Block() as blk:
            @blk.sync
            def _(sy):
                sy.dma_start(out=s_id[:], in_=ident_bf).then_inc(cc.sem, 16)
                sy.dma_start(out=sIg[:],
                             in_=idx_dram.rearrange("(g p) b -> p (g b)", p=128),
                             allow_slow_non_contiguous=True).then_inc(cc.sem, 16)
                for e in range(4):
                    if e >= 1:
                        sy.wait_ge(m1.sem, FCH * TPE * e)
                        sy.wait_ge(hq.sem, FCH * TPE * e)
                        sy.wait_ge(m2.sem, NCH * TPE * e)
                        sy.wait_ge(ev2.sem, NCH * TPE * e)
                    sy.dma_start(out=sW1[:],
                                 in_=W1_bf[e * D:(e + 1) * D, :].rearrange(
                                     "(c p) m -> p c m", p=128)).then_inc(wl.sem, 16)
                    sy.dma_start(out=sW2[:],
                                 in_=W2_bf[e * FF:(e + 1) * FF, :].rearrange(
                                     "(c p) m -> p c m", p=128)).then_inc(wl.sem, 16)
                    sy.dma_start(out=sb1[:],
                                 in_=b1_f[e * FF:(e + 1) * FF].rearrange("(c p) -> p c", p=128),
                                 allow_slow_non_contiguous=True).then_inc(wl.sem, 16)
                    sy.dma_start(out=sb2[:],
                                 in_=b2_f[e * D:(e + 1) * D].rearrange("(c p) -> p c", p=128),
                                 allow_slow_non_contiguous=True).then_inc(wl.sem, 16)

            @blk.gpsimd
            def _(g):
                def scat(k):
                    g.wait_ge(toc.sem, 32 * (k + 1))
                    for q in range(4):
                        g.wait_ge(sc[0].sem, scn[0] * 16)
                        g.wait_ge(sc[1].sem, scn[1] * 16)
                        g.indirect_dma_start(
                            out=ff_tm,
                            out_offset=IndirectOffsetOnAxis(sIg[:, k * 4 + q:k * 4 + q + 1], 0),
                            in_=sYt[k % 2][:, q, :], in_offset=None,
                        ).then_inc(sc[k % 2].sem, 16)
                        scn[k % 2] += 1
                g.wait_ge(cc.sem, 32)
                g.wait_ge(zz.sem, 2)
                scn = [0, 0]
                for gt in range(NGT):
                    if gt >= 2:
                        g.wait_ge(tit.sem, 32 * (gt - 1))
                    for q in range(4):
                        g.indirect_dma_start(
                            out=sGx[gt % 2][:, q, :], out_offset=None,
                            in_=xtm_bf,
                            in_offset=IndirectOffsetOnAxis(sIg[:, gt * 4 + q:gt * 4 + q + 1], 0),
                        ).then_inc(ga[gt % 2].sem, 16)
                    if gt >= 1:
                        scat(gt - 1)
                scat(NGT - 1)
                g.wait_ge(sc[0].sem, 16 * 4 * (NGT // 2))
                g.wait_ge(sc[1].sem, 16 * 4 * (NGT // 2))

            @blk.tensor
            def _(t):
                t.wait_ge(cc.sem, 32)

                def tin(gt):
                    t.wait_ge(ga[gt % 2].sem, 64 * (gt // 2 + 1))
                    for b32 in range(32):
                        q, kc = b32 // 8, b32 % 8
                        tb = gt * 32 + b32
                        if tb >= 2:
                            t.wait_ge(tic.sem, tb - 1)
                        t.transpose(ps_ti[tb % 2][:],
                                    in_=sGx[gt % 2][:, q, kc * 128:(kc + 1) * 128],
                                    identity=s_id[:]).then_inc(tit.sem, 1)
                tin(0)
                for gt in range(NGT):
                    e = gt // TPE
                    t.wait_ge(wl.sem, 64 * (e + 1))
                    t.wait_ge(tic.sem, 32 * (gt + 1))
                    for fc in range(FCH):
                        tot1 = gt * FCH + fc
                        if tot1 >= 2:
                            t.wait_ge(hq.sem, tot1 - 1)
                        for kc in range(NCH):
                            inst = t.matmul(ps1[fc % 2][:],
                                            lhsT=sW1[:, kc, fc * 128:(fc + 1) * 128],
                                            rhs=sXc[:, kc, :],
                                            start=(kc == 0), stop=(kc == NCH - 1))
                        inst.then_inc(m1.sem, 1)
                    if gt + 1 < NGT:
                        tin(gt + 1)
                    t.wait_ge(hq.sem, FCH * (gt + 1))

                    def tout(mc):
                        t.wait_ge(ev2.sem, gt * NCH + mc + 1)
                        for q in range(4):
                            tb2 = gt * 32 + mc * 4 + q
                            if tb2 >= 2:
                                t.wait_ge(toc.sem, tb2 - 1)
                            t.transpose(ps_to[tb2 % 2][:],
                                        in_=syo[mc % 2][:, q * 128:(q + 1) * 128],
                                        identity=s_id[:]).then_inc(tot.sem, 1)
                    for mc in range(NCH):
                        tot2 = gt * NCH + mc
                        if tot2 >= 2:
                            t.wait_ge(ev2.sem, tot2 - 1)
                        for kc in range(FCH):
                            inst = t.matmul(ps2[mc % 2][:],
                                            lhsT=sW2[:, kc, mc * 128:(mc + 1) * 128],
                                            rhs=sh[:, kc, :],
                                            start=(kc == 0), stop=(kc == FCH - 1))
                        inst.then_inc(m2.sem, 1)
                        if mc >= 1:
                            tout(mc - 1)
                    tout(NCH - 1)

            @blk.scalar
            def _(s):
                for gt in range(NGT):
                    e = gt // TPE
                    for fc in range(FCH):
                        tot1 = gt * FCH + fc
                        if fc == 0:
                            s.wait_ge(wl.sem, 64 * (e + 1))
                            if gt >= 1:
                                s.wait_ge(m2.sem, NCH * gt)
                        s.wait_ge(m1.sem, tot1 + 1)
                        s.activation(out=sh[:, fc, :], in_=ps1[fc % 2][:], func=AF.Relu,
                                     bias=sb1[:, fc:fc + 1], scale=1.0).then_inc(hq.sem, 1)

            @blk.vector
            def _(v):
                v.memset(sGx[0][:], 0.0).then_inc(zz.sem, 1)
                v.memset(sGx[1][:], 0.0).then_inc(zz.sem, 1)

                def tinc(gt):
                    for b32 in range(32):
                        q, kc = b32 // 8, b32 % 8
                        tb = gt * 32 + b32
                        v.wait_ge(tit.sem, tb + 1)
                        v.tensor_copy(out=sXc[:, kc, q * 128:(q + 1) * 128],
                                      in_=ps_ti[tb % 2][:]).then_inc(tic.sem, 1)
                tinc(0)
                for gt in range(NGT):
                    if gt + 1 < NGT:
                        tinc(gt + 1)

                    def toutc(mc):
                        for q in range(4):
                            tb2 = gt * 32 + mc * 4 + q
                            v.wait_ge(tot.sem, tb2 + 1)
                            if mc == 0 and q == 0 and gt >= 2:
                                v.wait_ge(sc[gt % 2].sem, 64 * (gt // 2))
                            v.tensor_copy(out=sYt[gt % 2][:, q, mc * 128:(mc + 1) * 128],
                                          in_=ps_to[tb2 % 2][:]).then_inc(toc.sem, 1)
                    for mc in range(NCH):
                        tot2 = gt * NCH + mc
                        v.wait_ge(m2.sem, tot2 + 1)
                        if tot2 >= 2:
                            v.wait_ge(tot.sem, 4 * (tot2 - 1))
                        v.tensor_scalar(out=syo[mc % 2][:], in0=ps2[mc % 2][:],
                                        scalar1=sb2[:, mc:mc + 1], scalar2=None,
                                        op0=ALU.add).then_inc(ev2.sem, 1)
                        if mc >= 1:
                            toutc(mc - 1)
                    toutc(NCH - 1)
        phase_end(nc, pp)


def emit_l1ffn_epi(nc, IN, tag, xtm_bf, ff_tm, gamma, beta, xres_src, xres_dst, ident_bf):
    NTI = L // 128
    with ExitStack() as ctx:
        pp = P(nc, ctx)
        s_id = pp.sbuf(f"{tag}_id", [128, 128], F32)
        sgr = pp.sbuf(f"{tag}_gr", [1, D], F32)
        sbr = pp.sbuf(f"{tag}_br", [1, D], F32)
        ones_c = pp.sbuf(f"{tag}_on", [1, 128], F32)
        sgb = pp.sbuf(f"{tag}_gb", [128, D], F32)
        sbb = pp.sbuf(f"{tag}_bb", [128, D], F32)
        sff = [pp.sbuf(f"{tag}_f{b}", [128, D], BF16) for b in range(2)]
        sxm = [pp.sbuf(f"{tag}_m{b}", [128, D], BF16) for b in range(2)]
        sxr = [pp.sbuf(f"{tag}_r{b}", [128, NCH, 128], F32) for b in range(2)]
        sz = pp.sbuf(f"{tag}_z", [128, D], F32)
        ssq = pp.sbuf(f"{tag}_sq", [128, D], F32)
        sst = pp.sbuf(f"{tag}_st", [128, 8], F32)
        szb = pp.sbuf(f"{tag}_zb", [128, D], F32)
        sy_ = [pp.sbuf(f"{tag}_s{b}", [128, NCH, 128], F32) for b in range(2)]
        ps_g = pp.psum(f"{tag}_pg", [128, TT], F32)
        ps_t = [pp.psum(f"{tag}_pt{b}", [128, 128], F32) for b in range(2)]
        cc = pp.ctr(f"{tag}_cc"); gb = pp.ctr(f"{tag}_gb")
        ld = [pp.ctr(f"{tag}_ld{b}") for b in range(2)]
        vq = pp.ctr(f"{tag}_vq"); aq = pp.ctr(f"{tag}_aq"); zq = pp.ctr(f"{tag}_zq")
        tp = pp.ctr(f"{tag}_tp"); ev = pp.ctr(f"{tag}_ev")
        st = [pp.ctr(f"{tag}_st{b}") for b in range(2)]

        with nc.Block() as blk:
            @blk.sync
            def _(sy):
                sy.dma_start(out=s_id[:], in_=IN["ident_f32"]).then_inc(cc.sem, 16)
                sy.dma_start(out=ones_c[:], in_=IN["ones_row"]).then_inc(cc.sem, 16)
                sy.dma_start(out=sgr[:], in_=gamma.rearrange("(a d) -> a d", a=1)
                             ).then_inc(cc.sem, 16)
                sy.dma_start(out=sbr[:], in_=beta.rearrange("(a d) -> a d", a=1)
                             ).then_inc(cc.sem, 16)
                for ti in range(NTI):
                    b = ti % 2
                    if ti >= 2:
                        sy.wait_ge(zq.sem, 2 * ti - 3)
                        sy.wait_ge(ev.sem, NCH * (ti - 1))
                    sy.dma_start(out=sff[b][:], in_=ff_tm[ti * 128:(ti + 1) * 128, :]
                                 ).then_inc(ld[b].sem, 16)
                    sy.dma_start(out=sxm[b][:], in_=xtm_bf[ti * 128:(ti + 1) * 128, :]
                                 ).then_inc(ld[b].sem, 16)
                    sy.dma_start(out=sxr[b][:], in_=fm_view(xres_src, 0, NCH, ti * 128, 128)
                                 ).then_inc(ld[b].sem, 16)

            @blk.tensor
            def _(t):
                t.wait_ge(cc.sem, 64)
                for j in range(2):
                    t.matmul(ps_g[:], lhsT=ones_c[:], rhs=sgr[:, j * TT:(j + 1) * TT],
                             start=True, stop=True).then_inc(gb.sem, 1)
                    t.wait_ge(gb.sem, 4 * j + 2)
                    t.matmul(ps_g[:], lhsT=ones_c[:], rhs=sbr[:, j * TT:(j + 1) * TT],
                             start=True, stop=True).then_inc(gb.sem, 1)
                    t.wait_ge(gb.sem, 4 * j + 4)
                for ti in range(NTI):
                    for c in range(NCH):
                        gc = ti * NCH + c
                        t.wait_ge(zq.sem, 2 * ti + 2)
                        if gc >= 2:
                            t.wait_ge(ev.sem, gc - 1)
                        t.transpose(ps_t[gc % 2][:], in_=szb[:, c * 128:(c + 1) * 128],
                                    identity=s_id[:]).then_inc(tp.sem, 1)

            @blk.scalar
            def _(s):
                for ti in range(NTI):
                    s.wait_ge(zq.sem, 2 * ti + 1)
                    s.activation(out=ssq[:], in_=sz[:], func=AF.Square).then_inc(aq.sem, 1)
                    s.wait_ge(vq.sem, 10 * ti + 6)
                    s.activation(out=sst[:, 5:6], in_=sst[:, 4:5], func=AF.Sqrt
                                 ).then_inc(aq.sem, 1)

            @blk.vector
            def _(v):
                def stepv(inst):
                    inst.then_inc(vq.sem, 1)
                    vq.n += 1
                    v.wait_ge(vq.sem, vq.n)
                v.wait_ge(gb.sem, 1)
                v.tensor_scalar(out=sgb[:, 0:TT], in0=ps_g[:], scalar1=0.5, scalar2=None,
                                op0=ALU.mult).then_inc(gb.sem, 1)
                v.wait_ge(gb.sem, 3)
                v.tensor_scalar(out=sbb[:, 0:TT], in0=ps_g[:], scalar1=0.5, scalar2=None,
                                op0=ALU.mult).then_inc(gb.sem, 1)
                v.wait_ge(gb.sem, 5)
                v.tensor_scalar(out=sgb[:, TT:D], in0=ps_g[:], scalar1=0.5, scalar2=None,
                                op0=ALU.mult).then_inc(gb.sem, 1)
                v.wait_ge(gb.sem, 7)
                v.tensor_scalar(out=sbb[:, TT:D], in0=ps_g[:], scalar1=0.5, scalar2=None,
                                op0=ALU.mult).then_inc(gb.sem, 1)
                for ti in range(NTI):
                    b = ti % 2
                    v.wait_ge(ld[b].sem, 48 * (ti // 2 + 1))
                    if ti >= 1:
                        v.wait_ge(aq.sem, 2 * ti)
                    if ti >= 2:
                        v.wait_ge(tp.sem, NCH * (ti - 1))
                    v.tensor_tensor(out=sz[:], in0=sff[b][:], in1=sxm[b][:],
                                    op=ALU.add).then_inc(zq.sem, 1)
                    v.wait_ge(zq.sem, 2 * ti + 1)
                    stepv(v.tensor_reduce(out=sst[:, 0:1], in_=sz[:], axis=AX.X, op=ALU.add))
                    v.wait_ge(aq.sem, 2 * ti + 1)
                    stepv(v.tensor_reduce(out=sst[:, 1:2], in_=ssq[:], axis=AX.X, op=ALU.add))
                    stepv(v.tensor_scalar(out=sst[:, 2:4], in0=sst[:, 0:2], scalar1=1.0 / D,
                                          scalar2=None, op0=ALU.mult))
                    stepv(v.tensor_tensor(out=sst[:, 4:5], in0=sst[:, 2:3], in1=sst[:, 2:3],
                                          op=ALU.mult))
                    stepv(v.tensor_tensor(out=sst[:, 4:5], in0=sst[:, 3:4], in1=sst[:, 4:5],
                                          op=ALU.subtract))
                    v.tensor_scalar(out=sst[:, 4:5], in0=sst[:, 4:5], scalar1=1e-5,
                                    scalar2=None, op0=ALU.add).then_inc(vq.sem, 1)
                    vq.n += 1
                    v.wait_ge(aq.sem, 2 * ti + 2)
                    stepv(v.reciprocal(out=sst[:, 6:7], in_=sst[:, 5:6]))
                    stepv(v.tensor_scalar(out=sst[:, 3:4], in0=sst[:, 2:3], scalar1=-1.0,
                                          scalar2=None, op0=ALU.mult))
                    stepv(v.tensor_scalar(out=sz[:], in0=sz[:], scalar1=sst[:, 3:4],
                                          scalar2=None, op0=ALU.add))
                    stepv(v.scalar_tensor_tensor(out=sz[:], in0=sz[:], scalar=sst[:, 6:7],
                                                 in1=sgb[:], op0=ALU.mult, op1=ALU.mult))
                    v.tensor_tensor(out=szb[:], in0=sz[:], in1=sbb[:],
                                    op=ALU.add).then_inc(zq.sem, 1)
                    for c in range(NCH):
                        gc = ti * NCH + c
                        v.wait_ge(tp.sem, gc + 1)
                        if c == 0 and ti >= 2:
                            v.wait_ge(st[ti % 2].sem, 16 * (ti // 2))
                        v.tensor_tensor(out=sy_[ti % 2][:, c, :], in0=ps_t[gc % 2][:],
                                        in1=sxr[b][:, c, :], op=ALU.add).then_inc(ev.sem, 1)

            @blk.gpsimd
            def _(g):
                for ti in range(NTI):
                    g.wait_ge(ev.sem, NCH * (ti + 1))
                    g.dma_start(out=fm_view(xres_dst, 0, NCH, ti * 128, 128),
                                in_=sy_[ti % 2][:]).then_inc(st[ti % 2].sem, 16)
                g.wait_ge(st[0].sem, 16 * (NTI // 2))
                g.wait_ge(st[1].sem, 16 * (NTI // 2))
        phase_end(nc, pp)


def emit_l1ffn(nc, IN, SC, xres_src, xres_dst, tag):
    q_bf = SC[f"l1_q"].ap(); k_bf = SC[f"l1_k"].ap(); va_bf = SC[f"l1_va"].ap()
    attn_bf = SC[f"l1_attn"].ap(); t1 = SC[f"l1_t1"].ap()
    t1n_bf = SC[f"l1_t1nb"].ap(); t1n_f = SC[f"l1_t1nf"].ap()
    keys = SC[f"l1_keys"].ap(); emask = SC[f"l1_emask"].ap(); ffacc = SC[f"l1_ffacc"].ap()
    xn_bf = SC["xn_bf"].ap(); xn_f = SC["xn_f32"].ap()
    emit_proj(nc, IN, tag + "q", SC["Wq_bf"].ap(), D, D, xn_bf, q_bf, BF16,
              bias=IN["bq"], post_scale=0.125)
    emit_proj(nc, IN, tag + "k", SC["Wk_bf"].ap(), D, D, xn_bf, k_bf, BF16,
              bias=IN["bk"])
    emit_vaug(nc, IN, tag + "v", SC["Wv_bf"].ap(), xn_bf, IN["bv"], va_bf)
    den = SC[f"l1_den"].ap()
    emit_l1attn(nc, IN, tag + "a", q_bf, k_bf, va_bf, attn_bf, den)
    emit_proj(nc, IN, tag + "o", SC["Wo_bf"].ap(), D, D, attn_bf, t1, F32,
              bias=IN["bo"], res_f32=xn_f, div_den=den)
    emit_ln(nc, IN, SC, t1, want_f32=True, tag=tag + "l1",
            gamma=IN["ln1_g"], beta=IN["ln1_b"], out_bf=t1n_bf, out_f32=t1n_f)
    emit_proj(nc, IN, tag + "r", IN["router_w"], D, 1, t1n_f, keys, F32,
              bias=IN["router_b"], f32_in=True)
    xtm = SC["l1_xtm"].ap(); fftm = SC["l1_fftm"].ap(); idxl = SC["l1_idx"].ap()
    emit_x2tm(nc, IN, tag + "x", t1n_f, xtm)
    emit_router_masks(nc, IN, tag + "m", keys, emask, idxl)
    emit_moe_routed(nc, IN, tag + "e", xtm, idxl, SC["W1_bf"].ap(), IN["b1"],
                    SC["W2_bf"].ap(), IN["b2"], fftm, SC["ident"].ap())
    emit_l1ffn_epi(nc, IN, tag + "p", xtm, fftm, IN["ln2_g"], IN["ln2_b"],
                   xres_src, xres_dst, SC["ident"].ap())





def emit_fm2tm(nc, IN, tag, src_fm, R, dst_tm, ident_bf, col0=0):
    RCH = R // 128 if R >= 128 else 1
    rr = min(R, 128)
    NTOK = L // 128
    with ExitStack() as ctx:
        pp = P(nc, ctx)
        s_id = pp.sbuf(f"{tag}_id", [128, 128], BF16)
        s_in = [pp.sbuf(f"{tag}_i{b}", [rr, 128], BF16) for b in range(2)]
        s_out = [pp.sbuf(f"{tag}_o{b}", [128, R], BF16) for b in range(2)]
        ps_t = pp.psum(f"{tag}_ps", [128, 128], BF16)
        cc = pp.ctr(f"{tag}_cc")
        ld = [pp.ctr(f"{tag}_ld{b}") for b in range(2)]
        e2 = pp.ctr(f"{tag}_e2"); v2 = pp.ctr(f"{tag}_v2")
        st = [pp.ctr(f"{tag}_st{b}") for b in range(2)]
        steps = [(i, rc) for i in range(NTOK) for rc in range(RCH)]

        with nc.Block() as blk:
            @blk.sync
            def _(sy):
                sy.dma_start(out=s_id[:], in_=ident_bf).then_inc(cc.sem, 16)
                for n, (i, rc) in enumerate(steps):
                    if n >= 2:
                        sy.wait_ge(e2.sem, n - 1)
                    sy.dma_start(out=s_in[n % 2][:],
                                 in_=src_fm[rc * 128:rc * 128 + rr, i * 128:(i + 1) * 128]
                                 ).then_inc(ld[n % 2].sem, 16)

            @blk.tensor
            def _(t):
                t.wait_ge(cc.sem, 16)
                for n, (i, rc) in enumerate(steps):
                    t.wait_ge(ld[n % 2].sem, 16 * (n // 2 + 1))
                    if n >= 1:
                        t.wait_ge(v2.sem, n)
                    t.transpose(ps_t[:, :rr], in_=s_in[n % 2][:], identity=s_id[0:rr, 0:rr]
                                ).then_inc(e2.sem, 1)

            @blk.vector
            def _(v):
                for n, (i, rc) in enumerate(steps):
                    v.wait_ge(e2.sem, n + 1)
                    if n >= 2 * RCH:
                        v.wait_ge(st[(n // RCH) % 2].sem, 16 * (n // (2 * RCH)))
                    v.tensor_copy(out=s_out[(n // RCH) % 2][:, rc * 128:rc * 128 + rr],
                                  in_=ps_t[:, :rr]).then_inc(v2.sem, 1)

            @blk.gpsimd
            def _(g):
                for i in range(NTOK):
                    g.wait_ge(v2.sem, RCH * (i + 1))
                    g.dma_start(out=dst_tm[i * 128:(i + 1) * 128, col0:col0 + R],
                                in_=s_out[i % 2][:]).then_inc(st[i % 2].sem, 16)
                g.wait_ge(st[0].sem, 16 * ((NTOK + 1) // 2))
                g.wait_ge(st[1].sem, 16 * (NTOK // 2))
        phase_end(nc, pp)


def emit_tm_proj(nc, IN, tag, W_bf, K, M, x_fm, dst_tm, col0):
    KCH = K // 128
    NM = (M + 511) // 512
    NTOK = L // 128
    with ExitStack() as ctx:
        pp = P(nc, ctx)
        sW = pp.sbuf(f"{tag}_W", [128, KCH, M], BF16)
        sx = [pp.sbuf(f"{tag}_x{b}", [128, KCH, 128], BF16) for b in range(2)]
        so = [pp.sbuf(f"{tag}_o{b}", [128, M], BF16) for b in range(2)]
        ps = [pp.psum(f"{tag}_p{b}", [128, min(M, 512)], F32) for b in range(2)]
        cc = pp.ctr(f"{tag}_cc")
        ld = [pp.ctr(f"{tag}_ld{b}") for b in range(2)]
        mm = pp.ctr(f"{tag}_mm"); ev = pp.ctr(f"{tag}_ev")
        st = [pp.ctr(f"{tag}_st{b}") for b in range(2)]

        with nc.Block() as blk:
            @blk.sync
            def _(sy):
                sy.dma_start(out=sW[:], in_=W_bf.rearrange("(c p) m -> p c m", p=128)
                             ).then_inc(cc.sem, 16)
                for i in range(NTOK):
                    if i >= 2:
                        sy.wait_ge(st[i % 2].sem, 16 * (i // 2))
                    sy.dma_start(out=sx[i % 2][:], in_=fm_view(x_fm, 0, KCH, i * 128, 128)
                                 ).then_inc(ld[i % 2].sem, 16)

            @blk.tensor
            def _(t):
                t.wait_ge(cc.sem, 16)
                for i in range(NTOK):
                    t.wait_ge(ld[i % 2].sem, 16 * (i // 2 + 1))
                    for nm in range(NM):
                        tot = i * NM + nm
                        cw = min(512, M - nm * 512)
                        if tot >= 2:
                            t.wait_ge(ev.sem, tot - 1)
                        for kc in range(KCH):
                            inst = t.matmul(ps[tot % 2][:, :cw],
                                            lhsT=sx[i % 2][:, kc, :],
                                            rhs=sW[:, kc, nm * 512:nm * 512 + cw],
                                            start=(kc == 0), stop=(kc == KCH - 1))
                        inst.then_inc(mm.sem, 1)

            @blk.vector
            def _(v):
                for i in range(NTOK):
                    if i >= 2:
                        v.wait_ge(st[i % 2].sem, 16 * (i // 2))
                    for nm in range(NM):
                        tot = i * NM + nm
                        cw = min(512, M - nm * 512)
                        v.wait_ge(mm.sem, tot + 1)
                        v.tensor_copy(out=so[i % 2][:, nm * 512:nm * 512 + cw],
                                      in_=ps[tot % 2][:, :cw]).then_inc(ev.sem, 1)

            @blk.gpsimd
            def _(g):
                for i in range(NTOK):
                    g.wait_ge(ev.sem, NM * (i + 1))
                    g.dma_start(out=dst_tm[i * 128:(i + 1) * 128, col0:col0 + M],
                                in_=so[i % 2][:]).then_inc(st[i % 2].sem, 16)
                g.wait_ge(st[0].sem, 16 * ((NTOK + 1) // 2))
                g.wait_ge(st[1].sem, 16 * (NTOK // 2))
        phase_end(nc, pp)


def emit_l0_tables(nc, IN, SC, tag):
    kr = SC["l0_krr"].ap().rearrange("(s d) t -> d s t", s=2)

    def half_ap(dram, sh):
        return dram.rearrange("(h x) t -> h x t", x=32)[:, sh * 16:(sh + 1) * 16, :]



    qr = SC["l0_qrr"].ap()
    s96 = float(96.0 ** -0.5)
    with ExitStack() as ctx:
        pp = P(nc, ctx)
        sd = pp.sbuf(f"{tag}_d", [16, 8], F32)
        sbias = pp.sbuf(f"{tag}_bi", [128, 2], F32)
        skr = pp.sbuf(f"{tag}_kr", [16, 2, L], F32)
        stk = pp.sbuf(f"{tag}_tk", [16, 2, L], F32)
        stq = pp.sbuf(f"{tag}_tq", [128, 2, L], F32)
        sqr = pp.sbuf(f"{tag}_qr", [128, 2, L], F32)
        stt_ = pp.sbuf(f"{tag}_t", [128, 2, L], F32)
        sqo = pp.sbuf(f"{tag}_qo", [128, L], F32)
        srep = pp.sbuf(f"{tag}_rp", [16, 128], F32)
        sdq = pp.sbuf(f"{tag}_dq", [128, 2], F32)
        ps_r = pp.psum(f"{tag}_pr", [128, 2], F32)
        rp = pp.ctr(f"{tag}_rpc")
        cc = pp.ctr(f"{tag}_cc"); vq = pp.ctr(f"{tag}_vq"); aq = pp.ctr(f"{tag}_aq")
        dn = pp.ctr(f"{tag}_dn"); so_ = pp.ctr(f"{tag}_so")
        bi = pp.ctr(f"{tag}_bi"); ck2 = pp.ctr(f"{tag}_k2")

        with nc.Block() as blk:
            @blk.sync
            def _(sy):
                sy.dma_start(out=sd[:, 0:1], in_=IN["raw_delta"].rearrange("(p a) -> p a", a=1)
                             ).then_inc(cc.sem, 16)
                sy.dma_start(out=stk[:, 0, :], in_=IN["tab_kc"]).then_inc(cc.sem, 16)
                sy.dma_start(out=stk[:, 1, :], in_=IN["tab_ks"]).then_inc(cc.sem, 16)
                sy.dma_start(out=stq[:, 0, :], in_=IN["tab_qc"]).then_inc(cc.sem, 16)
                sy.dma_start(out=stq[:, 1, :], in_=IN["tab_qs"]).then_inc(cc.sem, 16)
                sy.dma_start(out=srep[:], in_=IN["rep16"]).then_inc(cc.sem, 16)

            @blk.gpsimd
            def _(g):
                g.dma_start(out=skr[:], in_=kr).then_inc(ck2.sem, 16)
                for sh in range(2):
                    hap = half_ap(qr, sh)
                    for h in range(8):
                        g.dma_start(out=sqr[h * 16:(h + 1) * 16, sh, :], in_=hap[h]
                                    ).then_inc(ck2.sem, 16)
                g.wait_ge(dn.sem, 2)
                g.dma_start(out=SC["l0_mfm"].ap().rearrange("(s d) t -> d s t", s=2),
                            in_=skr[:]).then_inc(so_.sem, 16)
                g.wait_ge(dn.sem, 3)
                g.dma_start(out=SC["l0_krope"].ap().rearrange("(s d) t -> d s t", s=2),
                            in_=stt_[0:16, :, :]).then_inc(so_.sem, 16)
                g.wait_ge(dn.sem, 4)
                for sh in range(2):
                    hap = half_ap(SC["l0_qmu"].ap(), sh)
                    for h in range(8):
                        g.dma_start(out=hap[h], in_=sqr[h * 16:(h + 1) * 16, sh, :]
                                    ).then_inc(so_.sem, 16)
                g.wait_ge(dn.sem, 5)
                hap = half_ap(SC["l0_qband"].ap(), 0)
                for h in range(8):
                    g.dma_start(out=hap[h], in_=sqo[h * 16:(h + 1) * 16, :]
                                ).then_inc(so_.sem, 16)
                g.wait_ge(dn.sem, 6)
                hap = half_ap(SC["l0_qband"].ap(), 1)
                for h in range(8):
                    g.dma_start(out=hap[h], in_=sqo[h * 16:(h + 1) * 16, :]
                                ).then_inc(so_.sem, 16)
                g.wait_ge(so_.sem, 16 * (2 + 16 + 8 + 8))

            @blk.tensor
            def _(t):
                t.wait_ge(cc.sem, 96)
                t.wait_ge(rp.sem, 1)
                t.matmul(ps_r[:], lhsT=srep[:], rhs=sd[:, 5:7], start=True, stop=True
                         ).then_inc(rp.sem, 1)

            @blk.scalar
            def _(s):
                def steps(inst):
                    inst.then_inc(aq.sem, 1)
                    aq.n += 1
                    s.wait_ge(aq.sem, aq.n)
                s.wait_ge(cc.sem, 96)
                s.wait_ge(ck2.sem, 16 * 17)
                s.wait_ge(bi.sem, 1)
                steps(s.activation(out=sd[:, 1:2], in_=sd[:, 0:1], func=AF.Sigmoid))
                s.wait_ge(dn.sem, 1)
                steps(s.activation(out=sd[:, 3:4], in_=sd[:, 2:3], func=AF.Sin))
                steps(s.activation(out=sd[:, 4:5], in_=sd[:, 2:3], func=AF.Sin,
                                   bias=sbias[0:16, 1:2], scale=1.0))
                steps(s.activation(out=stt_[0:16, :, :], in_=skr[:], func=AF.Abs))
                steps(s.activation(out=stt_[0:16, :, :], in_=stt_[0:16, :, :],
                                   func=AF.Exp, scale=-1.0))
                steps(s.activation(out=stt_[0:16, :, :], in_=stt_[0:16, :, :],
                                   func=AF.Ln, bias=sbias[0:16, 0:1]))
                steps(s.activation(out=skr[:], in_=skr[:], func=AF.Relu))
                s.wait_ge(dn.sem, 3)
                s.wait_ge(so_.sem, 32)
                steps(s.activation(out=stt_[:], in_=sqr[:], func=AF.Abs))
                steps(s.activation(out=stt_[:], in_=stt_[:], func=AF.Exp, scale=-1.0))
                steps(s.activation(out=stt_[:], in_=stt_[:], func=AF.Ln, bias=sbias[:, 0:1]))
                steps(s.activation(out=sqr[:], in_=sqr[:], func=AF.Relu))

            @blk.vector
            def _(v):
                def stepv(inst):
                    inst.then_inc(vq.sem, 1)
                    vq.n += 1
                    v.wait_ge(vq.sem, vq.n)
                v.memset(sbias[:, 0:1], 1.0)
                v.memset(sbias[:, 1:2], math.pi / 2).then_inc(bi.sem, 1)
                v.wait_ge(aq.sem, 1)
                v.tensor_scalar(out=sd[:, 2:3], in0=sd[:, 1:2], scalar1=-math.pi,
                                scalar2=None, op0=ALU.mult).then_inc(dn.sem, 1)
                v.wait_ge(aq.sem, 3)
                stepv(v.tensor_tensor(out=sd[:, 5:6], in0=sd[:, 3:4], in1=sd[:, 3:4],
                                      op=ALU.mult))
                stepv(v.tensor_scalar(out=sd[:, 5:6], in0=sd[:, 5:6], scalar1=-2.0,
                                      scalar2=1.0, op0=ALU.mult, op1=ALU.add))
                stepv(v.tensor_tensor(out=sd[:, 6:7], in0=sd[:, 3:4], in1=sd[:, 4:5],
                                      op=ALU.mult))
                stepv(v.tensor_scalar(out=sd[:, 6:7], in0=sd[:, 6:7], scalar1=2.0,
                                      scalar2=None, op0=ALU.mult))
                v.tensor_copy(out=sdq[0:16, 0:2], in_=sd[:, 5:7]).then_inc(rp.sem, 1)
                v.wait_ge(rp.sem, 2)
                stepv(v.tensor_copy(out=sdq[:], in_=ps_r[:]))

                def rotate(tab, rows, scr, dsc):
                    tc = tab[0:rows, 0, :]; tsn = tab[0:rows, 1, :]
                    s1 = scr[0:rows, 0, :]; s2 = scr[0:rows, 1, :]
                    cosd = dsc[0:rows, 0:1]; sind = dsc[0:rows, 1:2]
                    stepv(v.tensor_scalar(out=s1, in0=tc, scalar1=sind,
                                          scalar2=None, op0=ALU.mult))
                    stepv(v.tensor_scalar(out=tc, in0=tc, scalar1=cosd,
                                          scalar2=None, op0=ALU.mult))
                    stepv(v.tensor_scalar(out=s2, in0=tsn, scalar1=sind,
                                          scalar2=None, op0=ALU.mult))
                    stepv(v.tensor_scalar(out=tsn, in0=tsn, scalar1=cosd,
                                          scalar2=None, op0=ALU.mult))
                    stepv(v.tensor_tensor(out=tc, in0=tc, in1=s2, op=ALU.subtract))
                    stepv(v.tensor_tensor(out=tsn, in0=tsn, in1=s1, op=ALU.add))

                v.wait_ge(aq.sem, 7)
                stepv(v.tensor_tensor(out=skr[:], in0=skr[:], in1=stt_[0:16, :, :],
                                      op=ALU.add))
                v.tensor_copy(out=sd[:, 7:8], in_=sd[:, 5:6]).then_inc(dn.sem, 1)
                rotate(stk, 16, stt_[0:16, :, :], sdq)
                m1 = skr[:, 0, :]; m2 = skr[:, 1, :]
                crot = stk[:, 0, :]; srt = stk[:, 1, :]
                stepv(v.tensor_tensor(out=stt_[0:16, 0, :], in0=m1, in1=crot, op=ALU.mult))
                stepv(v.tensor_tensor(out=stt_[0:16, 1, :], in0=m2, in1=srt, op=ALU.mult))
                stepv(v.tensor_tensor(out=stt_[0:16, 0, :], in0=stt_[0:16, 0, :],
                                      in1=stt_[0:16, 1, :], op=ALU.subtract))
                stepv(v.tensor_tensor(out=stt_[0:16, 1, :], in0=m2, in1=crot, op=ALU.mult))
                stepv(v.tensor_tensor(out=sqo[0:16, :], in0=m1, in1=srt, op=ALU.mult))
                stepv(v.tensor_tensor(out=stt_[0:16, 1, :], in0=stt_[0:16, 1, :],
                                      in1=sqo[0:16, :], op=ALU.add))
                v.tensor_copy(out=sd[:, 7:8], in_=sd[:, 5:6]).then_inc(dn.sem, 1)
                v.wait_ge(aq.sem, 11)
                stepv(v.tensor_tensor(out=sqr[:], in0=sqr[:], in1=stt_[:], op=ALU.add))
                stepv(v.tensor_scalar(out=sqr[:], in0=sqr[:], scalar1=0.10206207261596575,
                                      scalar2=None, op0=ALU.mult))
                v.tensor_copy(out=sd[:, 7:8], in_=sd[:, 5:6]).then_inc(dn.sem, 1)
                rotate(stq, 128, stt_, sdq)
                cq = stq[:, 0, :]; sq_ = stq[:, 1, :]
                mu1 = sqr[:, 0, :]; mu2 = sqr[:, 1, :]
                stepv(v.tensor_tensor(out=sqo[:], in0=mu1, in1=cq, op=ALU.mult))
                stepv(v.tensor_tensor(out=stt_[:, 0, :], in0=mu2, in1=sq_, op=ALU.mult))
                stepv(v.tensor_tensor(out=sqo[:], in0=sqo[:], in1=stt_[:, 0, :],
                                      op=ALU.subtract))
                v.tensor_copy(out=sd[:, 7:8], in_=sd[:, 5:6]).then_inc(dn.sem, 1)
                v.wait_ge(so_.sem, 16 * (2 + 16 + 8))
                stepv(v.tensor_tensor(out=sqo[:], in0=mu1, in1=sq_, op=ALU.mult))
                stepv(v.tensor_tensor(out=stt_[:, 0, :], in0=mu2, in1=cq, op=ALU.mult))
                stepv(v.tensor_tensor(out=sqo[:], in0=sqo[:], in1=stt_[:, 0, :],
                                      op=ALU.add))
                v.tensor_copy(out=sd[:, 7:8], in_=sd[:, 5:6]).then_inc(dn.sem, 1)
        phase_end(nc, pp)

    with ExitStack() as ctx:
        pp = P(nc, ctx)
        tbuf = pp.sbuf(f"{tag}_cp", [128, L], BF16)
        lc = pp.ctr(f"{tag}_lc"); sc_ = pp.ctr(f"{tag}_sc")
        jobs = []
        for h in range(8):
            jobs.append((SC["l0_k"].ap()[h * 64:(h + 1) * 64, :],
                         SC["l0_Kcat"].ap()[h * 96:h * 96 + 64, :], 64))
            jobs.append((SC["l0_krope"].ap(),
                         SC["l0_Kcat"].ap()[h * 96 + 64:h * 96 + 96, :], 32))
            jobs.append((SC["l0_qc"].ap()[h * 64:(h + 1) * 64, :],
                         SC["l0_Qcat"].ap()[h * 96:h * 96 + 64, :], 64))
            jobs.append((SC["l0_qband"].ap()[h * 32:(h + 1) * 32, :],
                         SC["l0_Qcat"].ap()[h * 96 + 64:h * 96 + 96, :], 32))
        with nc.Block() as blk:
            @blk.gpsimd
            def _(g):
                for n, (src, dst, rows) in enumerate(jobs):
                    if n >= 1:
                        g.wait_ge(sc_.sem, 16 * n)
                    g.dma_start(out=tbuf[0:rows, :], in_=src).then_inc(lc.sem, 16)
                    g.wait_ge(lc.sem, 16 * (n + 1))
                    g.dma_start(out=dst, in_=tbuf[0:rows, :]).then_inc(sc_.sem, 16)
                g.wait_ge(sc_.sem, 16 * len(jobs))
        phase_end(nc, pp)


def emit_l0_core1(nc, IN, SC, tag):
    NTI = L // 128
    with ExitStack() as ctx:
        pp = P(nc, ctx)
        sQi = pp.sbuf(f"{tag}_qi", [64, 4, L], BF16)
        sKi = pp.sbuf(f"{tag}_ki", [64, L], BF16)
        sPr = pp.sbuf(f"{tag}_pr", [128, L], F32)
        sLv = pp.sbuf(f"{tag}_lv", [128, L], F32)
        sNG = pp.sbuf(f"{tag}_ng", [128, TT], F32)
        sW4 = pp.sbuf(f"{tag}_w4", [128, 4], F32)
        sw1 = pp.sbuf(f"{tag}_w1", [1, 4], F32)
        siop = pp.sbuf(f"{tag}_io", [128, 1], F32)
        ones_r = pp.sbuf(f"{tag}_onr", [1, 128], F32)
        srel = [pp.sbuf(f"{tag}_re{b}", [128, TT], F32) for b in range(2)]
        sI = pp.sbuf(f"{tag}_I", [128, L], F32)
        sCa = pp.sbuf(f"{tag}_Ca", [128, L], F32)
        sms = pp.sbuf(f"{tag}_ms", [128, 3 * TT + 4], F32)
        smki = pp.sbuf(f"{tag}_mi", [128, TT], I32)
        smli = pp.sbuf(f"{tag}_li", [128, TT], I32)
        sVal = pp.sbuf(f"{tag}_va", [128, 32], F32)
        sIdxU = pp.sbuf(f"{tag}_iu", [128, 32], U32)
        sIdxI = [pp.sbuf(f"{tag}_ii{b}", [128, 32], I32) for b in range(2)]
        ps_I = [pp.psum(f"{tag}_pI{b}", [128, TT], F32) for b in range(2)]
        ps_w = pp.psum(f"{tag}_pw", [128, 4], F32)
        cc = pp.ctr(f"{tag}_cc"); bw = pp.ctr(f"{tag}_bw")
        mm = pp.ctr(f"{tag}_mm"); rl = pp.ctr(f"{tag}_rl"); vq = pp.ctr(f"{tag}_vq")
        cb = pp.ctr(f"{tag}_cb"); tk = pp.ctr(f"{tag}_tk")
        st = [pp.ctr(f"{tag}_st{b}") for b in range(2)]

        with nc.Block() as blk:
            @blk.sync
            def _(sy):
                sy.dma_start(out=sQi[:], in_=SC["l0_qi"].ap().rearrange("(h p) t -> p h t", p=64)
                             ).then_inc(cc.sem, 16)
                sy.dma_start(out=sKi[:], in_=SC["l0_ki"].ap()).then_inc(cc.sem, 16)
                sy.dma_start(out=sPr[:], in_=IN["iota_row"]).then_inc(cc.sem, 16)
                sy.dma_start(out=sLv[:], in_=IN["ladv"]).then_inc(cc.sem, 16)
                sy.dma_start(out=sw1[:], in_=IN["idx_w"].rearrange("(a w) -> a w", a=1)
                             ).then_inc(cc.sem, 16)
                sy.dma_start(out=ones_r[:], in_=IN["ones_row"]).then_inc(cc.sem, 16)
                sy.dma_start(out=siop[:], in_=IN["iota_p"]).then_inc(cc.sem, 16)

            def ncs(ti):
                return min(8, ti // 4 + 1)
            triples = [(ti, sc, h) for ti in range(NTI)
                       for sc in range(ncs(ti)) for h in range(4)]

            @blk.tensor
            def _(t):
                t.wait_ge(cc.sem, 112)
                t.matmul(ps_w[:], lhsT=ones_r[:], rhs=sw1[:], start=True, stop=True
                         ).then_inc(bw.sem, 1)
                for g, (ti, sc, h) in enumerate(triples):
                    if g >= 2:
                        t.wait_ge(rl.sem, g - 1)
                    t.matmul(ps_I[g % 2][:],
                             lhsT=sQi[:, h, ti * 128:(ti + 1) * 128],
                             rhs=sKi[:, sc * TT:(sc + 1) * TT],
                             start=True, stop=True).then_inc(mm.sem, 1)

            REL_MARK = {}

            @blk.vector
            def _(v):
                def stepv(inst):
                    inst.then_inc(vq.sem, 1)
                    vq.n += 1
                    v.wait_ge(vq.sem, vq.n)
                v.memset(sNG[:], NEG)
                v.wait_ge(bw.sem, 1)
                v.tensor_copy(out=sW4[:], in_=ps_w[:]).then_inc(bw.sem, 1)
                v.wait_ge(bw.sem, 2)
                gg = 0
                for ti in range(NTI):
                    t0 = ti * 128
                    stepv(v.tensor_scalar(out=sms[:, 3 * TT:3 * TT + 1], in0=siop[:],
                                          scalar1=float(t0 - 64), scalar2=None, op0=ALU.add))
                    stepv(v.tensor_scalar(out=sms[:, 3 * TT + 1:3 * TT + 2], in0=siop[:],
                                          scalar1=float(t0), scalar2=63.0,
                                          op0=ALU.add, op1=ALU.max))
                    for sc in range(ncs(ti)):
                        for h in range(4):
                            g = gg
                            gg += 1
                            v.wait_ge(rl.sem, g + 1)
                            dst = sI[:, sc * TT:(sc + 1) * TT]
                            if h == 0:
                                stepv(v.tensor_scalar(out=dst, in0=srel[g % 2][:],
                                                      scalar1=sW4[:, 0:1], scalar2=None,
                                                      op0=ALU.mult))
                            else:
                                stepv(v.scalar_tensor_tensor(out=dst, in0=srel[g % 2][:],
                                                             scalar=sW4[:, h:h + 1], in1=dst,
                                                             op0=ALU.mult, op1=ALU.add))
                            REL_MARK[g] = vq.n
                        stepv(v.tensor_scalar(out=smki[:], in0=sPr[:, sc * TT:(sc + 1) * TT],
                                              scalar1=sms[:, 3 * TT:3 * TT + 1], scalar2=None,
                                              op0=ALU.is_le))
                        stepv(v.tensor_scalar(out=smli[:], in0=sPr[:, sc * TT:(sc + 1) * TT],
                                              scalar1=sms[:, 3 * TT + 1:3 * TT + 2], scalar2=None,
                                              op0=ALU.is_gt))
                        stepv(v.tensor_copy(out=sms[:, 2 * TT:3 * TT], in_=sNG[:]))
                        stepv(v.copy_predicated(out=sms[:, 2 * TT:3 * TT],
                                                mask=smli[:],
                                                data=sLv[:, sc * TT:(sc + 1) * TT]))
                        stepv(v.tensor_copy(out=sCa[:, sc * TT:(sc + 1) * TT],
                                            in_=sms[:, 2 * TT:3 * TT]))
                        stepv(v.copy_predicated(out=sCa[:, sc * TT:(sc + 1) * TT],
                                                mask=smki[:],
                                                data=sI[:, sc * TT:(sc + 1) * TT]))
                    W = ncs(ti) * TT
                    if ti >= 2:
                        v.wait_ge(st[ti % 2].sem, 16 * (ti // 2))
                    for r in range(4):
                        stepv(v.max(out=sVal[:, r * 8:(r + 1) * 8], in_=sCa[:, :W]))
                        stepv(v.max_index(out=sIdxU[:, r * 8:(r + 1) * 8],
                                          in_max=sVal[:, r * 8:(r + 1) * 8],
                                          in_values=sCa[:, :W]))
                        stepv(v.match_replace(out=sCa[:, :W],
                                              in_to_replace=sVal[:, r * 8:(r + 1) * 8],
                                              in_values=sCa[:, :W], imm_value=NEG))
                    stepv(v.tensor_copy(out=sIdxI[ti % 2][:], in_=sIdxU[:]))
                    v.tensor_copy(out=sms[:, 3 * TT + 3:3 * TT + 4],
                                  in_=sVal[:, 0:1]).then_inc(tk.sem, 1)

            @blk.scalar
            def _(s):
                for g, (ti, sc, h) in enumerate(triples):
                    s.wait_ge(mm.sem, g + 1)
                    if g >= 2:
                        s.wait_ge(vq.sem, REL_MARK[g - 2])
                    s.activation(out=srel[g % 2][:], in_=ps_I[g % 2][:],
                                 func=AF.Relu).then_inc(rl.sem, 1)

            @blk.gpsimd
            def _(g):
                for ti in range(NTI):
                    g.wait_ge(tk.sem, ti + 1)
                    g.dma_start(out=SC["l0_idx"].ap()[ti * 128:(ti + 1) * 128, :],
                                in_=sIdxI[ti % 2][:]).then_inc(st[ti % 2].sem, 16)
                g.wait_ge(st[0].sem, 16 * (NTI // 2))
                g.wait_ge(st[1].sem, 16 * (NTI // 2))
        phase_end(nc, pp)


def emit_l0_core2(nc, IN, SC, tag):
    NTI = L // 128
    kmv = SC["l0_kmv"].ap()
    with ExitStack() as ctx:
        pp = P(nc, ctx)
        s_id = pp.sbuf(f"{tag}_id", [128, 128], BF16)
        smA = pp.sbuf(f"{tag}_mA", [128, 192], F32)
        smB = pp.sbuf(f"{tag}_mB", [128, 192], F32)
        sIdx = [pp.sbuf(f"{tag}_ix{b}", [128, 32], I32) for b in range(2)]
        sQct = [pp.sbuf(f"{tag}_qc{b}", [128, TT], BF16) for b in range(2)]
        sQmt = [pp.sbuf(f"{tag}_qm{b}", [128, 256], BF16) for b in range(2)]
        sTc = [pp.sbuf(f"{tag}_tc{b}", [128, TT], BF16) for b in range(2)]
        sTs = [pp.sbuf(f"{tag}_ts{b}", [128, TT], BF16) for b in range(2)]
        sQc8 = [pp.sbuf(f"{tag}_q8{b}", [96, 8, 128], BF16) for b in range(2)]
        sKc8 = [pp.sbuf(f"{tag}_k8{b}", [96, 8, 192], BF16) for b in range(2)]
        sVp = [pp.sbuf(f"{tag}_vp{b}", [128, 2, TT], BF16) for b in range(2)]
        sG = [pp.sbuf(f"{tag}_G{b}", [128, 32, 1056], BF16) for b in range(2)]
        sMg = pp.sbuf(f"{tag}_Mg", [128, 32, 32], BF16)
        sAB = pp.sbuf(f"{tag}_AB", [128, 32, 32], BF16)
        sABf = pp.sbuf(f"{tag}_ABf", [128, 32, 32], BF16)
        sKp = pp.sbuf(f"{tag}_Kp", [128, 32, 64], BF16)
        sT32 = pp.sbuf(f"{tag}_T32", [128, 32], F32)
        sSel = pp.sbuf(f"{tag}_S", [128, 8, 32], F32)
        sPs = pp.sbuf(f"{tag}_Ps", [128, 32], BF16)
        sPb = [pp.sbuf(f"{tag}_Pb{b}", [128, 192], BF16) for b in range(2)]
        sPbT = pp.sbuf(f"{tag}_PbT", [128, 2, 128], BF16)
        sSv = pp.sbuf(f"{tag}_Sv", [128, 64], F32)
        sden = pp.sbuf(f"{tag}_dn", [128, 4], F32)
        sAt = pp.sbuf(f"{tag}_At", [128, TT], BF16)
        sAf = [pp.sbuf(f"{tag}_Af{b}", [128, 4, 128], BF16) for b in range(2)]
        ps_b = [pp.psum(f"{tag}_pb{b}", [128, 192], F32) for b in range(2)]
        ps_t = [pp.psum(f"{tag}_pt{b}", [128, 128], BF16) for b in range(2)]
        ps_o = [pp.psum(f"{tag}_po{b}", [128, 64], F32) for b in range(2)]
        cc = pp.ctr(f"{tag}_cc")
        ld = [pp.ctr(f"{tag}_ld{b}") for b in range(2)]
        ldg = [pp.ctr(f"{tag}_lg{b}") for b in range(2)]
        ga = [pp.ctr(f"{tag}_ga{b}") for b in range(2)]
        vq = pp.ctr(f"{tag}_vq")
        pb = pp.ctr(f"{tag}_pbc"); nm = pp.ctr(f"{tag}_nm"); ve = pp.ctr(f"{tag}_ve")
        tc = pp.ctr(f"{tag}_tcc"); po = pp.ctr(f"{tag}_poc"); ev = pp.ctr(f"{tag}_ev")
        ft = pp.ctr(f"{tag}_ft"); fc = pp.ctr(f"{tag}_fc")
        st = [pp.ctr(f"{tag}_st{b}") for b in range(2)]

        with nc.Block() as blk:
            @blk.sync
            def _(sy):
                sy.dma_start(out=s_id[:], in_=SC["ident"].ap()).then_inc(cc.sem, 16)
                sy.dma_start(out=smA[:], in_=IN["bmaskA"]).then_inc(cc.sem, 16)
                sy.dma_start(out=smB[:], in_=IN["bmaskB"]).then_inc(cc.sem, 16)
                for ti in range(NTI):
                    t0 = ti * 128
                    P0 = max(0, t0 - 64)
                    b = ti % 2
                    if ti >= 2:
                        sy.wait_ge(ev.sem, 8 * (ti - 1))
                        sy.wait_ge(ga[b].sem, 32 * 16 * (ti // 2))
                    sy.dma_start(out=sIdx[b][:], in_=SC["l0_idx"].ap()[t0:t0 + 128, :]
                                 ).then_inc(ld[b].sem, 16)
                    sy.dma_start(out=sQct[b][:], in_=SC["l0_qct"].ap()[t0:t0 + 128, :]
                                 ).then_inc(ld[b].sem, 16)
                    sy.dma_start(out=sQmt[b][:], in_=SC["l0_qmut"].ap()[t0:t0 + 128, :]
                                 ).then_inc(ld[b].sem, 16)
                    sy.dma_start(out=sQc8[b][:],
                                 in_=SC["l0_Qcat"].ap()[:, t0:t0 + 128].rearrange(
                                     "(h p) t -> p h t", p=96)).then_inc(ld[b].sem, 16)
                    sy.dma_start(out=sKc8[b][:],
                                 in_=SC["l0_Kcat"].ap()[:, P0:P0 + 192].rearrange(
                                     "(h p) t -> p h t", p=96)).then_inc(ld[b].sem, 16)
                    sy.dma_start(out=sVp[b][:, 0, :],
                                 in_=kmv[P0:P0 + 128, 544:1056]).then_inc(ld[b].sem, 16)
                    sy.dma_start(out=sVp[b][0:64, 1, :],
                                 in_=kmv[P0 + 128:P0 + 192, 544:1056]).then_inc(ld[b].sem, 16)

            @blk.gpsimd
            def _(g):
                for ti in range(NTI):
                    b = ti % 2
                    g.wait_ge(ld[b].sem, 112 * (ti // 2 + 1))
                    if ti >= 2:
                        g.wait_ge(ev.sem, 8 * (ti - 1))
                    g.dma_start(out=sTc[b][:], in_=IN["tselc"][ti * 128:ti * 128 + 128, :]
                                ).then_inc(ldg[b].sem, 16)
                    g.dma_start(out=sTs[b][:], in_=IN["tsels"][ti * 128:ti * 128 + 128, :]
                                ).then_inc(ldg[b].sem, 16)
                    for j in range(32):
                        g.indirect_dma_start(
                            out=sG[b][:, j, :], out_offset=None,
                            in_=kmv, in_offset=IndirectOffsetOnAxis(sIdx[b][:, j:j + 1], 0),
                        ).then_inc(ga[b].sem, 16)
                    if ti >= 1:
                        g.wait_ge(fc.sem, 4 * ti)
                        g.dma_start(out=fm_view(SC["l0_attnf"].ap(), 0, 4, (ti - 1) * 128, 128),
                                    in_=sAf[(ti - 1) % 2][:]).then_inc(st[(ti - 1) % 2].sem, 16)
                g.wait_ge(fc.sem, 4 * NTI)
                g.dma_start(out=fm_view(SC["l0_attnf"].ap(), 0, 4, (NTI - 1) * 128, 128),
                            in_=sAf[(NTI - 1) % 2][:]).then_inc(st[(NTI - 1) % 2].sem, 16)
                g.wait_ge(st[0].sem, 16 * (NTI // 2))
                g.wait_ge(st[1].sem, 16 * (NTI // 2))

            @blk.tensor
            def _(t):
                t.wait_ge(cc.sem, 48)
                for ti in range(NTI):
                    b = ti % 2
                    t.wait_ge(ld[b].sem, 112 * (ti // 2 + 1))
                    if ti >= 1:
                        t.wait_ge(fc.sem, 4 * ti)
                    for h in range(8):
                        g8 = ti * 8 + h
                        if g8 >= 2:
                            t.wait_ge(ve.sem, 2 * (g8 - 1))
                        t.matmul(ps_b[h % 2][:], lhsT=sQc8[b][:, h, :], rhs=sKc8[b][:, h, :],
                                 start=True, stop=True).then_inc(pb.sem, 1)
                        t.wait_ge(ve.sem, 2 * g8 + 2)
                        if g8 >= 1:
                            t.wait_ge(tc.sem, 2 * g8)
                        t.transpose(ps_t[0][:], in_=sPb[h % 2][:, 0:128], identity=s_id[:])
                        t.transpose(ps_t[1][0:64, 0:128], in_=sPb[h % 2][:, 128:192],
                                    identity=s_id[:]).then_inc(ft.sem, 1)
                        t.wait_ge(tc.sem, 2 * (g8 + 1))
                        if g8 >= 2:
                            t.wait_ge(ev.sem, g8 - 1)
                        t.matmul(ps_o[h % 2][:], lhsT=sPbT[:, 0, :],
                                 rhs=sVp[b][:, 0, h * 64:(h + 1) * 64], start=True, stop=False)
                        t.matmul(ps_o[h % 2][:], lhsT=sPbT[0:64, 1, :],
                                 rhs=sVp[b][0:64, 1, h * 64:(h + 1) * 64], start=False,
                                 stop=True).then_inc(po.sem, 1)
                    t.wait_ge(ev.sem, 8 * (ti + 1))
                    t.wait_ge(tc.sem, 2 * 8 * (ti + 1))
                    for q4 in range(4):
                        if q4 >= 2:
                            t.wait_ge(fc.sem, 4 * ti + q4 - 1)
                        t.transpose(ps_t[q4 % 2][:], in_=sAt[:, q4 * 128:(q4 + 1) * 128],
                                    identity=s_id[:]).then_inc(ft.sem, 1)

            @blk.scalar
            def _(s):
                for ti in range(NTI):
                    for h in range(8):
                        g8 = ti * 8 + h
                        s.wait_ge(nm.sem, g8 + 1)
                        if g8 >= 2:
                            s.wait_ge(tc.sem, 2 * (g8 - 1))
                        if g8 >= 1:
                            s.wait_ge(ev.sem, g8)
                        s.activation(out=sPb[h % 2][:], in_=ps_b[h % 2][:], func=AF.Exp,
                                     bias=sden[:, 2:3], scale=1.0)
                        s.activation(out=sPs[:], in_=sSel[:, h, :], func=AF.Exp,
                                     bias=sden[:, 2:3], scale=1.0).then_inc(ve.sem, 2)

            @blk.vector
            def _(v):
                def stepv(inst):
                    inst.then_inc(vq.sem, 1)
                    vq.n += 1
                    v.wait_ge(vq.sem, vq.n)
                for ti in range(NTI):
                    b = ti % 2
                    v.wait_ge(ga[b].sem, 32 * 16 * (ti // 2 + 1))
                    v.wait_ge(ld[b].sem, 112 * (ti // 2 + 1))
                    v.wait_ge(ldg[b].sem, 32 * (ti // 2 + 1))
                    stepv(v.tensor_copy(out=sMg[:], in_=sG[b][:, :, 0:32]))
                    tcv = sTc[b][:].rearrange("p (j d) -> p j d", d=16)
                    tsv = sTs[b][:].rearrange("p (j d) -> p j d", d=16)
                    m1 = sMg[:, :, 0:16]; m2 = sMg[:, :, 16:32]
                    stepv(v.tensor_tensor(out=sABf[:, :, 0:16], in0=tcv, in1=m1, op=ALU.mult))
                    stepv(v.tensor_tensor(out=sABf[:, :, 16:32], in0=tsv, in1=m2, op=ALU.mult))
                    stepv(v.tensor_tensor(out=sAB[:, :, 0:16], in0=sABf[:, :, 0:16],
                                          in1=sABf[:, :, 16:32], op=ALU.add))
                    stepv(v.tensor_tensor(out=sABf[:, :, 0:16], in0=tcv, in1=m2, op=ALU.mult))
                    stepv(v.tensor_tensor(out=sABf[:, :, 16:32], in0=tsv, in1=m1, op=ALU.mult))
                    stepv(v.tensor_tensor(out=sABf[:, :, 0:16], in0=sABf[:, :, 0:16],
                                          in1=sABf[:, :, 16:32], op=ALU.subtract))
                    stepv(v.tensor_copy(out=sAB[:, :, 16:32], in_=sABf[:, :, 0:16]))
                    for h in range(8):
                        qm = sQmt[b][:, h * 32:(h + 1) * 32].rearrange(
                            "p (a d) -> p a d", a=1).to_broadcast([128, 32, 32])
                        stepv(v.tensor_tensor(out=sABf[:], in0=sAB[:], in1=qm, op=ALU.mult))
                        stepv(v.tensor_reduce(out=sSel[:, h, :], in_=sABf[:], axis=AX.X,
                                              op=ALU.add))
                        qc = sQct[b][:, h * 64:(h + 1) * 64].rearrange(
                            "p (a d) -> p a d", a=1).to_broadcast([128, 32, 64])
                        stepv(v.tensor_tensor(out=sKp[:],
                                              in0=sG[b][:, :, 32 + h * 64:32 + (h + 1) * 64],
                                              in1=qc, op=ALU.mult))
                        stepv(v.tensor_reduce(out=sT32[:], in_=sKp[:], axis=AX.X, op=ALU.add))
                        stepv(v.tensor_tensor(out=sSel[:, h, :], in0=sSel[:, h, :],
                                              in1=sT32[:], op=ALU.add))
                    for h in range(8):
                        g8 = ti * 8 + h
                        v.wait_ge(pb.sem, g8 + 1)
                        msk = smB if ti == 0 else smA
                        stepv(v.tensor_tensor(out=ps_b[h % 2][:], in0=ps_b[h % 2][:],
                                              in1=msk[:], op=ALU.add))
                        stepv(v.tensor_reduce(out=sden[:, 0:1], in_=ps_b[h % 2][:],
                                              axis=AX.X, op=ALU.max))
                        stepv(v.tensor_reduce(out=sden[:, 1:2], in_=sSel[:, h, :],
                                              axis=AX.X, op=ALU.max))
                        stepv(v.tensor_tensor(out=sden[:, 0:1], in0=sden[:, 0:1],
                                              in1=sden[:, 1:2], op=ALU.max))
                        v.tensor_scalar(out=sden[:, 2:3], in0=sden[:, 0:1], scalar1=-1.0,
                                        scalar2=None, op0=ALU.mult).then_inc(nm.sem, 1)
                        v.wait_ge(ve.sem, 2 * g8 + 2)
                        stepv(v.tensor_reduce(out=sden[:, 0:1], in_=sPb[h % 2][:],
                                              axis=AX.X, op=ALU.add))
                        stepv(v.tensor_reduce(out=sden[:, 1:2], in_=sPs[:],
                                              axis=AX.X, op=ALU.add))
                        stepv(v.tensor_tensor(out=sden[:, 0:1], in0=sden[:, 0:1],
                                              in1=sden[:, 1:2], op=ALU.add))
                        stepv(v.reciprocal(out=sden[:, 3:4], in_=sden[:, 0:1]))
                        v.wait_ge(ft.sem, 12 * ti + h + 1)
                        stepv(v.tensor_copy(out=sPbT[:, 0, :], in_=ps_t[0][:]))
                        v.tensor_copy(out=sPbT[0:64, 1, :], in_=ps_t[1][0:64, :]
                                      ).then_inc(tc.sem, 2)
                        ps_bc = sPs[:].rearrange("p (j a) -> p j a", a=1).to_broadcast(
                            [128, 32, 64])
                        stepv(v.tensor_tensor(out=sKp[:],
                                              in0=sG[b][:, :, 544 + h * 64:544 + (h + 1) * 64],
                                              in1=ps_bc, op=ALU.mult))
                        stepv(v.tensor_reduce(out=sSv[:],
                                              in_=sKp[:].rearrange("p j d -> p d j"),
                                              axis=AX.X, op=ALU.add))
                        v.wait_ge(po.sem, g8 + 1)
                        stepv(v.tensor_tensor(out=sSv[:], in0=sSv[:],
                                              in1=ps_o[h % 2][:], op=ALU.add))
                        v.tensor_scalar(out=sAt[:, h * 64:(h + 1) * 64], in0=sSv[:],
                                        scalar1=sden[:, 3:4], scalar2=None, op0=ALU.mult
                                        ).then_inc(ev.sem, 1)
                    for q4 in range(4):
                        v.wait_ge(ft.sem, 12 * ti + 8 + q4 + 1)
                        if ti >= 2:
                            v.wait_ge(st[ti % 2].sem, 16 * (ti // 2))
                        v.tensor_copy(out=sAf[ti % 2][:, q4, :], in_=ps_t[q4 % 2][:]
                                      ).then_inc(fc.sem, 1)
        phase_end(nc, pp)


def emit_l0attn_block(nc, IN, SC, xres_src, xres_dst, tag):
    xn = SC["xn_bf"].ap()
    emit_proj(nc, IN, tag + "pA", SC["WcatA_bf"].ap(), D, 1120, xn,
              SC["l0_catA"].ap(), BF16)
    emit_proj(nc, IN, tag + "kf", SC["W_uk_bf"].ap(), 256, 512, SC["l0_ckv"].ap(),
              SC["l0_k"].ap(), BF16)
    emit_proj(nc, IN, tag + "pB", SC["WcatB_bf"].ap(), 512, 768, SC["l0_cq"].ap(),
              SC["l0_catB"].ap(), BF16,
              post_scale=[float(96.0 ** -0.5)] * 4 + [1.0] * 2)
    emit_l0_tables(nc, IN, SC, tag + "tb")
    emit_tm_proj(nc, IN, tag + "kt", SC["W_uk_bf"].ap(), 256, 512, SC["l0_ckv"].ap(),
                 SC["l0_kmv"].ap(), 32)
    emit_tm_proj(nc, IN, tag + "vt", SC["W_uv_bf"].ap(), 256, 512, SC["l0_ckv"].ap(),
                 SC["l0_kmv"].ap(), 544)
    emit_fm2tm(nc, IN, tag + "mt", SC["l0_mfm"].ap(), 32, SC["l0_kmv"].ap(),
               SC["ident"].ap(), col0=0)
    emit_fm2tm(nc, IN, tag + "qt", SC["l0_qc"].ap(), 512, SC["l0_qct"].ap(),
               SC["ident"].ap(), col0=0)
    emit_fm2tm(nc, IN, tag + "mu", SC["l0_qmu"].ap(), 256, SC["l0_qmut"].ap(),
               SC["ident"].ap(), col0=0)
    emit_l0_core1(nc, IN, SC, tag + "c1")
    emit_l0_core2(nc, IN, SC, tag + "c2")
    emit_proj(nc, IN, tag + "ou", SC["W_out_bf"].ap(), 512, D, SC["l0_attnf"].ap(),
              xres_dst, F32, res_f32=xres_src)



def emit_l1attn_block(nc, IN, SC, xres_src, xres_dst, tag):
    lat_bf = SC["a_lat"].ap(); va2 = SC["a_va"].ap(); y_bf = SC["a_y"].ap()
    xn_bf = SC["xn_bf"].ap()
    emit_proj(nc, IN, tag + "l", SC["aWbT_bf"].ap(), D, 64, xn_bf, lat_bf, BF16)
    emit_vaug(nc, IN, tag + "v", SC["aWvT_bf"].ap(), xn_bf, IN["abv"], va2,
              nh=1, hd=1024)

    NKT = L // 128
    with ExitStack() as ctx:
        pp = P(nc, ctx)
        ones_r = pp.sbuf(f"{tag}_onr", [1, 128], F32)
        ones_cb = pp.sbuf(f"{tag}_ocb", [128, 1], BF16)
        ones_cf = pp.sbuf(f"{tag}_ocf", [128, 1], F32)
        sLat = pp.sbuf(f"{tag}_lat", [128, L], BF16)
        sVa = pp.sbuf(f"{tag}_va", [128, NKT, 1025], BF16)
        scm = pp.sbuf(f"{tag}_cm", [128, 4, TT], F32)
        ssq = pp.sbuf(f"{tag}_sq", [64, TT], F32)
        snr = pp.sbuf(f"{tag}_nr", [1, 8], F32)
        sBn = pp.sbuf(f"{tag}_Bn", [128, 1], F32)
        sPall = pp.sbuf(f"{tag}_P", [128, NKT, TT], BF16)
        sY = [pp.sbuf(f"{tag}_Y{b}", [128, NCH, TT], BF16) for b in range(2)]
        snb = pp.sbuf(f"{tag}_nb2", [128, TT], F32)
        srd = pp.sbuf(f"{tag}_rd", [1, TT], F32)
        ps_g = [pp.psum(f"{tag}_pg{b}", [128, 2, TT], F32) for b in range(2)]
        ps_d = pp.psum(f"{tag}_pd", [1, TT], F32)
        ps_nb = pp.psum(f"{tag}_nb", [128, TT], F32)
        ps_B = pp.psum(f"{tag}_pB", [128, 1], F32)
        cc = pp.ctr(f"{tag}_cc"); z1 = pp.ctr(f"{tag}_z1")
        nq = pp.ctr(f"{tag}_nq"); np_ = pp.ctr(f"{tag}_np"); vq = pp.ctr(f"{tag}_vq")
        b1 = pp.ctr(f"{tag}_b1")
        ms = pp.ctr(f"{tag}_ms"); mk = pp.ctr(f"{tag}_mk"); ex = pp.ctr(f"{tag}_ex")
        pv = pp.ctr(f"{tag}_pv"); dn = pp.ctr(f"{tag}_dn"); rc = pp.ctr(f"{tag}_rc")
        bc = pp.ctr(f"{tag}_bc"); eo = pp.ctr(f"{tag}_eo")
        nbq = pp.ctr(f"{tag}_nbq")
        st = [pp.ctr(f"{tag}_st{b}") for b in range(2)]
        base_of = lambda qt: sum(2 * q + 2 for q in range(qt))
        NGT = base_of(8)

        with nc.Block() as blk:
            @blk.sync
            def _(sy):
                sy.dma_start(out=ones_r[:], in_=IN["ones_row"]).then_inc(cc.sem, 16)
                sy.dma_start(out=ones_cb[:], in_=IN["ones_col_bf"]).then_inc(cc.sem, 16)
                sy.dma_start(out=ones_cf[:], in_=IN["ones_col"]).then_inc(cc.sem, 16)
                sy.dma_start(out=scm[:], in_=IN["cmask"].rearrange("(o p) t -> p o t", p=128)
                             ).then_inc(cc.sem, 16)
                sy.wait_ge(z1.sem, 1)
                sy.dma_start(out=sLat[0:64, :], in_=lat_bf).then_inc(cc.sem, 16)
                sy.dma_start(out=sVa[:], in_=va2.rearrange("(kt p) c -> p kt c", p=128)
                             ).then_inc(cc.sem, 16)

            @blk.scalar
            def _(s):
                s.wait_ge(cc.sem, 96)
                for i in range(NT):
                    if i >= 1:
                        s.wait_ge(np_.sem, i)
                    s.activation(out=ssq[:], in_=sLat[0:64, i * TT:(i + 1) * TT],
                                 func=AF.Square).then_inc(nq.sem, 1)
                s.wait_ge(b1.sem, 3)
                for qt in range(8):
                    ngr = 2 * qt + 2
                    base_g = base_of(qt)
                    if qt >= 1:
                        s.wait_ge(pv.sem, NCH * qt)
                    for g in range(ngr):
                        if g < ngr - 2:
                            s.wait_ge(ms.sem, base_g + g + 1)
                        else:
                            s.wait_ge(mk.sem, 2 * qt + (g - (ngr - 2)) + 1)
                        s.activation(out=sPall[:, 2 * g:2 * g + 2, :], in_=ps_g[g % 2][:],
                                     func=AF.Exp, bias=sBn[:], scale=0.125
                                     ).then_inc(ex.sem, 1)

            @blk.tensor
            def _(t):
                t.wait_ge(cc.sem, 96)
                for i in range(NT):
                    t.wait_ge(nq.sem, i + 1)
                    if i >= 1:
                        t.wait_ge(vq.sem, 2 * i)
                    t.matmul(ps_d[:], lhsT=ones_cf[0:64, :], rhs=ssq[:],
                             start=True, stop=True).then_inc(np_.sem, 1)
                t.wait_ge(b1.sem, 1)
                t.matmul(ps_B[:], lhsT=ones_r[:], rhs=snr[:, 2:3],
                         start=True, stop=True).then_inc(b1.sem, 1)
                for qt in range(8):
                    nkt = 4 * qt + 4
                    ngr = nkt // 2
                    base_g = base_of(qt)
                    if qt >= 1:
                        t.wait_ge(eo.sem, NCH * qt)
                    for g in range(ngr):
                        G = base_g + g
                        if G >= 2:
                            t.wait_ge(ex.sem, G - 1)
                        for j in range(2):
                            kt = 2 * g + j
                            inst = t.matmul(ps_g[g % 2][:, j, :],
                                            lhsT=sLat[:, kt * 128:(kt + 1) * 128],
                                            rhs=sLat[:, qt * TT:(qt + 1) * TT],
                                            start=True, stop=True)
                        inst.then_inc(ms.sem, 1)
                    t.wait_ge(ex.sem, base_g + ngr)
                    if qt == 0:
                        t.wait_ge(vq.sem, 17)
                    else:
                        t.wait_ge(rc.sem, 2 * qt - 1)
                    for kt in range(nkt):
                        inst = t.matmul(ps_d[:], lhsT=ones_cb[:], rhs=sPall[:, kt, :],
                                        start=(kt == 0), stop=(kt == nkt - 1))
                    inst.then_inc(dn.sem, 1)
                    t.wait_ge(rc.sem, 2 * qt + 2)
                    if qt >= 1:
                        t.wait_ge(nbq.sem, qt)
                    t.matmul(ps_nb[:], lhsT=ones_r[:], rhs=srd[:],
                             start=True, stop=True).then_inc(bc.sem, 1)
                    for mc in range(NCH):
                        b = mc % 4
                        if mc >= 4:
                            t.wait_ge(eo.sem, qt * NCH + (mc - 4) + 1)
                        for kt in range(nkt):
                            inst = t.matmul(ps_g[b // 2][:, b % 2, :],
                                            lhsT=sVa[:, kt, mc * 128:(mc + 1) * 128],
                                            rhs=sPall[:, kt, :],
                                            start=(kt == 0), stop=(kt == nkt - 1))
                        inst.then_inc(pv.sem, 1)

            @blk.vector
            def _(v):
                def stepv(inst):
                    inst.then_inc(vq.sem, 1)
                    vq.n += 1
                    v.wait_ge(vq.sem, vq.n)
                v.memset(sLat[64:128, :], 0.0).then_inc(z1.sem, 1)
                stepv(v.memset(snr[:], 0.0))
                for i in range(NT):
                    v.wait_ge(np_.sem, i + 1)
                    stepv(v.tensor_reduce(out=snr[:, 1:2], in_=ps_d[:], axis=AX.X,
                                          op=ALU.max))
                    stepv(v.tensor_tensor(out=snr[:, 0:1], in0=snr[:, 0:1],
                                          in1=snr[:, 1:2], op=ALU.max))
                v.tensor_scalar(out=snr[:, 2:3], in0=snr[:, 0:1], scalar1=-0.125,
                                scalar2=None, op0=ALU.mult).then_inc(b1.sem, 1)
                v.wait_ge(b1.sem, 2)
                v.tensor_copy(out=sBn[:], in_=ps_B[:]).then_inc(b1.sem, 1)
                for qt in range(8):
                    ngr = 2 * qt + 2
                    base_g = base_of(qt)
                    for gi, g in enumerate([ngr - 2, ngr - 1]):
                        v.wait_ge(ms.sem, base_g + g + 1)
                        v.tensor_tensor(out=ps_g[g % 2][:, 0, :], in0=ps_g[g % 2][:, 0, :],
                                        in1=scm[:, 2 * gi, :], op=ALU.add)
                        v.tensor_tensor(out=ps_g[g % 2][:, 1, :], in0=ps_g[g % 2][:, 1, :],
                                        in1=scm[:, 2 * gi + 1, :], op=ALU.add
                                        ).then_inc(mk.sem, 1)
                    v.wait_ge(dn.sem, qt + 1)
                    v.tensor_copy(out=srd[:], in_=ps_d[:]).then_inc(rc.sem, 1)
                    v.wait_ge(rc.sem, 2 * qt + 1)
                    v.reciprocal(out=srd[:], in_=srd[:]).then_inc(rc.sem, 1)
                    v.wait_ge(bc.sem, qt + 1)
                    v.tensor_copy(out=snb[:], in_=ps_nb[:]).then_inc(nbq.sem, 1)
                    v.wait_ge(nbq.sem, qt + 1)
                    for mc in range(NCH):
                        v.wait_ge(pv.sem, qt * NCH + mc + 1)
                        if qt >= 2:
                            v.wait_ge(st[qt % 2].sem, 16 * (qt // 2))
                        b = mc % 4
                        v.tensor_tensor(out=sY[qt % 2][:, mc, :],
                                        in0=ps_g[b // 2][:, b % 2, :], in1=snb[:],
                                        op=ALU.mult).then_inc(eo.sem, 1)

            @blk.gpsimd
            def _(g):
                for qt in range(8):
                    g.wait_ge(eo.sem, NCH * (qt + 1))
                    g.dma_start(out=fm_view(y_bf, 0, NCH, qt * TT, TT), in_=sY[qt % 2][:]
                                ).then_inc(st[qt % 2].sem, 16)
                g.wait_ge(st[0].sem, 16 * 4)
                g.wait_ge(st[1].sem, 16 * 4)
        phase_end(nc, pp)

    emit_proj(nc, IN, tag + "o", SC["aWoT_bf"].ap(), D, D, y_bf, xres_dst, F32,
              bias=IN["abo"], res_f32=xres_src)



L1W = ["Wq", "Wk", "Wv", "Wo"]
_L0_SHAPES = {"W_dkv": (D, 256), "W_uk": (256, 512), "W_uv": (256, 512),
              "W_dq": (D, 512), "W_uq": (512, 512), "W_qr": (512, 256),
              "W_kr": (D, 32), "W_out": (512, D),
              "idx_wq": (D, 256), "idx_wk": (D, 64)}


def nc_shape_of(w):
    return _L0_SHAPES[w]


def build_program(stages=("ffn0a",)):
    nc = bass.Bass("TRN2", target_bir_lowering=False)
    IN = {}

    def ein(name, shape, dt=F32):
        IN[name] = nc.dram_tensor(name, list(shape), dt, kind="ExternalInput").ap()

    ein("x", (D, L))
    ein("w_freq_real", (D, D)); ein("w_freq_imag", (D, D)); ein("ffn_bias", (D,))
    ein("Fre", (D, D)); ein("Fim_neg", (D, D))
    ein("Gre", (D, D)); ein("Gim", (D, D)); ein("Gim_neg", (D, D))
    ein("ident_f32", (128, 128))
    ein("ones_col", (128, 1)); ein("ones_row", (1, 128))
    for w in L1W:
        ein(w, (D, D))
    ein("bq", (D,)); ein("bk", (D,)); ein("bv", (D,)); ein("bo", (D,))
    ein("router_w", (D, 1)); ein("router_b", (1,))
    ein("W1", (4 * D, 4096)); ein("b1", (4 * 4096,))
    ein("W2", (4 * 4096, D)); ein("b2", (4 * D,))
    ein("ln1_g", (D,)); ein("ln1_b", (D,)); ein("ln2_g", (D,)); ein("ln2_b", (D,))
    ein("aWb", (64, D)); ein("aWv", (D, D)); ein("abv", (D,))
    ein("aWo", (D, D)); ein("abo", (D,))
    ein("cmask", (512, TT)); ein("ones_col_bf", (128, 1), BF16)
    for w, sh in [("W_dkv", (D, 256)), ("W_uk", (256, 512)), ("W_uv", (256, 512)),
                  ("W_dq", (D, 512)), ("W_uq", (512, 512)), ("W_qr", (512, 256)),
                  ("W_kr", (D, 32)), ("W_out", (512, D)),
                  ("idx_wq", (D, 256)), ("idx_wk", (D, 64))]:
        ein(w, sh)
    ein("idx_w", (4,)); ein("raw_delta", (16,))
    ein("iota_row", (128, L)); ein("ladv", (128, L)); ein("iota_p", (128, 1))
    ein("tab_kc", (16, L)); ein("tab_ks", (16, L))
    ein("tab_qc", (128, L)); ein("tab_qs", (128, L))
    ein("tselc", (L, 512)); ein("tsels", (L, 512))
    ein("bmaskA", (128, 192)); ein("bmaskB", (128, 192)); ein("rep16", (16, 128))
    ein("sel2", (16, 8 * 128))
    ein("colsel", (128, 32 * 32)); ein("rowsel", (32, 32 * 128))
    ein("striu34", (34, 32)); ein("triuI", (128, 128))
    ein("iota_tok", (128, 32), I32); ein("ce4m1", (2, 4))
    ein("oobfill", (1, 4 * MOE_CAP), I32)

    y = nc.dram_tensor("y", [D, L], F32, kind="ExternalOutput").ap()

    SC = {}

    def scr(name, shape, dt):
        SC[name] = nc.dram_tensor("sc_" + name, list(shape), dt, kind="Internal")

    for n in ["Fre", "Fim_neg", "Gre", "Gim", "Gim_neg",
              "wreT", "wimT", "WGre", "WGim", "M"]:
        scr(n + "_bf", (D, D), BF16)
    scr("ident", (128, 128), BF16)
    scr("xn_bf", (D, L), BF16)
    scr("xn_f32", (D, L), F32)
    scr("xres_a", (D, L), F32)
    scr("xres_b", (D, L), F32)
    need_l1 = any(st.startswith("l1ffn") for st in stages)
    need_l1a = any(st.startswith("l1attn") for st in stages)
    need_l0a = any(st.startswith("l0attn") for st in stages)
    L0W = ["W_uk", "W_uv", "W_out"]
    if need_l0a:
        for w in L0W:
            scr(w + "_bf", nc_shape_of(w), BF16)
        scr("WcatA_bf", (D, 1120), BF16)
        scr("WcatB_bf", (512, 768), BF16)
        scr("l0_catA", (1120, L), BF16)
        scr("l0_catB", (768, L), BF16)
        scr("l0_k", (512, L), BF16)

        class _SCS:
            def __init__(self, ap):
                self._ap = ap

            def ap(self):
                return self._ap
        catA = SC["l0_catA"].ap(); catB = SC["l0_catB"].ap()
        SC["l0_ckv"] = _SCS(catA[0:256, :])
        SC["l0_cq"] = _SCS(catA[256:768, :])
        SC["l0_krr"] = _SCS(catA[768:800, :])
        SC["l0_qi"] = _SCS(catA[800:1056, :])
        SC["l0_ki"] = _SCS(catA[1056:1120, :])
        SC["l0_qc"] = _SCS(catB[0:512, :])
        SC["l0_qrr"] = _SCS(catB[512:768, :])
        scr("l0_mfm", (32, L), BF16); scr("l0_krope", (32, L), BF16)
        scr("l0_qmu", (256, L), BF16); scr("l0_qband", (256, L), BF16)
        scr("l0_qct", (L, 512), BF16); scr("l0_qmut", (L, 256), BF16)
        scr("l0_Kcat", (768, L), BF16); scr("l0_Qcat", (768, L), BF16)
        scr("l0_kmv", (L, 1056), BF16)
        scr("l0_idx", (L, 32), I32)
        scr("l0_attnf", (512, L), BF16)
    if need_l1a:
        scr("aWbT_bf", (D, 64), BF16)
        scr("aWvT_bf", (D, D), BF16)
        scr("aWoT_bf", (D, D), BF16)
        scr("a_lat", (64, L), BF16)
        scr("a_va", (L, 1025), BF16)
        scr("a_y", (D, L), BF16)
    if need_l1:
        for w in L1W:
            scr(w + "_bf", (D, D), BF16)
        scr("rw_bf", (D, 1), BF16)
        scr("W1_bf", (4 * D, 4096), BF16)
        scr("W2_bf", (4 * 4096, D), BF16)
        scr("l1_q", (D, L), BF16); scr("l1_k", (D, L), BF16)
        scr("l1_va", (L, 16 * 65), BF16)
        scr("l1_attn", (D, L), BF16)
        scr("l1_t1", (D, L), F32)
        scr("l1_den", (16, L), F32)
        scr("l1_xtm", (L + 128, D), BF16)
        scr("l1_fftm", (L + 128, D), BF16)
        scr("l1_idx", (4 * MOE_CAP, 1), I32)
        scr("l1_t1nb", (D, L), BF16); scr("l1_t1nf", (D, L), F32)
        scr("l1_keys", (1, L), F32); scr("l1_emask", (4, L), F32)
        scr("l1_ffacc", (D, L), F32)

    with ExitStack() as c0:
        p0 = P(nc, c0)
        s_id = p0.sbuf("id0", [128, 128], BF16)
        l0 = p0.ctr("id0_l")
        with nc.Block() as blk:
            @blk.gpsimd
            def _(g):
                g.dma_start(out=s_id[:], in_=IN["ident_f32"]).then_inc(l0.sem, 16)
                g.wait_ge(l0.sem, 16)
                g.dma_start(out=SC["ident"].ap(), in_=s_id[:]).then_inc(l0.sem, 32)
                g.wait_ge(l0.sem, 48)
        phase_end(nc, p0)

    cast_jobs = [(IN[n], SC[n + "_bf"].ap())
                 for n in ["Fre", "Fim_neg", "Gre", "Gim", "Gim_neg"]]
    if need_l1:
        cast_jobs += [(IN[w], SC[w + "_bf"].ap()) for w in L1W]
        cast_jobs += [(IN["router_w"], SC["rw_bf"].ap()),
                      (IN["W1"], SC["W1_bf"].ap()), (IN["W2"], SC["W2_bf"].ap())]
    if need_l0a:
        cast_jobs += [(IN[w], SC[w + "_bf"].ap()) for w in L0W]
        WA = SC["WcatA_bf"].ap()
        WB = SC["WcatB_bf"].ap()
        cast_jobs += [(IN["W_dkv"], WA[:, 0:256]), (IN["W_dq"], WA[:, 256:768]),
                      (IN["W_kr"], WA[:, 768:800]), (IN["idx_wq"], WA[:, 800:1056]),
                      (IN["idx_wk"], WA[:, 1056:1120]),
                      (IN["W_uq"], WB[:, 0:512]), (IN["W_qr"], WB[:, 512:768])]
    emit_cast_bf16(nc, cast_jobs, "cc")
    tjobs = [(IN["w_freq_real"], SC["wreT_bf"].ap()),
             (IN["w_freq_imag"], SC["wimT_bf"].ap())]
    if need_l1a:
        tjobs += [(IN["aWb"], SC["aWbT_bf"].ap()),
                  (IN["aWv"], SC["aWvT_bf"].ap()),
                  (IN["aWo"], SC["aWoT_bf"].ap())]
    emit_transpose_bf16(nc, tjobs, IN["ident_f32"], "tw")
    emit_mm1024(nc, [(SC["wreT_bf"].ap(), SC["Gre_bf"].ap()),
                     (SC["wimT_bf"].ap(), SC["Gim_neg_bf"].ap())], SC["WGre_bf"].ap(), "m1")
    emit_mm1024(nc, [(SC["wreT_bf"].ap(), SC["Gim_bf"].ap()),
                     (SC["wimT_bf"].ap(), SC["Gre_bf"].ap())], SC["WGim_bf"].ap(), "m2")
    emit_mm1024(nc, [(SC["Fre_bf"].ap(), SC["WGre_bf"].ap()),
                     (SC["Fim_neg_bf"].ap(), SC["WGim_bf"].ap())], SC["M_bf"].ap(), "m3")

    cur = IN["x"]
    nxt = [SC["xres_a"].ap(), SC["xres_b"].ap()]
    for si, stg in enumerate(stages):
        dst = nxt[si % 2]
        wf32 = stg.startswith("l1ffn")
        emit_ln(nc, IN, SC, cur, want_f32=wf32, tag=f"ln{si}")
        if stg.startswith("ffn0"):
            emit_ffn0(nc, IN, SC, cur, dst, tag=f"ff{si}")
        elif stg.startswith("l1ffn"):
            emit_l1ffn(nc, IN, SC, cur, dst, tag=f"s{si}")
        elif stg.startswith("l1attn"):
            emit_l1attn_block(nc, IN, SC, cur, dst, tag=f"a{si}")
        elif stg.startswith("l0attn"):
            emit_l0attn_block(nc, IN, SC, cur, dst, tag=f"z{si}")
        else:
            raise ValueError(stg)
        cur = dst

    if DEBUG_COPY is not None:
        src_t = SC[DEBUG_COPY]
        R = src_t.shape[0] if hasattr(src_t, "shape") else src_t.ap().tensor.shape[0]
        dsrc = src_t.ap()
        rch = min(NCH, (R + 127) // 128)
    else:
        dsrc = cur
        rch = NCH
    with ExitStack() as c9:
        p9 = P(nc, c9)
        t9 = p9.sbuf("fin", [128, rch, TT], F32)
        l9 = p9.ctr("fin_l"); s9 = p9.ctr("fin_s")
        with nc.Block() as blk:
            @blk.gpsimd
            def _(sy):
                for i in range(NT):
                    if i >= 1:
                        sy.wait_ge(s9.sem, 16 * i)
                    sy.dma_start(out=t9[:], in_=fm_view(dsrc, 0, rch, i * TT, TT)
                                 ).then_inc(l9.sem, 16)
                    sy.wait_ge(l9.sem, 16 * (i + 1))
                    sy.dma_start(out=fm_view(y, 0, rch, i * TT, TT), in_=t9[:]
                                 ).then_inc(s9.sem, 16)
                sy.wait_ge(s9.sem, 16 * NT)
    return nc



def host_constants():
    d = np.arange(D)
    ang = 2 * np.pi * np.outer(d, d) / D
    C = {}
    C["Fre"] = np.cos(ang).astype(np.float32)
    C["Fim_neg"] = np.sin(ang).astype(np.float32)
    C["Gre"] = (np.cos(ang) / D).astype(np.float32)
    C["Gim"] = (np.sin(ang) / D).astype(np.float32)
    C["Gim_neg"] = (-np.sin(ang) / D).astype(np.float32)
    C["ident_f32"] = np.eye(128, dtype=np.float32)
    C["ones_col"] = np.ones((128, 1), np.float32)
    C["ones_row"] = np.ones((1, 128), np.float32)
    C["ones_col_bf"] = np.ones((128, 1), ml_dtypes.bfloat16)
    cm = np.zeros((4, 128, TT), np.float32)
    for o in range(4):
        kp = np.arange(128)[:, None] + 128 * o
        qf = np.arange(TT)[None, :]
        cm[o] = np.where(qf >= kp, 0.0, NEG)
    C["cmask"] = cm.reshape(512, TT)
    th = (1.0 / 10000.0 ** (2.0 * np.arange(16) / 32.0)).astype(np.float64)
    p = np.arange(L)
    C["tab_kc"] = np.cos(np.outer(th, p)).astype(np.float32)
    C["tab_ks"] = np.sin(np.outer(th, p)).astype(np.float32)
    tq = np.maximum(p, 2 * p - 63)
    C["tab_qc"] = np.tile(np.cos(np.outer(th, tq)).astype(np.float32), (8, 1))
    C["tab_qs"] = np.tile(np.sin(np.outer(th, tq)).astype(np.float32), (8, 1))
    jj = np.arange(64, 96)
    rel = p[:, None] - jj[None, :]
    ang = rel[:, :, None] * th[None, None, :]
    C["tselc"] = np.cos(ang).astype(np.float32).reshape(L, 512)
    C["tsels"] = np.sin(ang).astype(np.float32).reshape(L, 512)
    C["iota_row"] = np.tile(p.astype(np.float32), (128, 1))
    C["ladv"] = np.tile((-LADC - p * LADD).astype(np.float32), (128, 1))
    C["iota_p"] = np.arange(128, dtype=np.float32).reshape(128, 1)
    r = np.arange(128)[:, None]; c = np.arange(192)[None, :]
    C["bmaskA"] = np.where((c >= r + 1) & (c <= r + 64), 0.0, NEG).astype(np.float32)
    b0 = np.maximum(0, r - 63)
    C["bmaskB"] = np.where((c >= b0) & (c <= b0 + 63), 0.0, NEG).astype(np.float32)
    C["rep16"] = (np.arange(128)[None, :] % 16 == np.arange(16)[:, None]).astype(np.float32)
    sel2 = np.zeros((16, 8, 128), np.float32)
    for kc in range(8):
        for p in range(128):
            sel2[2 * kc + p // 64, kc, p] = 1.0
    C["sel2"] = sel2.reshape(16, 8 * 128)
    CAPm = 4 * MOE_CAP
    C["colsel"] = np.repeat((np.arange(32)[None, :, None] == np.arange(32)[None, None, :])
                            .astype(np.float32), 128, axis=0).reshape(128, 32 * 32)
    C["rowsel"] = np.repeat((np.arange(32)[:, None, None] == np.arange(32)[None, :, None])
                            .astype(np.float32), 128, axis=2).reshape(32, 32 * 128)
    su = np.zeros((34, 32), np.float32)
    su[:32, :] = (np.arange(32)[:, None] < np.arange(32)[None, :]).astype(np.float32)
    su[32, :] = 1.0
    su[33, :] = 1.0
    C["striu34"] = su
    C["triuI"] = (np.arange(128)[:, None] <= np.arange(128)[None, :]).astype(np.float32)
    C["iota_tok"] = (np.arange(32)[None, :] * 128
                     + np.arange(128)[:, None]).astype(np.int32)
    C["ce4m1"] = np.stack([np.arange(4, dtype=np.float32) * MOE_CAP,
                           np.full(4, -1.0, np.float32)])
    C["oobfill"] = np.full((1, CAPm), L, np.int32)
    return C


def make_in_map(b, inputs, C, stages=("ffn0a",)):
    f = lambda k: np.ascontiguousarray(np.asarray(inputs[k], np.float32))
    m = dict(C)
    m["x"] = np.ascontiguousarray(np.asarray(inputs["L0_attn_x"][b], np.float32).T)
    m["w_freq_real"] = f("L0_ffn_w_freq_real")
    m["w_freq_imag"] = f("L0_ffn_w_freq_imag")
    m["ffn_bias"] = f("L0_ffn_bias")
    if any(st.startswith("l1ffn") for st in stages):
        m["Wq"] = f("L1_ffn_Wq"); m["bq"] = f("L1_ffn_bq")
        m["Wk"] = f("L1_ffn_Wk"); m["bk"] = f("L1_ffn_bk")
        m["Wv"] = f("L1_ffn_Wv"); m["bv"] = f("L1_ffn_bv")
        m["Wo"] = f("L1_ffn_Wo"); m["bo"] = f("L1_ffn_bo")
        m["router_w"] = f("L1_ffn_router_w"); m["router_b"] = f("L1_ffn_router_b")
        m["W1"] = f("L1_ffn_W1").reshape(4 * D, 4096)
        m["b1"] = f("L1_ffn_b1").reshape(-1)
        m["W2"] = f("L1_ffn_W2").reshape(4 * 4096, D)
        m["b2"] = f("L1_ffn_b2").reshape(-1)
        m["ln1_g"] = f("L1_ffn_ln1_g"); m["ln1_b"] = f("L1_ffn_ln1_b")
        m["ln2_g"] = f("L1_ffn_ln2_g"); m["ln2_b"] = f("L1_ffn_ln2_b")
    if any(st.startswith("l1attn") for st in stages):
        m["aWb"] = f("L1_attn_W_basis")
        m["aWv"] = f("L1_attn_W_v"); m["abv"] = f("L1_attn_b_v")
        m["aWo"] = f("L1_attn_W_o"); m["abo"] = f("L1_attn_b_o")
    else:
        m["aWb"] = np.zeros((64, D), np.float32)
        m["aWv"] = np.zeros((D, D), np.float32); m["abv"] = np.zeros((D,), np.float32)
        m["aWo"] = np.zeros((D, D), np.float32); m["abo"] = np.zeros((D,), np.float32)
    if any(st.startswith("l0attn") for st in stages):
        m["W_dkv"] = f("L0_attn_W_dkv"); m["W_uk"] = f("L0_attn_W_uk")
        m["W_uv"] = f("L0_attn_W_uv"); m["W_dq"] = f("L0_attn_W_dq")
        m["W_uq"] = f("L0_attn_W_uq"); m["W_qr"] = f("L0_attn_W_qr")
        m["W_kr"] = f("L0_attn_W_kr"); m["W_out"] = f("L0_attn_W_out")
        m["idx_wq"] = f("L0_attn_idx_wq"); m["idx_wk"] = f("L0_attn_idx_wk")
        m["idx_w"] = f("L0_attn_idx_w"); m["raw_delta"] = f("L0_attn_raw_delta")
    else:
        for w, sh in _L0_SHAPES.items():
            m[w] = np.zeros(sh, np.float32)
        m["idx_w"] = np.zeros((4,), np.float32)
        m["raw_delta"] = np.zeros((16,), np.float32)
    if not any(st.startswith("l1ffn") for st in stages):
        for w in L1W:
            m[w] = np.zeros((D, D), np.float32)
        for bnm in ["bq", "bk", "bv", "bo"]:
            m[bnm] = np.zeros((D,), np.float32)
        m["router_w"] = np.zeros((D, 1), np.float32)
        m["router_b"] = np.zeros((1,), np.float32)
        m["W1"] = np.zeros((4 * D, 4096), np.float32)
        m["b1"] = np.zeros((4 * 4096,), np.float32)
        m["W2"] = np.zeros((4 * 4096, D), np.float32)
        m["b2"] = np.zeros((4 * D,), np.float32)
        for g in ["ln1_g", "ln1_b", "ln2_g", "ln2_b"]:
            m[g] = np.zeros((D,), np.float32)
    return m


FULL_STAGES = ("ffn0a", "l0attn", "ffn0b", "l1ffn_a", "l1attn", "l1ffn_b")


def kernel(**inputs):
    C = host_constants()
    nc = build_program(stages=FULL_STAGES)
    in_maps = [make_in_map(b, inputs, C, FULL_STAGES) for b in range(8)]
    res = run_bass_kernel_spmd(nc, in_maps, core_ids=list(range(8)))
    return np.stack([np.asarray(r["y"], np.float32).T for r in res.results], axis=0)
